# version 03
# baseline (speedup 1.0000x reference)
import numpy as np
import ml_dtypes
from contextlib import ExitStack

import concourse.bass as bass
import concourse.mybir as mybir
from concourse.bass_utils import run_bass_kernel_spmd

F32 = mybir.dt.float32
BF16 = mybir.dt.bfloat16
I32 = mybir.dt.int32
AF = mybir.ActivationFunctionType
ALU = mybir.AluOpType
ET = mybir.EngineType
ds = bass.ds
BF = ml_dtypes.bfloat16

D = 2048
S = 8192
DC = 16
H = 16
OWN = 1024
NSC = 16
I_FFN = 8192
EPS = 1e-5
EXP_SCALE = float(1.0 / np.sqrt(128.0))
EXP_BIAS = -12.0

ENGS = ("sync", "gpsimd", "tensor", "vector", "scalar")


class Prog:
    def __init__(self, nc, es):
        self.nc = nc
        self.es = es
        self.q = {e: [] for e in ENGS}
        self.cnt = {}
        self.sems = {}
        for s in ("sy", "gp", "pe", "act", "dve"):
            self.sems[s] = es.enter_context(nc.semaphore(s))
            self.cnt[s] = 0

    def on(self, eng, fn):
        self.q[eng].append(fn)

    def wait(self, eng, name, val):
        if val is None or val <= 0:
            return
        self.q[eng].append(lambda e, s=self.sems[name], v=val: e.wait_ge(s, v))

    def emit(self, eng, fn, sem=None, by=1):
        if sem is None:
            self.q[eng].append(fn)
            return None
        self.cnt[sem] += by
        n = self.cnt[sem]
        self.q[eng].append(lambda e, f=fn, s=self.sems[sem], b=by: f(e).then_inc(s, b))
        return n

    def dma(self, eng, out_ap, in_ap):
        sem = "sy" if eng == "sync" else "gp"
        return self.emit(eng, lambda e, o=out_ap, i=in_ap: e.dma_start(out=o, in_=i),
                         sem=sem, by=16)

    def barrier(self, *engs):
        snap = dict(self.cnt)
        for eng in engs:
            for s, v in snap.items():
                self.wait(eng, s, v)


def build_program():
    nc = bass.Bass("TRN2", target_bir_lowering=False)

    xT = nc.dram_tensor("xT", [D, S], F32, kind="ExternalInput")
    x_own = nc.dram_tensor("x_own", [D, OWN], F32, kind="ExternalInput")
    wqkvT = nc.dram_tensor("wqkvT", [D, 2304], F32, kind="ExternalInput")
    qkv_b2d = nc.dram_tensor("qkv_b2d", [128, 18], F32, kind="ExternalInput")
    owT = nc.dram_tensor("owT", [D, D], F32, kind="ExternalInput")
    ob2d = nc.dram_tensor("ob2d", [128, 16], F32, kind="ExternalInput")
    wgT = nc.dram_tensor("wgT", [D, I_FFN], I32, kind="ExternalInput")
    wuT = nc.dram_tensor("wuT", [D, I_FFN], I32, kind="ExternalInput")
    wdT = nc.dram_tensor("wdT", [I_FFN, D], I32, kind="ExternalInput")
    gsc2d = nc.dram_tensor("gsc2d", [128, 64], F32, kind="ExternalInput")
    usc2d = nc.dram_tensor("usc2d", [128, 64], F32, kind="ExternalInput")
    dsc2d = nc.dram_tensor("dsc2d", [128, 16], F32, kind="ExternalInput")
    ktbl = nc.dram_tensor("ktbl", [1, 4], I32, kind="ExternalInput")
    bkill = nc.dram_tensor("bkill", [128, 48], F32, kind="ExternalInput")
    maskd = nc.dram_tensor("maskd", [128, 2048], F32, kind="ExternalInput")
    out_own = nc.dram_tensor("out_own", [OWN, D], F32, kind="ExternalOutput")

    id_b = nc.inline_tensor(np.eye(128, dtype=BF), "id_b")
    id_f = nc.inline_tensor(np.eye(128, dtype=np.float32), "id_f")
    onesc_b = nc.inline_tensor(np.ones((128, 1), dtype=BF), "onesc_b")
    onesc_f = nc.inline_tensor(np.ones((128, 1), dtype=np.float32), "onesc_f")
    onesr_f = nc.inline_tensor(np.ones((1, 128), dtype=np.float32), "onesr_f")
    neg128_r = nc.inline_tensor(np.full((1, 128), -128.0, dtype=BF), "neg128_r")
    zeros_r = nc.inline_tensor(np.zeros((1, 512), dtype=BF), "zeros_r")
    eps_c = nc.inline_tensor(np.full((128, 1), EPS, dtype=np.float32), "eps_c")
    expb_c = nc.inline_tensor(np.full((128, 1), EXP_BIAS, dtype=np.float32), "expb_c")

    es = ExitStack()
    p = Prog(nc, es)
    sb = lambda n, sh, dt: es.enter_context(nc.sbuf_tensor(n, sh, dt))

    resid = sb("resid", [128, DC * OWN], F32)
    cb_id = sb("cb_id", [128, 128], BF16)
    cb_onec = sb("cb_onec", [128, 1], BF16)
    cb_neg128 = sb("cb_neg128", [1, 128], BF16)
    cb_zrow = sb("cb_zrow", [1, 512], BF16)
    cf_id = sb("cf_id", [128, 128], F32)
    cf_onec = sb("cf_onec", [128, 1], F32)
    cf_oner = sb("cf_oner", [1, 128], F32)
    maskb = sb("maskb", [128, 2048], BF16)
    ktb = sb("ktb", [1, 4], I32)
    bkb = sb("bkb", [128, 48], F32)
    bqkv = sb("bqkv", [128, 18], F32)
    bo = sb("bo", [128, 16], F32)
    gsc = sb("gsc", [128, 64], F32)
    usc = sb("usc", [128, 64], F32)
    dsc = sb("dsc", [128, 16], F32)
    cf_eps = sb("cf_eps", [128, 1], F32)
    cf_expb = sb("cf_expb", [128, 1], F32)
    ow0 = sb("ow0", [128, DC * 128], BF16)
    ow1 = sb("ow1", [128, DC * 128], BF16)

    for dst, src in ((cb_id, id_b), (cb_onec, onesc_b), (cb_neg128, neg128_r),
                     (cb_zrow, zeros_r), (cf_id, id_f), (cf_onec, onesc_f),
                     (cf_oner, onesr_f), (cf_eps, eps_c), (cf_expb, expb_c)):
        p.dma("sync", dst[:], src.ap())
    p.dma("gpsimd", maskb[:], maskd.ap())
    for dst, src in ((ktb, ktbl), (bkb, bkill), (bqkv, qkv_b2d), (bo, ob2d),
                     (gsc, gsc2d), (usc, usc2d), (dsc, dsc2d)):
        p.dma("sync", dst[:], src.ap())
    e_resid_init = p.dma("sync", resid[:].rearrange("p (dc s) -> p dc s", dc=DC),
                         x_own.ap().rearrange("(dc p) s -> p dc s", p=128))
    e_consts_sy = p.cnt["sy"]
    e_consts_gp = p.cnt["gp"]
    for _eng in ("tensor", "vector", "scalar"):
        p.wait(_eng, "sy", e_consts_sy)

    ac = ExitStack()
    asb = lambda n, sh, dt: ac.enter_context(nc.sbuf_tensor(n, sh, dt))
    k_fm = asb("k_fm", [128, S], BF16)
    v_tm = asb("v_tm", [128, S], BF16)

    pa = ExitStack()
    pa_t = lambda n, sh, dt: pa.enter_context(nc.sbuf_tensor(n, sh, dt))
    pa_p = lambda n, sh, dt: pa.enter_context(nc.psum_tensor(n, sh, dt))
    m_row_b = pa_t("m_row_b", [1, 512], BF16)
    m_row_f = pa_t("m_row_f", [1, 512], F32)
    msq_row = pa_t("msq_row", [1, 512], F32)
    var_row = pa_t("var_row", [1, 512], F32)
    sd_row = pa_t("sd_row", [1, 512], F32)
    rstd_row = pa_t("rstd_row", [1, 512], F32)
    xstage = pa_t("xstage", [128, 2 * DC * 512], BF16)
    wkv = pa_t("wkv", [128, DC * 256], BF16)
    wsn_kv = pa_t("wsn_kv", [1, 256], BF16)
    rstdb_sb = pa_t("rstdb_sb", [128, 512], F32)
    kvtmp = pa_t("kvtmp", [128, 512], F32)
    vchunk = pa_t("vchunk", [128, 512], BF16)
    xsq = pa_t("xsq", [128, 2 * 512], BF16)
    pa_sum = pa_p("pa_sum", [1, 512], F32)
    pa_sumsq = pa_p("pa_sumsq", [1, 512], F32)
    pa_kv = pa_p("pa_kv", [128, 1024], F32)
    pa_bc = pa_p("pa_bc", [128, 512], F32)
    pa_vt = pa_p("pa_vt", [128, 128], BF16)
    pa_ws = pa_p("pa_ws", [1, 256], F32)
    if True:
        e_wkv = p.dma("gpsimd",
                      wkv[:].rearrange("p (dc e) -> p dc e", dc=DC),
                      wqkvT.ap().rearrange("(dc p) e -> p dc e", p=128)[:, :, 2048:2304])

        def emit_xs_dma(sc):
            buf = sc % 2
            return p.dma("gpsimd",
                         xstage[:, buf * DC * 512:(buf + 1) * DC * 512]
                         .rearrange("p (dc s) -> p dc s", dc=DC),
                         xT.ap().rearrange("(dc p) s -> p dc s", p=128)
                         [:, :, sc * 512:(sc + 1) * 512])

        e_xs = [emit_xs_dma(0), emit_xs_dma(1)]

        p.wait("tensor", "gp", e_wkv)
        for dc in range(DC):
            p.emit("tensor",
                   lambda e, dc=dc: e.matmul(pa_ws[:], cb_onec[:],
                                             wkv[:, dc * 256:(dc + 1) * 256],
                                             start=(dc == 0), stop=(dc == DC - 1)),
                   sem="pe" if dc == DC - 1 else None)
        e_ws = p.cnt["pe"]
        p.wait("vector", "pe", e_ws)
        e_wsn = p.emit("vector",
                       lambda e: e.tensor_scalar_mul(wsn_kv[:], pa_ws[:], -1.0), sem="dve")

        e_prev_dve_end = 0
        e_prev_act_v = 0
        e_prev_vt_pe = 0
        for sc in range(NSC):
            buf = sc % 2
            xsl = lambda dc, buf=buf: xstage[:, buf * DC * 512 + dc * 512:
                                             buf * DC * 512 + (dc + 1) * 512]
            p.wait("tensor", "gp", e_xs[sc])
            p.wait("tensor", "dve", e_prev_dve_end)
            for dc in range(DC):
                p.emit("tensor",
                       lambda e, dc=dc, xsl=xsl: e.matmul(pa_sum[:], cb_onec[:], xsl(dc),
                                                          start=(dc == 0), stop=(dc == DC - 1)),
                       sem="pe" if dc == DC - 1 else None)
            e_sum = p.cnt["pe"]
            for dc in range(DC):
                for jc in range(2):
                    p.emit("tensor",
                           lambda e, dc=dc, jc=jc, xsl=xsl: e.matmul(
                               pa_kv[:, jc * 512:(jc + 1) * 512],
                               wkv[:, dc * 256 + jc * 128: dc * 256 + (jc + 1) * 128],
                               xsl(dc), start=(dc == 0), stop=False,
                               skip_group_check=True))
            p.wait("scalar", "gp", e_xs[sc])
            e_mm_sq = {}
            e_sq = {}
            for dc in range(DC):
                if dc >= 2:
                    p.wait("scalar", "pe", e_mm_sq[dc - 2])
                e_sq[dc] = p.emit("scalar",
                                  lambda e, dc=dc, xsl=xsl: e.activation(
                                      out=xsq[:, (dc % 2) * 512:(dc % 2 + 1) * 512],
                                      in_=xsl(dc), func=AF.Square),
                                  sem="act")
                p.wait("tensor", "act", e_sq[dc])
                e_mm_sq[dc] = p.emit("tensor",
                                     lambda e, dc=dc: e.matmul(
                                         pa_sumsq[:], cb_onec[:],
                                         xsq[:, (dc % 2) * 512:(dc % 2 + 1) * 512],
                                         start=(dc == 0), stop=(dc == DC - 1)),
                                     sem="pe")
            e_sumsq = p.cnt["pe"]
            p.wait("vector", "pe", e_sum)
            p.emit("vector", lambda e: e.tensor_scalar_mul(m_row_b[:], pa_sum[:], 1.0 / D))
            e_mb = p.emit("vector",
                          lambda e: e.tensor_scalar_mul(m_row_f[:], pa_sum[:], 1.0 / D),
                          sem="dve")
            p.wait("vector", "dve", e_mb)
            p.emit("vector", lambda e: e.tensor_mul(msq_row[:], m_row_f[:], m_row_f[:]))
            p.wait("vector", "pe", e_sumsq)
            e_var = p.emit("vector",
                           lambda e: e.scalar_tensor_tensor(var_row[:], pa_sumsq[:], 1.0 / D,
                                                            msq_row[:], ALU.mult, ALU.subtract),
                           sem="dve")
            p.wait("scalar", "dve", e_var)
            e_sd = p.emit("scalar",
                          lambda e: e.activation(out=sd_row[:], in_=var_row[:],
                                                 func=AF.Sqrt, bias=cf_eps[0:1, :]),
                          sem="act")
            p.wait("vector", "act", e_sd)
            e_rstd = p.emit("vector", lambda e: e.reciprocal(rstd_row[:], sd_row[:]), sem="dve")
            p.wait("tensor", "dve", max(e_mb, e_wsn))
            for jc in range(2):
                p.emit("tensor",
                       lambda e, jc=jc: e.matmul(pa_kv[:, jc * 512:(jc + 1) * 512],
                                                 wsn_kv[0:1, jc * 128:(jc + 1) * 128],
                                                 m_row_b[:], start=False, stop=True,
                                                 skip_group_check=True))
            p.wait("tensor", "dve", e_rstd)
            e_bc = p.emit("tensor",
                          lambda e: e.matmul(pa_bc[:], cf_oner[:], rstd_row[:],
                                             start=True, stop=True),
                          sem="pe")
            p.wait("vector", "pe", e_bc)
            e_bcc = p.emit("vector", lambda e: e.tensor_copy(out=rstdb_sb[:], in_=pa_bc[:]),
                           sem="dve")
            p.wait("vector", "dve", e_bcc)
            e_km = p.emit("vector",
                          lambda e: e.tensor_mul(kvtmp[:], pa_kv[:, 0:512], rstdb_sb[:]),
                          sem="dve")
            p.emit("vector",
                   lambda e, sc=sc: e.tensor_scalar_add(
                       k_fm[:, sc * 512:(sc + 1) * 512], kvtmp[:], bqkv[:, 16:17]))
            e_vm = p.emit("vector",
                          lambda e: e.tensor_mul(kvtmp[:], pa_kv[:, 512:1024], rstdb_sb[:]),
                          sem="dve")
            p.wait("vector", "pe", e_prev_vt_pe)
            e_v = p.emit("vector",
                         lambda e: e.tensor_scalar_add(vchunk[:], kvtmp[:], bqkv[:, 17:18]),
                         sem="dve")
            e_prev_act_v = e_v
            p.wait("tensor", "dve", e_v)
            e_vc = 0
            for j in range(4):
                p.wait("tensor", "dve", e_vc)
                e_t = p.emit("tensor",
                             lambda e, j=j: e.transpose(pa_vt[:],
                                                        vchunk[:, j * 128:(j + 1) * 128],
                                                        cb_id[:]),
                             sem="pe")
                p.wait("vector", "pe", e_t)
                e_vc = p.emit("vector",
                              lambda e, sc=sc, j=j: e.tensor_copy(
                                  out=v_tm[:, sc * 512 + j * 128: sc * 512 + (j + 1) * 128],
                                  in_=pa_vt[:]),
                              sem="dve")
            e_prev_vt_pe = p.cnt["pe"]
            e_prev_dve_end = p.cnt["dve"]
            if sc + 2 < NSC:
                p.wait("gpsimd", "pe", p.cnt["pe"])
                p.wait("gpsimd", "act", e_sq[DC - 1])
                e_xs.append(emit_xs_dma(sc + 2))

        p.barrier("tensor", "vector", "scalar")
    pa.close()

    ctxb = asb("ctxb", [128, H * OWN], BF16)
    x1own = asb("x1own", [128, DC * OWN], BF16)
    bsub = ExitStack()
    m_own_bc = bsub.enter_context(nc.sbuf_tensor("m_own_bc", [128, OWN], F32))
    rstd_own_bc = bsub.enter_context(nc.sbuf_tensor("rstd_own_bc", [128, OWN], F32))
    xsq = bsub.enter_context(nc.sbuf_tensor("xsq_b", [128, 2 * 512], BF16))
    kvtmp = bsub.enter_context(nc.sbuf_tensor("kvtmp_b", [128, 512], F32))
    m_row_f = bsub.enter_context(nc.sbuf_tensor("m_row_fb", [1, 512], F32))
    msq_row = bsub.enter_context(nc.sbuf_tensor("msq_rowb", [1, 512], F32))
    var_row = bsub.enter_context(nc.sbuf_tensor("var_rowb", [1, 512], F32))
    sd_row = bsub.enter_context(nc.sbuf_tensor("sd_rowb", [1, 512], F32))
    rstd_row = bsub.enter_context(nc.sbuf_tensor("rstd_rowb", [1, 512], F32))

    p.wait("gpsimd", "pe", p.cnt["pe"])
    p.wait("gpsimd", "act", p.cnt["act"])
    p.wait("gpsimd", "dve", p.cnt["dve"])
    e_xob = p.dma("gpsimd",
                  x1own[:].rearrange("p (dc s) -> p dc s", dc=DC),
                  x_own.ap().rearrange("(dc p) s -> p dc s", p=128))

    with nc.psum_tensor("pb_sum", [1, 512], F32) as pb_sum, \
         nc.psum_tensor("pb_sumsq", [1, 512], F32) as pb_sumsq, \
         nc.psum_tensor("pb_bc", [128, 1024], F32) as pb_bc:
        p.wait("tensor", "gp", e_xob)
        p.wait("scalar", "gp", e_xob)
        for sc in range(2):
            xol = lambda dc, sc=sc: x1own[:, dc * OWN + sc * 512: dc * OWN + (sc + 1) * 512]
            if sc == 1:
                p.wait("tensor", "dve", p.cnt["dve"])
            for dc in range(DC):
                p.emit("tensor",
                       lambda e, dc=dc, xol=xol: e.matmul(pb_sum[:], cb_onec[:], xol(dc),
                                                          start=(dc == 0), stop=(dc == DC - 1)),
                       sem="pe" if dc == DC - 1 else None)
            e_sum = p.cnt["pe"]
            e_mm_sq = {}
            e_sq = {}
            for dc in range(DC):
                if dc >= 2:
                    p.wait("scalar", "pe", e_mm_sq[dc - 2])
                e_sq[dc] = p.emit("scalar",
                                  lambda e, dc=dc, xol=xol: e.activation(
                                      out=xsq[:, (dc % 2) * 512:(dc % 2 + 1) * 512],
                                      in_=xol(dc), func=AF.Square),
                                  sem="act")
                p.wait("tensor", "act", e_sq[dc])
                e_mm_sq[dc] = p.emit("tensor",
                                     lambda e, dc=dc: e.matmul(
                                         pb_sumsq[:], cb_onec[:],
                                         xsq[:, (dc % 2) * 512:(dc % 2 + 1) * 512],
                                         start=(dc == 0), stop=(dc == DC - 1)),
                                     sem="pe")
            e_sumsq = p.cnt["pe"]
            p.wait("vector", "pe", e_sum)
            e_mf = p.emit("vector",
                          lambda e: e.tensor_scalar_mul(m_row_f[:], pb_sum[:], 1.0 / D),
                          sem="dve")
            p.wait("vector", "dve", e_mf)
            p.emit("vector", lambda e: e.tensor_mul(msq_row[:], m_row_f[:], m_row_f[:]))
            p.wait("vector", "pe", e_sumsq)
            e_var = p.emit("vector",
                           lambda e: e.scalar_tensor_tensor(var_row[:], pb_sumsq[:], 1.0 / D,
                                                            msq_row[:], ALU.mult, ALU.subtract),
                           sem="dve")
            p.wait("scalar", "dve", e_var)
            e_sd = p.emit("scalar",
                          lambda e: e.activation(out=sd_row[:], in_=var_row[:],
                                                 func=AF.Sqrt, bias=cf_eps[0:1, :]),
                          sem="act")
            p.wait("vector", "act", e_sd)
            e_rstd = p.emit("vector", lambda e: e.reciprocal(rstd_row[:], sd_row[:]), sem="dve")
            p.wait("tensor", "dve", e_mf)
            e_b1 = p.emit("tensor",
                          lambda e: e.matmul(pb_bc[:, 0:512], cf_oner[:], m_row_f[:],
                                             start=True, stop=True),
                          sem="pe")
            p.wait("tensor", "dve", e_rstd)
            e_b2 = p.emit("tensor",
                          lambda e: e.matmul(pb_bc[:, 512:1024], cf_oner[:], rstd_row[:],
                                             start=True, stop=True),
                          sem="pe")
            p.wait("vector", "pe", e_b2)
            p.emit("vector",
                   lambda e, sc=sc: e.tensor_copy(out=m_own_bc[:, sc * 512:(sc + 1) * 512],
                                                  in_=pb_bc[:, 0:512]))
            p.emit("vector",
                   lambda e, sc=sc: e.tensor_copy(out=rstd_own_bc[:, sc * 512:(sc + 1) * 512],
                                                  in_=pb_bc[:, 512:1024]),
                   sem="dve")
        e_bcast_done = p.cnt["dve"]
        p.wait("vector", "dve", e_bcast_done)
        for dc in range(DC):
            for sc in range(2):
                p.emit("vector",
                       lambda e, dc=dc, sc=sc: e.tensor_sub(
                           kvtmp[:],
                           x1own[:, dc * OWN + sc * 512: dc * OWN + (sc + 1) * 512],
                           m_own_bc[:, sc * 512:(sc + 1) * 512]))
                p.emit("vector",
                       lambda e, dc=dc, sc=sc: e.tensor_mul(
                           x1own[:, dc * OWN + sc * 512: dc * OWN + (sc + 1) * 512],
                           kvtmp[:],
                           rstd_own_bc[:, sc * 512:(sc + 1) * 512]),
                       sem="dve")
    e_x1_done = p.cnt["dve"]
    bsub.close()

    q_sb = asb("q_sb", [128, 2 * OWN], BF16)
    p0_sb = asb("p0_sb", [128, 2 * 1024], BF16)
    wq0 = asb("wq0", [128, DC * 128], BF16)
    wq1 = asb("wq1", [128, DC * 128], BF16)
    kst = asb("kst", [128, 4 * 256], BF16)
    vst = asb("vst", [128, 4 * 256], BF16)
    recA = asb("recA", [1, 512], F32)
    recB = asb("recB", [1, 512], F32)
    recbcA = asb("recbcA", [128, 512], F32)
    recbcB = asb("recbcB", [128, 512], F32)

    cps = ExitStack()
    S0 = cps.enter_context(nc.psum_tensor("pc_s0", [128, 1024], F32))
    S1 = cps.enter_context(nc.psum_tensor("pc_s1", [128, 1024], F32))
    ctx_ps = cps.enter_context(nc.psum_tensor("pc_ctx", [128, 1024], F32))
    den_ps = cps.enter_context(nc.psum_tensor("pc_den", [1, 1024], F32))
    SBUFS = (S0, S1)
    NG = 48

    def emit_wq_dma(h):
        wqb = (wq0, wq1)[h % 2]
        return p.dma("gpsimd",
                     wqb[:].rearrange("p (dc j) -> p dc j", dc=DC),
                     wqkvT.ap().rearrange("(dc p) e -> p dc e", p=128)
                     [:, :, h * 128:(h + 1) * 128])

    p.wait("gpsimd", "pe", p.cnt["pe"])
    p.wait("gpsimd", "act", p.cnt["act"])
    p.wait("gpsimd", "dve", p.cnt["dve"])
    e_wq = [emit_wq_dma(0), emit_wq_dma(1)]

    p.barrier("tensor", "vector", "scalar")
    p.wait("vector", "gp", e_consts_gp)
    p.wait("vector", "sy", e_consts_sy)
    p.wait("tensor", "sy", e_consts_sy)
    p.wait("scalar", "sy", e_consts_sy)

    _dve_et = nc.vector.engine
    r_kt = [nc.alloc_register(_dve_et, f"r_kt{j}") for j in range(4)]
    for j in range(4):
        p.emit("vector", lambda e, j=j: e.reg_load(r_kt[j], ktb[0:1, j:j + 1]))
        kv_ = bass.RuntimeValue(val=r_kt[j], min_val=0, max_val=7936)
        p.emit("vector",
               lambda e, j=j, kv_=kv_: e.tensor_copy(
                   out=kst[:, j * 256:(j + 1) * 256], in_=k_fm[:, ds(kv_, 256)]))
        p.emit("vector",
               lambda e, j=j, kv_=kv_: e.tensor_copy(
                   out=vst[:, j * 256:(j + 1) * 256], in_=v_tm[:, ds(kv_, 256)]),
               sem="dve")
    e_stage = p.cnt["dve"]
    p.wait("tensor", "dve", e_stage)

    def slot_params(s):
        if s == 0:
            return (lambda i: kst[:, 0 * 256 + i * 128: 0 * 256 + (i + 1) * 128],
                    lambda i: vst[:, 0 * 256 + i * 128: 0 * 256 + (i + 1) * 128], 0, 0)
        if s == 1:
            return (lambda i: kst[:, 1 * 256 + i * 128: 1 * 256 + (i + 1) * 128],
                    lambda i: vst[:, 1 * 256 + i * 128: 1 * 256 + (i + 1) * 128], 0, 1024)
        if s == 46:
            return (lambda i: kst[:, 2 * 256 + i * 128: 2 * 256 + (i + 1) * 128],
                    lambda i: vst[:, 2 * 256 + i * 128: 2 * 256 + (i + 1) * 128], 512, 0)
        if s == 47:
            return (lambda i: kst[:, 3 * 256 + i * 128: 3 * 256 + (i + 1) * 128],
                    lambda i: vst[:, 3 * 256 + i * 128: 3 * 256 + (i + 1) * 128], 512, 1024)
        if s < 16:
            kb = 256 * (s - 2)
            co = 0
        else:
            kb = 256 * (s - 16)
            co = 512
        return (lambda i, kb=kb: k_fm[:, kb + i * 128: kb + (i + 1) * 128],
                lambda i, kb=kb: v_tm[:, kb + i * 128: kb + (i + 1) * 128], co, None)

    slot = 0
    e_cons = {}
    e_p0rdy = {}
    last_p0_pe = [0, 0]
    pend_grp = [None]
    e_evac_dve = 0
    e_qk_end_byhead = {}

    def emit_pv(gi):
        s = gi % NG
        _, v_src, co, _ = slot_params(s)
        p.wait("tensor", *e_p0rdy[gi])
        for i in range(2):
            p.emit("tensor",
                   lambda e, gi=gi, i=i, co=co, v_src=v_src: e.matmul(
                       ctx_ps[:, co:co + 512],
                       v_src(i),
                       p0_sb[:, (gi % 2) * 1024 + i * 512:(gi % 2) * 1024 + (i + 1) * 512],
                       start=False, stop=False, skip_group_check=True))
        for i in range(2):
            p.emit("tensor",
                   lambda e, gi=gi, i=i, co=co: e.matmul(
                       den_ps[0:1, co:co + 512],
                       cb_onec[:],
                       p0_sb[:, (gi % 2) * 1024 + i * 512:(gi % 2) * 1024 + (i + 1) * 512],
                       start=False, stop=False, skip_group_check=True),
                   sem="pe" if i == 1 else None)
        last_p0_pe[gi % 2] = p.cnt["pe"]

    for h in range(H):
        wqb = (wq0, wq1)[h % 2]
        t_qp = slot
        slot += 1
        qps = SBUFS[t_qp % 2]
        if t_qp - 2 in e_cons:
            p.wait("tensor", *e_cons[t_qp - 2])
        p.wait("tensor", "gp", e_wq[h])
        if h == 0:
            p.wait("tensor", "dve", e_x1_done)
        for sc in range(2):
            for dc in range(DC):
                p.emit("tensor",
                       lambda e, dc=dc, sc=sc, qps=qps, wqb=wqb: e.matmul(
                           qps[:, sc * 512:(sc + 1) * 512],
                           wqb[:, dc * 128:(dc + 1) * 128],
                           x1own[:, dc * OWN + sc * 512: dc * OWN + (sc + 1) * 512],
                           start=(dc == 0), stop=(dc == DC - 1),
                           skip_group_check=True),
                       sem="pe" if (dc == DC - 1 and sc == 1) else None)
        e_qp = p.cnt["pe"]
        p.wait("vector", "pe", e_qp)
        if h >= 2:
            p.wait("vector", "pe", e_qk_end_byhead[h - 2])
        e_qe = p.emit("vector",
                      lambda e, h=h, qps=qps: e.tensor_scalar_add(
                          q_sb[:, (h % 2) * OWN:(h % 2 + 1) * OWN],
                          qps[:], bqkv[:, h:h + 1]),
                      sem="dve")
        e_cons[t_qp] = ("dve", e_qe)
        if h + 2 < H:
            p.wait("gpsimd", "pe", e_qp)
            e_wq.append(emit_wq_dma(h + 2))

        for g in range(NG):
            gi = h * NG + g
            t = slot
            slot += 1
            Sb = SBUFS[t % 2]
            k_src, v_src, co, moff = slot_params(g)
            if t - 2 in e_cons:
                p.wait("tensor", *e_cons[t - 2])
            if g == 0:
                p.wait("tensor", "dve", e_qe)

            if g == 1:
                p.wait("tensor", "dve", e_evac_dve)
                p.emit("tensor", lambda e: e.matmul(ctx_ps[:, 0:512], cb_zrow[0:1, 0:128],
                                                    cb_zrow[:], start=True, stop=True,
                                                    skip_group_check=True))
                p.emit("tensor", lambda e: e.matmul(ctx_ps[:, 512:1024], cb_zrow[0:1, 0:128],
                                                    cb_zrow[:], start=True, stop=True,
                                                    skip_group_check=True))
                p.emit("tensor", lambda e: e.matmul(den_ps[0:1, 0:512], cb_zrow[0:1, 0:1],
                                                    cb_zrow[:], start=True, stop=True,
                                                    skip_group_check=True))
                p.emit("tensor", lambda e: e.matmul(den_ps[0:1, 512:1024], cb_zrow[0:1, 0:1],
                                                    cb_zrow[:], start=True, stop=True,
                                                    skip_group_check=True))

            for i in range(2):
                p.emit("tensor",
                       lambda e, Sb=Sb, k_src=k_src, co=co, h=h, i=i: e.matmul(
                           Sb[:, i * 512:(i + 1) * 512],
                           k_src(i),
                           q_sb[:, (h % 2) * OWN + co:(h % 2) * OWN + co + 512],
                           start=True, stop=True, skip_group_check=True),
                       sem="pe" if i == 1 else None)
            e_qk = p.cnt["pe"]

            if pend_grp[0] is not None:
                emit_pv(pend_grp[0])
            pend_grp[0] = gi

            p.wait("scalar", "pe", e_qk)
            p.wait("scalar", "pe", last_p0_pe[gi % 2])
            e_exp = p.emit("scalar",
                           lambda e, Sb=Sb, gi=gi, g=g: e.activation(
                               out=p0_sb[:, (gi % 2) * 1024:(gi % 2 + 1) * 1024],
                               in_=Sb[:], func=AF.Exp, scale=EXP_SCALE,
                               bias=bkb[:, g:g + 1]),
                           sem="act")
            e_cons[t] = ("act", e_exp)

            if moff is None:
                e_p0rdy[gi] = ("act", e_exp)
            else:
                p.wait("vector", "act", e_exp)
                e_m = p.emit("vector",
                             lambda e, gi=gi, moff=moff: e.tensor_mul(
                                 p0_sb[:, (gi % 2) * 1024:(gi % 2 + 1) * 1024],
                                 p0_sb[:, (gi % 2) * 1024:(gi % 2 + 1) * 1024],
                                 maskb[:, moff:moff + 1024]),
                             sem="dve")
                e_p0rdy[gi] = ("dve", e_m)

        e_qk_end_byhead[h] = p.cnt["pe"]

        emit_pv(pend_grp[0])
        pend_grp[0] = None
        e_last_pv = p.cnt["pe"]

        p.wait("vector", "pe", e_last_pv)
        e_rA = p.emit("vector", lambda e: e.reciprocal(recA[:], den_ps[0:1, 0:512]), sem="dve")
        e_rB = p.emit("vector", lambda e: e.reciprocal(recB[:], den_ps[0:1, 512:1024]), sem="dve")

        t_bA = slot
        slot += 1
        t_bB = slot
        slot += 1
        SbA = SBUFS[t_bA % 2]
        SbB = SBUFS[t_bB % 2]
        if t_bA - 2 in e_cons:
            p.wait("tensor", *e_cons[t_bA - 2])
        p.wait("tensor", "dve", e_rA)
        e_bA = p.emit("tensor",
                      lambda e, SbA=SbA: e.matmul(SbA[:, 0:512], cf_oner[:], recA[:],
                                                  start=True, stop=True,
                                                  skip_group_check=True),
                      sem="pe")
        if t_bB - 2 in e_cons:
            p.wait("tensor", *e_cons[t_bB - 2])
        p.wait("tensor", "dve", e_rB)
        e_bB = p.emit("tensor",
                      lambda e, SbB=SbB: e.matmul(SbB[:, 0:512], cf_oner[:], recB[:],
                                                  start=True, stop=True,
                                                  skip_group_check=True),
                      sem="pe")
        p.wait("vector", "pe", e_bA)
        e_cA = p.emit("vector",
                      lambda e, SbA=SbA: e.tensor_copy(out=recbcA[:], in_=SbA[:, 0:512]),
                      sem="dve")
        e_cons[t_bA] = ("dve", e_cA)
        p.wait("vector", "dve", e_cA)
        p.emit("vector",
               lambda e, h=h: e.tensor_mul(ctxb[:, h * OWN: h * OWN + 512],
                                           ctx_ps[:, 0:512], recbcA[:]),
               sem="dve")
        p.wait("vector", "pe", e_bB)
        e_cB = p.emit("vector",
                      lambda e, SbB=SbB: e.tensor_copy(out=recbcB[:], in_=SbB[:, 0:512]),
                      sem="dve")
        e_cons[t_bB] = ("dve", e_cB)
        p.wait("vector", "dve", e_cB)
        e_evac_dve = p.emit("vector",
                            lambda e, h=h: e.tensor_mul(ctxb[:, h * OWN + 512:(h + 1) * OWN],
                                                        ctx_ps[:, 512:1024], recbcB[:]),
                            sem="dve")

    e_ctx_done = e_evac_dve
    cps.close()

    def emit_ow_dma(ec):
        owb = (ow0, ow1)[ec % 2]
        return p.dma("gpsimd",
                     owb[:].rearrange("p (dc j) -> p dc j", dc=DC),
                     owT.ap().rearrange("(dc p) e -> p dc e", p=128)
                     [:, :, ec * 128:(ec + 1) * 128])

    e_ow = [emit_ow_dma(0), emit_ow_dma(1)]

    with nc.psum_tensor("pd", [128, 1024], F32) as pd:
        p.wait("tensor", "dve", e_ctx_done)
        p.wait("vector", "sy", e_resid_init)
        e_ev_bank = {0: 0, 1: 0}
        e_mm_byec = {}
        for ec in range(DC):
            owb = (ow0, ow1)[ec % 2]
            p.wait("tensor", "gp", e_ow[ec])
            for sc in range(2):
                bank = (2 * ec + sc) % 2
                p.wait("tensor", "dve", e_ev_bank[bank])
                for dc in range(DC):
                    p.emit("tensor",
                           lambda e, dc=dc, sc=sc, bank=bank, owb=owb: e.matmul(
                               pd[:, bank * 512:(bank + 1) * 512],
                               owb[:, dc * 128:(dc + 1) * 128],
                               ctxb[:, dc * OWN + sc * 512: dc * OWN + (sc + 1) * 512],
                               start=(dc == 0), stop=(dc == DC - 1),
                               skip_group_check=True),
                           sem="pe" if dc == DC - 1 else None)
                e_mm = p.cnt["pe"]
                p.wait("vector", "pe", e_mm)
                e_ev_bank[bank] = p.emit(
                    "vector",
                    lambda e, ec=ec, sc=sc, bank=bank: e.scalar_tensor_tensor(
                        resid[:, ec * OWN + sc * 512: ec * OWN + (sc + 1) * 512],
                        pd[:, bank * 512:(bank + 1) * 512],
                        bo[:, ec:ec + 1],
                        resid[:, ec * OWN + sc * 512: ec * OWN + (sc + 1) * 512],
                        ALU.add, ALU.add),
                    sem="dve")
            e_mm_byec[ec] = p.cnt["pe"]
            if ec + 2 < DC:
                p.wait("gpsimd", "pe", e_mm_byec[ec])
                e_ow.append(emit_ow_dma(ec + 2))
    e_D_dve = p.cnt["dve"]
    e_D_pe = p.cnt["pe"]

    ac.close()

    fs = ExitStack()
    fsb = lambda n, sh, dt: fs.enter_context(nc.sbuf_tensor(n, sh, dt))
    x2_sb = fsb("x2_sb", [128, DC * OWN], BF16)
    hmid = fsb("hmid", [128, 16 * OWN], BF16)
    csx2_row = fsb("csx2_row", [1, 1024], BF16)
    csh_row = fsb("csh_row", [1, 1024], BF16)
    gsil = fsb("gsil", [128, 1024], BF16)
    usca = fsb("usca", [128, 1024], BF16)

    p.barrier("tensor", "vector", "scalar")

    with nc.psum_tensor("pl_sum", [1, 512], F32) as pl_sum, \
         nc.psum_tensor("pl_sumsq", [1, 512], F32) as pl_sumsq, \
         nc.psum_tensor("pl_bc", [128, 1024], F32) as pl_bc, \
         nc.sbuf_tensor("m2bc", [128, OWN], F32) as m2bc, \
         nc.sbuf_tensor("rstd2bc", [128, OWN], F32) as rstd2bc, \
         nc.sbuf_tensor("m2_row", [1, 512], F32) as m2_row, \
         nc.sbuf_tensor("msq2", [1, 512], F32) as msq2, \
         nc.sbuf_tensor("var2", [1, 512], F32) as var2, \
         nc.sbuf_tensor("sd2", [1, 512], F32) as sd2, \
         nc.sbuf_tensor("rstd2", [1, 512], F32) as rstd2, \
         nc.sbuf_tensor("xsq2", [128, 2 * 512], BF16) as xsq2, \
         nc.sbuf_tensor("tmpf", [128, 512], F32) as tmpf:
        for sc in range(2):
            rsl = lambda dc, sc=sc: resid[:, dc * OWN + sc * 512: dc * OWN + (sc + 1) * 512]
            if sc == 1:
                p.wait("tensor", "dve", p.cnt["dve"])
            for dc in range(DC):
                p.emit("tensor",
                       lambda e, dc=dc, rsl=rsl: e.matmul(pl_sum[:], cf_onec[:], rsl(dc),
                                                          start=(dc == 0), stop=(dc == DC - 1)),
                       sem="pe" if dc == DC - 1 else None)
            e_sum = p.cnt["pe"]
            e_mm_sq = {}
            e_sq = {}
            for dc in range(DC):
                if dc >= 2:
                    p.wait("scalar", "pe", e_mm_sq[dc - 2])
                e_sq[dc] = p.emit("scalar",
                                  lambda e, dc=dc, rsl=rsl: e.activation(
                                      out=xsq2[:, (dc % 2) * 512:(dc % 2 + 1) * 512],
                                      in_=rsl(dc), func=AF.Square),
                                  sem="act")
                p.wait("tensor", "act", e_sq[dc])
                e_mm_sq[dc] = p.emit("tensor",
                                     lambda e, dc=dc: e.matmul(
                                         pl_sumsq[:], cb_onec[:],
                                         xsq2[:, (dc % 2) * 512:(dc % 2 + 1) * 512],
                                         start=(dc == 0), stop=(dc == DC - 1)),
                                     sem="pe")
            e_sumsq = p.cnt["pe"]
            p.wait("vector", "pe", e_sum)
            e_mf = p.emit("vector",
                          lambda e: e.tensor_scalar_mul(m2_row[:], pl_sum[:], 1.0 / D),
                          sem="dve")
            p.wait("vector", "dve", e_mf)
            p.emit("vector", lambda e: e.tensor_mul(msq2[:], m2_row[:], m2_row[:]))
            p.wait("vector", "pe", e_sumsq)
            e_var = p.emit("vector",
                           lambda e: e.scalar_tensor_tensor(var2[:], pl_sumsq[:], 1.0 / D,
                                                            msq2[:], ALU.mult, ALU.subtract),
                           sem="dve")
            p.wait("scalar", "dve", e_var)
            e_sd = p.emit("scalar",
                          lambda e: e.activation(out=sd2[:], in_=var2[:],
                                                 func=AF.Sqrt, bias=cf_eps[0:1, :]),
                          sem="act")
            p.wait("vector", "act", e_sd)
            e_rstd = p.emit("vector", lambda e: e.reciprocal(rstd2[:], sd2[:]), sem="dve")
            p.wait("tensor", "dve", e_mf)
            p.emit("tensor",
                   lambda e: e.matmul(pl_bc[:, 0:512], cf_oner[:], m2_row[:],
                                      start=True, stop=True))
            p.wait("tensor", "dve", e_rstd)
            e_b2 = p.emit("tensor",
                          lambda e: e.matmul(pl_bc[:, 512:1024], cf_oner[:], rstd2[:],
                                             start=True, stop=True),
                          sem="pe")
            p.wait("vector", "pe", e_b2)
            p.emit("vector",
                   lambda e, sc=sc: e.tensor_copy(out=m2bc[:, sc * 512:(sc + 1) * 512],
                                                  in_=pl_bc[:, 0:512]))
            p.emit("vector",
                   lambda e, sc=sc: e.tensor_copy(out=rstd2bc[:, sc * 512:(sc + 1) * 512],
                                                  in_=pl_bc[:, 512:1024]),
                   sem="dve")
        e_bc2 = p.cnt["dve"]
        p.wait("vector", "dve", e_bc2)
        for dc in range(DC):
            for sc in range(2):
                p.emit("vector",
                       lambda e, dc=dc, sc=sc: e.tensor_sub(
                           tmpf[:],
                           resid[:, dc * OWN + sc * 512: dc * OWN + (sc + 1) * 512],
                           m2bc[:, sc * 512:(sc + 1) * 512]))
                p.emit("vector",
                       lambda e, dc=dc, sc=sc: e.tensor_mul(
                           x2_sb[:, dc * OWN + sc * 512: dc * OWN + (sc + 1) * 512],
                           tmpf[:],
                           rstd2bc[:, sc * 512:(sc + 1) * 512]),
                       sem="dve")
        e_x2_done = p.cnt["dve"]
        p.wait("tensor", "dve", e_x2_done)
        for sc in range(2):
            tgt = (pl_sum, pl_sumsq)[sc]
            for dc in range(DC):
                p.emit("tensor",
                       lambda e, dc=dc, sc=sc, tgt=tgt: e.matmul(
                           tgt[:], cb_onec[:],
                           x2_sb[:, dc * OWN + sc * 512: dc * OWN + (sc + 1) * 512],
                           start=(dc == 0), stop=(dc == DC - 1)),
                       sem="pe" if dc == DC - 1 else None)
            e_cs = p.cnt["pe"]
            p.wait("vector", "pe", e_cs)
            p.emit("vector",
                   lambda e, sc=sc, tgt=tgt: e.tensor_copy(
                       out=csx2_row[0:1, sc * 512:(sc + 1) * 512], in_=tgt[:]),
                   sem="dve")
    e_csx2 = p.cnt["dve"]

    wg0 = fsb("wg0", [128, DC * 128], BF16)
    wg1 = fsb("wg1", [128, DC * 128], BF16)
    wu0 = fsb("wu0", [128, DC * 128], BF16)
    wu1 = fsb("wu1", [128, DC * 128], BF16)
    dw0 = fsb("dw0", [128, 8 * 512], BF16)
    dw1 = fsb("dw1", [128, 8 * 512], BF16)

    def emit_gu_dma(ic):
        wgb = (wg0, wg1)[ic % 2]
        wub = (wu0, wu1)[ic % 2]
        a = p.dma("gpsimd",
                  wgb[:].rearrange("p (dc j) -> p dc j", dc=DC),
                  wgT.ap().rearrange("(dc p) i -> p dc i", p=128)
                  [:, :, ic * 128:(ic + 1) * 128])
        b = p.dma("gpsimd",
                  wub[:].rearrange("p (dc j) -> p dc j", dc=DC),
                  wuT.ap().rearrange("(dc p) i -> p dc i", p=128)
                  [:, :, ic * 128:(ic + 1) * 128])
        return b

    def emit_dw_dma(q4, hg, half):
        dwb = (dw0, dw1)[(2 * hg + half) % 2]
        return p.dma("gpsimd",
                     dwb[:].rearrange("p (ic j) -> p ic j", ic=8),
                     wdT.ap().rearrange("(ic p) h -> p ic h", p=128)
                     [:, q4 * 16 + half * 8: q4 * 16 + (half + 1) * 8,
                      hg * 512:(hg + 1) * 512])

    p.wait("gpsimd", "pe", p.cnt["pe"])
    p.wait("gpsimd", "dve", p.cnt["dve"])
    p.wait("gpsimd", "act", p.cnt["act"])
    e_gu = {0: emit_gu_dma(0), 1: emit_gu_dma(1)}
    e_dw = {}

    with nc.psum_tensor("pf", [128, 4096], F32) as pf:
        e_gu_evac_act = {}
        e_hmid_dve = {}
        e_FB_pe = 0
        e_FB_evac_dve = 0
        e_csh_copy = 0
        for q4 in range(4):
            for il in range(16):
                ic = q4 * 16 + il
                wgb = (wg0, wg1)[ic % 2]
                wub = (wu0, wu1)[ic % 2]
                p.wait("tensor", "gp", e_gu[ic])
                if ic >= 1:
                    p.wait("tensor", "act", e_gu_evac_act.get(ic - 1, 0))
                if il == 0:
                    p.wait("tensor", "dve", max(e_csx2, e_FB_evac_dve))
                for w, wb, reg in ((0, wgb, 0), (1, wub, 1)):
                    for sc in range(2):
                        off = reg * 1024 + sc * 512
                        for dc in range(DC):
                            p.emit("tensor",
                                   lambda e, dc=dc, sc=sc, off=off, wb=wb: e.matmul(
                                       pf[:, off:off + 512],
                                       wb[:, dc * 128:(dc + 1) * 128],
                                       x2_sb[:, dc * OWN + sc * 512: dc * OWN + (sc + 1) * 512],
                                       start=(dc == 0), stop=False,
                                       skip_group_check=True))
                        p.emit("tensor",
                               lambda e, sc=sc, off=off: e.matmul(
                                   pf[:, off:off + 512], cb_neg128[:],
                                   csx2_row[0:1, sc * 512:(sc + 1) * 512],
                                   start=False, stop=True, skip_group_check=True),
                               sem="pe" if (w == 1 and sc == 1) else None)
                e_gumm = p.cnt["pe"]
                p.wait("scalar", "pe", e_gumm)
                if ic >= 1:
                    p.wait("scalar", "dve", e_hmid_dve.get(ic - 1, 0))
                p.emit("scalar",
                       lambda e, ic=ic: e.activation(out=gsil[:], in_=pf[:, 0:1024],
                                                     func=AF.Silu, scale=gsc[:, ic:ic + 1]))
                e_ua = p.emit("scalar",
                              lambda e, ic=ic: e.activation(out=usca[:], in_=pf[:, 1024:2048],
                                                            func=AF.Copy,
                                                            scale=usc[:, ic:ic + 1]),
                              sem="act")
                e_gu_evac_act[ic] = e_ua
                p.wait("vector", "act", e_ua)
                if q4 > 0 and il == 0:
                    p.wait("vector", "pe", e_FB_pe)
                e_hm = p.emit("vector",
                              lambda e, il=il: e.tensor_mul(
                                  hmid[:, il * OWN:(il + 1) * OWN], gsil[:], usca[:]),
                              sem="dve")
                e_hmid_dve[ic] = e_hm
                p.wait("tensor", "dve", e_hm)
                if il == 0:
                    p.wait("tensor", "dve", e_csh_copy)
                for sc in range(2):
                    p.emit("tensor",
                           lambda e, il=il, sc=sc: e.matmul(
                               pf[0:1, 2048 + sc * 512: 2048 + (sc + 1) * 512],
                               cb_onec[:],
                               hmid[:, il * OWN + sc * 512: il * OWN + (sc + 1) * 512],
                               start=(il == 0), stop=(il == 15),
                               skip_group_check=True),
                           sem="pe" if (il == 15 and sc == 1) else None)
                if ic + 2 < 64 and (ic + 2) // 16 == q4:
                    p.wait("gpsimd", "pe", e_gumm)
                    e_gu[ic + 2] = emit_gu_dma(ic + 2)
                elif ic + 2 < 64 and (ic + 2) // 16 != q4:
                    pass
            e_csh_mm = p.cnt["pe"]
            p.wait("vector", "pe", e_csh_mm)
            e_csh_copy = p.emit("vector",
                                lambda e: e.tensor_copy(out=csh_row[:], in_=pf[0:1, 2048:3072]),
                                sem="dve")

            e_dw[(q4, 0, 0)] = emit_dw_dma(q4, 0, 0)
            e_dw[(q4, 0, 1)] = emit_dw_dma(q4, 0, 1)
            e_dwmm = {}
            for hg in range(4):
                if hg == 0:
                    p.wait("tensor", "dve", max(e_csh_copy, e_FB_evac_dve))
                else:
                    p.wait("tensor", "dve", e_FB_evac_dve)
                for half in range(2):
                    dwb = (dw0, dw1)[half]
                    p.wait("tensor", "gp", e_dw[(q4, hg, half)])
                    for icl in range(8):
                        il = half * 8 + icl
                        for hcl in range(4):
                            for sc in range(2):
                                r = hcl * 2 + sc
                                last = (icl == 7 and hcl == 3 and sc == 1)
                                p.emit("tensor",
                                       lambda e, icl=icl, il=il, hcl=hcl, sc=sc, r=r,
                                       dwb=dwb: e.matmul(
                                           pf[:, r * 512:(r + 1) * 512],
                                           dwb[:, icl * 512 + hcl * 128:
                                               icl * 512 + (hcl + 1) * 128],
                                           hmid[:, il * OWN + sc * 512:
                                                il * OWN + (sc + 1) * 512],
                                           start=(il == 0), stop=False,
                                           skip_group_check=True),
                                       sem="pe" if last else None)
                    e_dwmm[(q4, hg, half)] = p.cnt["pe"]
                    if hg + 1 < 4:
                        p.wait("gpsimd", "pe", e_dwmm[(q4, hg, half)])
                        e_dw[(q4, hg + 1, half)] = emit_dw_dma(q4, hg + 1, half)
                for hcl in range(4):
                    for sc in range(2):
                        r = hcl * 2 + sc
                        p.emit("tensor",
                               lambda e, r=r, sc=sc: e.matmul(
                                   pf[:, r * 512:(r + 1) * 512], cb_neg128[:],
                                   csh_row[0:1, sc * 512:(sc + 1) * 512],
                                   start=False, stop=True, skip_group_check=True),
                               sem="pe" if r == 7 else None)
                e_dn = p.cnt["pe"]
                p.wait("vector", "pe", e_dn)
                for hcl in range(4):
                    hc = hg * 4 + hcl
                    for sc in range(2):
                        r = hcl * 2 + sc
                        p.emit("vector",
                               lambda e, hc=hc, sc=sc, r=r: e.scalar_tensor_tensor(
                                   resid[:, hc * OWN + sc * 512: hc * OWN + (sc + 1) * 512],
                                   pf[:, r * 512:(r + 1) * 512],
                                   dsc[:, hc:hc + 1],
                                   resid[:, hc * OWN + sc * 512: hc * OWN + (sc + 1) * 512],
                                   ALU.mult, ALU.add),
                               sem="dve" if r == 7 else None)
                e_FB_evac_dve = p.cnt["dve"]
            e_FB_pe = p.cnt["pe"]
            if q4 + 1 < 4:
                p.wait("gpsimd", "pe", e_FB_pe)
                e_gu[(q4 + 1) * 16] = emit_gu_dma((q4 + 1) * 16)
                e_gu[(q4 + 1) * 16 + 1] = emit_gu_dma((q4 + 1) * 16 + 1)

    e_F_dve = p.cnt["dve"]
    e_F_pe = p.cnt["pe"]

    ot0 = fsb("ot0", [128, D], F32)
    ot1 = fsb("ot1", [128, D], F32)
    with nc.psum_tensor("pg0", [128, 128], F32) as pg0, \
         nc.psum_tensor("pg1", [128, 128], F32) as pg1:
        p.wait("tensor", "dve", e_F_dve)
        e_store = {}
        e_otcp = {}
        for st8 in range(8):
            otb = (ot0, ot1)[st8 % 2]
            pgs = (pg0, pg1)
            for dc in range(DC):
                pg = pgs[dc % 2]
                prev = (st8, dc - 2) if dc >= 2 else (st8 - 1, DC - 2 + dc)
                p.wait("tensor", "dve", e_otcp.get(prev, 0))
                e_tp = p.emit("tensor",
                              lambda e, dc=dc, st8=st8, pg=pg: e.transpose(
                                  pg[:],
                                  resid[:, dc * OWN + st8 * 128: dc * OWN + (st8 + 1) * 128],
                                  cf_id[:]),
                              sem="pe")
                p.wait("vector", "pe", e_tp)
                if st8 >= 2 and dc == 0:
                    p.wait("vector", "sy", e_store[st8 - 2])
                e_otcp[(st8, dc)] = p.emit("vector",
                                           lambda e, dc=dc, pg=pg, otb=otb: e.tensor_copy(
                                               out=otb[:, dc * 128:(dc + 1) * 128], in_=pg[:]),
                                           sem="dve")
            p.wait("sync", "dve", e_otcp[(st8, DC - 1)])
            e_store[st8] = p.dma("sync", out_own.ap()[st8 * 128:(st8 + 1) * 128, :], otb[:])
        p.wait("sync", "sy", p.cnt["sy"])

    with nc.Block() as block:
        @block.sync
        def _(e):
            for f in p.q["sync"]:
                f(e)

        @block.gpsimd
        def _(e):
            for f in p.q["gpsimd"]:
                f(e)

        @block.tensor
        def _(e):
            for f in p.q["tensor"]:
                f(e)

        @block.vector
        def _(e):
            for f in p.q["vector"]:
                f(e)

        @block.scalar
        def _(e):
            for f in p.q["scalar"]:
                f(e)

    fs.close()
    es.close()
    return nc


def _host_inputs(L0_attn_x, L0_attn_Wqkv_w, L0_attn_Wqkv_b, L0_attn_out_w,
                 L0_attn_out_b, L0_ffn_gate_w, L0_ffn_gate_scale, L0_ffn_up_w,
                 L0_ffn_up_scale, L0_ffn_down_w, L0_ffn_down_scale):
    C = np.ascontiguousarray
    xT = C(np.asarray(L0_attn_x, np.float32)[0].T)
    wqkvT = C(np.asarray(L0_attn_Wqkv_w, np.float32).T)
    qkv_b2d = C(np.asarray(L0_attn_Wqkv_b, np.float32).reshape(18, 128).T)
    owT = C(np.asarray(L0_attn_out_w, np.float32).T)
    ob2d = C(np.asarray(L0_attn_out_b, np.float32).reshape(16, 128).T)
    wgT = C(np.asarray(L0_ffn_gate_w, np.int32).T)
    wuT = C(np.asarray(L0_ffn_up_w, np.int32).T)
    wdT = C(np.asarray(L0_ffn_down_w, np.int32).T)
    gsc2d = C(np.asarray(L0_ffn_gate_scale, np.float32).reshape(64, 128).T)
    usc2d = C(np.asarray(L0_ffn_up_scale, np.float32).reshape(64, 128).T)
    dsc2d = C(np.asarray(L0_ffn_down_scale, np.float32).reshape(16, 128).T)

    kk = np.arange(128)[:, None]
    qq = np.arange(512)[None, :]
    tri = [(qq >= r * 128 + kk).astype(np.float32) for r in range(4)]
    maskd = C(np.concatenate(tri, axis=1))

    shared = dict(xT=xT, wqkvT=wqkvT, qkv_b2d=qkv_b2d, owT=owT, ob2d=ob2d,
                  wgT=wgT, wuT=wuT, wdT=wdT, gsc2d=gsc2d, usc2d=usc2d,
                  dsc2d=dsc2d, maskd=maskd)

    in_maps = []
    for c in range(8):
        ca, cb = c, 15 - c
        x_own = C(np.concatenate([xT[:, ca * 512:(ca + 1) * 512],
                                  xT[:, cb * 512:(cb + 1) * 512]], axis=1))
        ktbl = np.asarray([[512 * ca, 512 * ca + 256,
                            512 * cb, 512 * cb + 256]], np.int32)
        bk = np.full((128, 48), -1e30, np.float32)
        bk[:, 0] = bk[:, 1] = bk[:, 46] = bk[:, 47] = -12.0
        for s in range(2, 16):
            if s - 2 < 2 * ca:
                bk[:, s] = -12.0
        for s in range(16, 46):
            if s - 16 < 2 * cb:
                bk[:, s] = -12.0
        m = dict(shared)
        m["x_own"] = x_own
        m["ktbl"] = C(ktbl)
        m["bkill"] = C(bk)
        in_maps.append(m)
    return in_maps


def kernel(**inputs):
    in_maps = _host_inputs(**inputs)
    nc = build_program()
    res = run_bass_kernel_spmd(nc, in_maps, core_ids=list(range(8)))
    out = np.empty((S, D), np.float32)
    for c in range(8):
        shard = np.asarray(res.results[c]["out_own"])
        out[c * 512:(c + 1) * 512] = shard[:512]
        out[(15 - c) * 512:(16 - c) * 512] = shard[512:]
    return out.reshape(1, S, D)


# version 05
# speedup vs baseline: 1.0194x; 1.0194x over previous
import numpy as np
import ml_dtypes
from contextlib import ExitStack

import concourse.bass as bass
import concourse.mybir as mybir
from concourse.bass_utils import run_bass_kernel_spmd

F32 = mybir.dt.float32
BF16 = mybir.dt.bfloat16
I32 = mybir.dt.int32
AF = mybir.ActivationFunctionType
ALU = mybir.AluOpType
ET = mybir.EngineType
ds = bass.ds
BF = ml_dtypes.bfloat16

D = 2048
S = 8192
DC = 16
H = 16
OWN = 1024
NSC = 16
I_FFN = 8192
EPS = 1e-5
EXP_SCALE = float(1.0 / np.sqrt(128.0))
EXP_BIAS = -12.0

ENGS = ("sync", "gpsimd", "tensor", "vector", "scalar")


class Prog:
    def __init__(self, nc, es):
        self.nc = nc
        self.es = es
        self.q = {e: [] for e in ENGS}
        self.cnt = {}
        self.sems = {}
        for s in ("sy", "gp", "pe", "act", "dve"):
            self.sems[s] = es.enter_context(nc.semaphore(s))
            self.cnt[s] = 0

    def on(self, eng, fn):
        self.q[eng].append(fn)

    def wait(self, eng, name, val):
        if val is None or val <= 0:
            return
        self.q[eng].append(lambda e, s=self.sems[name], v=val: e.wait_ge(s, v))

    def emit(self, eng, fn, sem=None, by=1):
        if sem is None:
            self.q[eng].append(fn)
            return None
        self.cnt[sem] += by
        n = self.cnt[sem]
        self.q[eng].append(lambda e, f=fn, s=self.sems[sem], b=by: f(e).then_inc(s, b))
        return n

    def dma(self, eng, out_ap, in_ap):
        sem = "sy" if eng == "sync" else "gp"
        return self.emit(eng, lambda e, o=out_ap, i=in_ap: e.dma_start(out=o, in_=i),
                         sem=sem, by=16)

    def barrier(self, *engs):
        snap = dict(self.cnt)
        for eng in engs:
            for s, v in snap.items():
                self.wait(eng, s, v)


def build_program():
    nc = bass.Bass("TRN2", target_bir_lowering=False)

    xT = nc.dram_tensor("xT", [D, S], F32, kind="ExternalInput")
    x_own = nc.dram_tensor("x_own", [D, OWN], F32, kind="ExternalInput")
    wqkvT = nc.dram_tensor("wqkvT", [D, 2304], F32, kind="ExternalInput")
    qkv_b2d = nc.dram_tensor("qkv_b2d", [128, 18], F32, kind="ExternalInput")
    owT = nc.dram_tensor("owT", [D, D], F32, kind="ExternalInput")
    ob2d = nc.dram_tensor("ob2d", [128, 16], F32, kind="ExternalInput")
    wgT = nc.dram_tensor("wgT", [D, I_FFN], I32, kind="ExternalInput")
    wuT = nc.dram_tensor("wuT", [D, I_FFN], I32, kind="ExternalInput")
    wdT = nc.dram_tensor("wdT", [I_FFN, D], I32, kind="ExternalInput")
    gsc2d = nc.dram_tensor("gsc2d", [128, 64], F32, kind="ExternalInput")
    usc2d = nc.dram_tensor("usc2d", [128, 64], F32, kind="ExternalInput")
    dsc2d = nc.dram_tensor("dsc2d", [128, 16], F32, kind="ExternalInput")
    ktbl = nc.dram_tensor("ktbl", [1, 4], I32, kind="ExternalInput")
    bkill = nc.dram_tensor("bkill", [128, 48], F32, kind="ExternalInput")
    maskd = nc.dram_tensor("maskd", [128, 2048], F32, kind="ExternalInput")
    out_own = nc.dram_tensor("out_own", [OWN, D], F32, kind="ExternalOutput")

    id_b = nc.inline_tensor(np.eye(128, dtype=BF), "id_b")
    id_f = nc.inline_tensor(np.eye(128, dtype=np.float32), "id_f")
    onesc_b = nc.inline_tensor(np.ones((128, 1), dtype=BF), "onesc_b")
    onesc_f = nc.inline_tensor(np.ones((128, 1), dtype=np.float32), "onesc_f")
    onesr_f = nc.inline_tensor(np.ones((1, 128), dtype=np.float32), "onesr_f")
    neg128_r = nc.inline_tensor(np.full((1, 128), -128.0, dtype=BF), "neg128_r")
    zeros_r = nc.inline_tensor(np.zeros((1, 512), dtype=BF), "zeros_r")
    eps_c = nc.inline_tensor(np.full((128, 1), EPS, dtype=np.float32), "eps_c")
    expb_c = nc.inline_tensor(np.full((128, 1), EXP_BIAS, dtype=np.float32), "expb_c")

    es = ExitStack()
    p = Prog(nc, es)
    sb = lambda n, sh, dt: es.enter_context(nc.sbuf_tensor(n, sh, dt))

    resid = sb("resid", [128, DC * OWN], F32)
    cb_id = sb("cb_id", [128, 128], BF16)
    cb_onec = sb("cb_onec", [128, 1], BF16)
    cb_neg128 = sb("cb_neg128", [1, 128], BF16)
    cb_zrow = sb("cb_zrow", [1, 512], BF16)
    cf_id = sb("cf_id", [128, 128], F32)
    cf_onec = sb("cf_onec", [128, 1], F32)
    cf_oner = sb("cf_oner", [1, 128], F32)
    maskb = sb("maskb", [128, 2048], BF16)
    ktb = sb("ktb", [1, 4], I32)
    bkb = sb("bkb", [128, 48], F32)
    bqkv = sb("bqkv", [128, 18], F32)
    bo = sb("bo", [128, 16], F32)
    gsc = sb("gsc", [128, 64], F32)
    usc = sb("usc", [128, 64], F32)
    dsc = sb("dsc", [128, 16], F32)
    cf_eps = sb("cf_eps", [128, 1], F32)
    cf_expb = sb("cf_expb", [128, 1], F32)
    ow0 = sb("ow0", [128, DC * 128], BF16)
    ow1 = sb("ow1", [128, DC * 128], BF16)

    for dst, src in ((cb_id, id_b), (cb_onec, onesc_b), (cb_neg128, neg128_r),
                     (cb_zrow, zeros_r), (cf_id, id_f), (cf_onec, onesc_f),
                     (cf_oner, onesr_f), (cf_eps, eps_c), (cf_expb, expb_c)):
        p.dma("sync", dst[:], src.ap())
    p.dma("gpsimd", maskb[:], maskd.ap())
    for dst, src in ((ktb, ktbl), (bkb, bkill), (bqkv, qkv_b2d), (bo, ob2d),
                     (gsc, gsc2d), (usc, usc2d), (dsc, dsc2d)):
        p.dma("sync", dst[:], src.ap())
    e_resid_init = p.dma("sync", resid[:].rearrange("p (dc s) -> p dc s", dc=DC),
                         x_own.ap().rearrange("(dc p) s -> p dc s", p=128))
    e_consts_sy = p.cnt["sy"]
    e_consts_gp = p.cnt["gp"]
    for _eng in ("tensor", "vector", "scalar"):
        p.wait(_eng, "sy", e_consts_sy)

    ac = ExitStack()
    asb = lambda n, sh, dt: ac.enter_context(nc.sbuf_tensor(n, sh, dt))
    k_fm = asb("k_fm", [128, S], BF16)
    v_tm = asb("v_tm", [128, S], BF16)

    pa = ExitStack()
    pa_t = lambda n, sh, dt: pa.enter_context(nc.sbuf_tensor(n, sh, dt))
    pa_p = lambda n, sh, dt: pa.enter_context(nc.psum_tensor(n, sh, dt))
    m_row_b = pa_t("m_row_b", [1, 512], BF16)
    m_row_f = pa_t("m_row_f", [1, 512], F32)
    msq_row = pa_t("msq_row", [1, 512], F32)
    var_row = pa_t("var_row", [1, 512], F32)
    sd_row = pa_t("sd_row", [1, 512], F32)
    rstd_row = pa_t("rstd_row", [1, 512], F32)
    xstage = pa_t("xstage", [128, 2 * DC * 512], BF16)
    wkv = pa_t("wkv", [128, DC * 256], BF16)
    wsn_kv = pa_t("wsn_kv", [1, 256], BF16)
    rstdb_sb = pa_t("rstdb_sb", [128, 512], F32)
    kvtmp = pa_t("kvtmp", [128, 512], F32)
    vchunk = pa_t("vchunk", [128, 512], BF16)
    xsq = pa_t("xsq", [128, 2 * 512], BF16)
    pa_sum = pa_p("pa_sum", [1, 512], F32)
    pa_sumsq = pa_p("pa_sumsq", [1, 512], F32)
    pa_kv = pa_p("pa_kv", [128, 1024], F32)
    pa_bc = pa_p("pa_bc", [128, 512], F32)
    pa_vt = pa_p("pa_vt", [128, 128], BF16)
    pa_ws = pa_p("pa_ws", [1, 256], F32)
    if True:
        e_wkv = p.dma("gpsimd",
                      wkv[:].rearrange("p (dc e) -> p dc e", dc=DC),
                      wqkvT.ap().rearrange("(dc p) e -> p dc e", p=128)[:, :, 2048:2304])

        def emit_xs_dma(sc):
            buf = sc % 2
            return p.dma("gpsimd",
                         xstage[:, buf * DC * 512:(buf + 1) * DC * 512]
                         .rearrange("p (dc s) -> p dc s", dc=DC),
                         xT.ap().rearrange("(dc p) s -> p dc s", p=128)
                         [:, :, sc * 512:(sc + 1) * 512])

        e_xs = [emit_xs_dma(0), emit_xs_dma(1)]

        p.wait("tensor", "gp", e_wkv)
        for dc in range(DC):
            p.emit("tensor",
                   lambda e, dc=dc: e.matmul(pa_ws[:], cb_onec[:],
                                             wkv[:, dc * 256:(dc + 1) * 256],
                                             start=(dc == 0), stop=(dc == DC - 1)),
                   sem="pe" if dc == DC - 1 else None)
        e_ws = p.cnt["pe"]
        p.wait("vector", "pe", e_ws)
        e_wsn = p.emit("vector",
                       lambda e: e.tensor_scalar_mul(wsn_kv[:], pa_ws[:], -1.0), sem="dve")

        e_prev_dve_end = 0
        e_prev_act_v = 0
        e_prev_vt_pe = 0
        for sc in range(NSC):
            buf = sc % 2
            xsl = lambda dc, buf=buf: xstage[:, buf * DC * 512 + dc * 512:
                                             buf * DC * 512 + (dc + 1) * 512]
            p.wait("tensor", "gp", e_xs[sc])
            p.wait("tensor", "dve", e_prev_dve_end)
            for dc in range(DC):
                p.emit("tensor",
                       lambda e, dc=dc, xsl=xsl: e.matmul(pa_sum[:], cb_onec[:], xsl(dc),
                                                          start=(dc == 0), stop=(dc == DC - 1)),
                       sem="pe" if dc == DC - 1 else None)
            e_sum = p.cnt["pe"]
            for dc in range(DC):
                for jc in range(2):
                    p.emit("tensor",
                           lambda e, dc=dc, jc=jc, xsl=xsl: e.matmul(
                               pa_kv[:, jc * 512:(jc + 1) * 512],
                               wkv[:, dc * 256 + jc * 128: dc * 256 + (jc + 1) * 128],
                               xsl(dc), start=(dc == 0), stop=False,
                               skip_group_check=True))
            p.wait("scalar", "gp", e_xs[sc])
            e_mm_sq = {}
            e_sq = {}
            for dc in range(DC):
                if dc >= 2:
                    p.wait("scalar", "pe", e_mm_sq[dc - 2])
                e_sq[dc] = p.emit("scalar",
                                  lambda e, dc=dc, xsl=xsl: e.activation(
                                      out=xsq[:, (dc % 2) * 512:(dc % 2 + 1) * 512],
                                      in_=xsl(dc), func=AF.Square),
                                  sem="act")
                p.wait("tensor", "act", e_sq[dc])
                e_mm_sq[dc] = p.emit("tensor",
                                     lambda e, dc=dc: e.matmul(
                                         pa_sumsq[:], cb_onec[:],
                                         xsq[:, (dc % 2) * 512:(dc % 2 + 1) * 512],
                                         start=(dc == 0), stop=(dc == DC - 1)),
                                     sem="pe")
            e_sumsq = p.cnt["pe"]
            p.wait("vector", "pe", e_sum)
            p.emit("vector", lambda e: e.tensor_scalar_mul(m_row_b[:], pa_sum[:], 1.0 / D))
            e_mb = p.emit("vector",
                          lambda e: e.tensor_scalar_mul(m_row_f[:], pa_sum[:], 1.0 / D),
                          sem="dve")
            p.wait("vector", "dve", e_mb)
            p.emit("vector", lambda e: e.tensor_mul(msq_row[:], m_row_f[:], m_row_f[:]))
            p.wait("vector", "pe", e_sumsq)
            e_var = p.emit("vector",
                           lambda e: e.scalar_tensor_tensor(var_row[:], pa_sumsq[:], 1.0 / D,
                                                            msq_row[:], ALU.mult, ALU.subtract),
                           sem="dve")
            p.wait("scalar", "dve", e_var)
            e_sd = p.emit("scalar",
                          lambda e: e.activation(out=sd_row[:], in_=var_row[:],
                                                 func=AF.Sqrt, bias=cf_eps[0:1, :]),
                          sem="act")
            p.wait("vector", "act", e_sd)
            e_rstd = p.emit("vector", lambda e: e.reciprocal(rstd_row[:], sd_row[:]), sem="dve")
            p.wait("tensor", "dve", max(e_mb, e_wsn))
            for jc in range(2):
                p.emit("tensor",
                       lambda e, jc=jc: e.matmul(pa_kv[:, jc * 512:(jc + 1) * 512],
                                                 wsn_kv[0:1, jc * 128:(jc + 1) * 128],
                                                 m_row_b[:], start=False, stop=True,
                                                 skip_group_check=True))
            p.wait("tensor", "dve", e_rstd)
            e_bc = p.emit("tensor",
                          lambda e: e.matmul(pa_bc[:], cf_oner[:], rstd_row[:],
                                             start=True, stop=True),
                          sem="pe")
            p.wait("vector", "pe", e_bc)
            e_bcc = p.emit("vector", lambda e: e.tensor_copy(out=rstdb_sb[:], in_=pa_bc[:]),
                           sem="dve")
            p.wait("vector", "dve", e_bcc)
            e_km = p.emit("vector",
                          lambda e: e.tensor_mul(kvtmp[:], pa_kv[:, 0:512], rstdb_sb[:]),
                          sem="dve")
            p.emit("vector",
                   lambda e, sc=sc: e.tensor_scalar_add(
                       k_fm[:, sc * 512:(sc + 1) * 512], kvtmp[:], bqkv[:, 16:17]))
            e_vm = p.emit("vector",
                          lambda e: e.tensor_mul(kvtmp[:], pa_kv[:, 512:1024], rstdb_sb[:]),
                          sem="dve")
            p.wait("vector", "pe", e_prev_vt_pe)
            e_v = p.emit("vector",
                         lambda e: e.tensor_scalar_add(vchunk[:], kvtmp[:], bqkv[:, 17:18]),
                         sem="dve")
            e_prev_act_v = e_v
            p.wait("tensor", "dve", e_v)
            e_vc = 0
            for j in range(4):
                p.wait("tensor", "dve", e_vc)
                e_t = p.emit("tensor",
                             lambda e, j=j: e.transpose(pa_vt[:],
                                                        vchunk[:, j * 128:(j + 1) * 128],
                                                        cb_id[:]),
                             sem="pe")
                p.wait("vector", "pe", e_t)
                e_vc = p.emit("vector",
                              lambda e, sc=sc, j=j: e.tensor_copy(
                                  out=v_tm[:, sc * 512 + j * 128: sc * 512 + (j + 1) * 128],
                                  in_=pa_vt[:]),
                              sem="dve")
            e_prev_vt_pe = p.cnt["pe"]
            e_prev_dve_end = p.cnt["dve"]
            if sc + 2 < NSC:
                p.wait("gpsimd", "pe", p.cnt["pe"])
                p.wait("gpsimd", "act", e_sq[DC - 1])
                e_xs.append(emit_xs_dma(sc + 2))

        p.barrier("tensor", "vector", "scalar")
    pa.close()

    ctxb = asb("ctxb", [128, H * OWN], BF16)
    x1own = asb("x1own", [128, DC * OWN], BF16)
    bsub = ExitStack()
    m_own_bc = bsub.enter_context(nc.sbuf_tensor("m_own_bc", [128, OWN], F32))
    rstd_own_bc = bsub.enter_context(nc.sbuf_tensor("rstd_own_bc", [128, OWN], F32))
    xsq = bsub.enter_context(nc.sbuf_tensor("xsq_b", [128, 2 * 512], BF16))
    kvtmp = bsub.enter_context(nc.sbuf_tensor("kvtmp_b", [128, 512], F32))
    m_row_f = bsub.enter_context(nc.sbuf_tensor("m_row_fb", [1, 512], F32))
    msq_row = bsub.enter_context(nc.sbuf_tensor("msq_rowb", [1, 512], F32))
    var_row = bsub.enter_context(nc.sbuf_tensor("var_rowb", [1, 512], F32))
    sd_row = bsub.enter_context(nc.sbuf_tensor("sd_rowb", [1, 512], F32))
    rstd_row = bsub.enter_context(nc.sbuf_tensor("rstd_rowb", [1, 512], F32))

    p.wait("gpsimd", "pe", p.cnt["pe"])
    p.wait("gpsimd", "act", p.cnt["act"])
    p.wait("gpsimd", "dve", p.cnt["dve"])
    e_xob = p.dma("gpsimd",
                  x1own[:].rearrange("p (dc s) -> p dc s", dc=DC),
                  x_own.ap().rearrange("(dc p) s -> p dc s", p=128))

    with nc.psum_tensor("pb_sum", [1, 512], F32) as pb_sum, \
         nc.psum_tensor("pb_sumsq", [1, 512], F32) as pb_sumsq, \
         nc.psum_tensor("pb_bc", [128, 1024], F32) as pb_bc:
        p.wait("tensor", "gp", e_xob)
        p.wait("scalar", "gp", e_xob)
        for sc in range(2):
            xol = lambda dc, sc=sc: x1own[:, dc * OWN + sc * 512: dc * OWN + (sc + 1) * 512]
            if sc == 1:
                p.wait("tensor", "dve", p.cnt["dve"])
            for dc in range(DC):
                p.emit("tensor",
                       lambda e, dc=dc, xol=xol: e.matmul(pb_sum[:], cb_onec[:], xol(dc),
                                                          start=(dc == 0), stop=(dc == DC - 1)),
                       sem="pe" if dc == DC - 1 else None)
            e_sum = p.cnt["pe"]
            e_mm_sq = {}
            e_sq = {}
            for dc in range(DC):
                if dc >= 2:
                    p.wait("scalar", "pe", e_mm_sq[dc - 2])
                e_sq[dc] = p.emit("scalar",
                                  lambda e, dc=dc, xol=xol: e.activation(
                                      out=xsq[:, (dc % 2) * 512:(dc % 2 + 1) * 512],
                                      in_=xol(dc), func=AF.Square),
                                  sem="act")
                p.wait("tensor", "act", e_sq[dc])
                e_mm_sq[dc] = p.emit("tensor",
                                     lambda e, dc=dc: e.matmul(
                                         pb_sumsq[:], cb_onec[:],
                                         xsq[:, (dc % 2) * 512:(dc % 2 + 1) * 512],
                                         start=(dc == 0), stop=(dc == DC - 1)),
                                     sem="pe")
            e_sumsq = p.cnt["pe"]
            p.wait("vector", "pe", e_sum)
            e_mf = p.emit("vector",
                          lambda e: e.tensor_scalar_mul(m_row_f[:], pb_sum[:], 1.0 / D),
                          sem="dve")
            p.wait("vector", "dve", e_mf)
            p.emit("vector", lambda e: e.tensor_mul(msq_row[:], m_row_f[:], m_row_f[:]))
            p.wait("vector", "pe", e_sumsq)
            e_var = p.emit("vector",
                           lambda e: e.scalar_tensor_tensor(var_row[:], pb_sumsq[:], 1.0 / D,
                                                            msq_row[:], ALU.mult, ALU.subtract),
                           sem="dve")
            p.wait("scalar", "dve", e_var)
            e_sd = p.emit("scalar",
                          lambda e: e.activation(out=sd_row[:], in_=var_row[:],
                                                 func=AF.Sqrt, bias=cf_eps[0:1, :]),
                          sem="act")
            p.wait("vector", "act", e_sd)
            e_rstd = p.emit("vector", lambda e: e.reciprocal(rstd_row[:], sd_row[:]), sem="dve")
            p.wait("tensor", "dve", e_mf)
            e_b1 = p.emit("tensor",
                          lambda e: e.matmul(pb_bc[:, 0:512], cf_oner[:], m_row_f[:],
                                             start=True, stop=True),
                          sem="pe")
            p.wait("tensor", "dve", e_rstd)
            e_b2 = p.emit("tensor",
                          lambda e: e.matmul(pb_bc[:, 512:1024], cf_oner[:], rstd_row[:],
                                             start=True, stop=True),
                          sem="pe")
            p.wait("vector", "pe", e_b2)
            p.emit("vector",
                   lambda e, sc=sc: e.tensor_copy(out=m_own_bc[:, sc * 512:(sc + 1) * 512],
                                                  in_=pb_bc[:, 0:512]))
            p.emit("vector",
                   lambda e, sc=sc: e.tensor_copy(out=rstd_own_bc[:, sc * 512:(sc + 1) * 512],
                                                  in_=pb_bc[:, 512:1024]),
                   sem="dve")
        e_bcast_done = p.cnt["dve"]
        p.wait("vector", "dve", e_bcast_done)
        for dc in range(DC):
            for sc in range(2):
                p.emit("vector",
                       lambda e, dc=dc, sc=sc: e.tensor_sub(
                           kvtmp[:],
                           x1own[:, dc * OWN + sc * 512: dc * OWN + (sc + 1) * 512],
                           m_own_bc[:, sc * 512:(sc + 1) * 512]))
                p.emit("vector",
                       lambda e, dc=dc, sc=sc: e.tensor_mul(
                           x1own[:, dc * OWN + sc * 512: dc * OWN + (sc + 1) * 512],
                           kvtmp[:],
                           rstd_own_bc[:, sc * 512:(sc + 1) * 512]),
                       sem="dve")
    e_x1_done = p.cnt["dve"]
    bsub.close()

    q_sb = asb("q_sb", [128, 2 * OWN], BF16)
    p0_sb = asb("p0_sb", [128, 2 * 1024], BF16)
    wq0 = asb("wq0", [128, DC * 128], BF16)
    wq1 = asb("wq1", [128, DC * 128], BF16)
    kst = asb("kst", [128, 4 * 256], BF16)
    vst = asb("vst", [128, 4 * 256], BF16)
    recA = asb("recA", [1, 512], F32)
    recB = asb("recB", [1, 512], F32)
    recbcA = asb("recbcA", [128, 512], F32)
    recbcB = asb("recbcB", [128, 512], F32)
    d32row = asb("d32row", [1, 512], F32)
    densum = asb("densum", [1, 512], F32)

    cps = ExitStack()
    S0 = cps.enter_context(nc.psum_tensor("pc_s0", [128, 1024], F32))
    S1 = cps.enter_context(nc.psum_tensor("pc_s1", [128, 1024], F32))
    ctx_ps = cps.enter_context(nc.psum_tensor("pc_ctx", [128, 1024], F32))
    den_ps = cps.enter_context(nc.psum_tensor("pc_den", [128, 1024], F32))
    SBUFS = (S0, S1)
    NG = 48

    def emit_wq_dma(h):
        wqb = (wq0, wq1)[h % 2]
        return p.dma("gpsimd",
                     wqb[:].rearrange("p (dc j) -> p dc j", dc=DC),
                     wqkvT.ap().rearrange("(dc p) e -> p dc e", p=128)
                     [:, :, h * 128:(h + 1) * 128])

    p.wait("gpsimd", "pe", p.cnt["pe"])
    p.wait("gpsimd", "act", p.cnt["act"])
    p.wait("gpsimd", "dve", p.cnt["dve"])
    e_wq = [emit_wq_dma(0), emit_wq_dma(1)]

    p.barrier("tensor", "vector", "scalar")
    p.wait("vector", "gp", e_consts_gp)
    p.wait("vector", "sy", e_consts_sy)
    p.wait("tensor", "sy", e_consts_sy)
    p.wait("scalar", "sy", e_consts_sy)

    _dve_et = nc.vector.engine
    r_kt = [nc.alloc_register(_dve_et, f"r_kt{j}") for j in range(4)]
    for j in range(4):
        p.emit("vector", lambda e, j=j: e.reg_load(r_kt[j], ktb[0:1, j:j + 1]))
        kv_ = bass.RuntimeValue(val=r_kt[j], min_val=0, max_val=7936)
        p.emit("vector",
               lambda e, j=j, kv_=kv_: e.tensor_copy(
                   out=kst[:, j * 256:(j + 1) * 256], in_=k_fm[:, ds(kv_, 256)]))
        p.emit("vector",
               lambda e, j=j, kv_=kv_: e.tensor_copy(
                   out=vst[:, j * 256:(j + 1) * 256], in_=v_tm[:, ds(kv_, 256)]),
               sem="dve")
    e_stage = p.cnt["dve"]
    p.wait("tensor", "dve", e_stage)

    def slot_params(s):
        if s == 0:
            return (lambda i: kst[:, 0 * 256 + i * 128: 0 * 256 + (i + 1) * 128],
                    lambda i: vst[:, 0 * 256 + i * 128: 0 * 256 + (i + 1) * 128], 0, 0)
        if s == 1:
            return (lambda i: kst[:, 1 * 256 + i * 128: 1 * 256 + (i + 1) * 128],
                    lambda i: vst[:, 1 * 256 + i * 128: 1 * 256 + (i + 1) * 128], 0, 1024)
        if s == 46:
            return (lambda i: kst[:, 2 * 256 + i * 128: 2 * 256 + (i + 1) * 128],
                    lambda i: vst[:, 2 * 256 + i * 128: 2 * 256 + (i + 1) * 128], 512, 0)
        if s == 47:
            return (lambda i: kst[:, 3 * 256 + i * 128: 3 * 256 + (i + 1) * 128],
                    lambda i: vst[:, 3 * 256 + i * 128: 3 * 256 + (i + 1) * 128], 512, 1024)
        if s < 16:
            kb = 256 * (s - 2)
            co = 0
        else:
            kb = 256 * (s - 16)
            co = 512
        return (lambda i, kb=kb: k_fm[:, kb + i * 128: kb + (i + 1) * 128],
                lambda i, kb=kb: v_tm[:, kb + i * 128: kb + (i + 1) * 128], co, None)

    slot = 0
    e_cons = {}
    e_p0rdy = {}
    last_p0_pe = [0, 0]
    pend_grp = [None]
    e_evac_dve = 0
    e_qk_end_byhead = {}

    def emit_pv(gi):
        s = gi % NG
        _, v_src, co, _ = slot_params(s)
        p.wait("tensor", *e_p0rdy[gi])
        for i in range(2):
            p.emit("tensor",
                   lambda e, gi=gi, i=i, co=co, v_src=v_src: e.matmul(
                       ctx_ps[:, co:co + 512],
                       v_src(i),
                       p0_sb[:, (gi % 2) * 1024 + i * 512:(gi % 2) * 1024 + (i + 1) * 512],
                       start=False, stop=False, skip_group_check=True))
        for i in range(2):
            p.emit("tensor",
                   lambda e, gi=gi, i=i, co=co: e.matmul(
                       den_ps[i * 32:i * 32 + 1, co:co + 512],
                       cb_onec[:],
                       p0_sb[:, (gi % 2) * 1024 + i * 512:(gi % 2) * 1024 + (i + 1) * 512],
                       start=False, stop=False, skip_group_check=True,
                       tile_position=(0, 32 * i)),
                   sem="pe" if i == 1 else None)
        last_p0_pe[gi % 2] = p.cnt["pe"]

    for h in range(H):
        wqb = (wq0, wq1)[h % 2]
        t_qp = slot
        slot += 1
        qps = SBUFS[t_qp % 2]
        if t_qp - 2 in e_cons:
            p.wait("tensor", *e_cons[t_qp - 2])
        p.wait("tensor", "gp", e_wq[h])
        if h == 0:
            p.wait("tensor", "dve", e_x1_done)
        for sc in range(2):
            for dc in range(DC):
                p.emit("tensor",
                       lambda e, dc=dc, sc=sc, qps=qps, wqb=wqb: e.matmul(
                           qps[:, sc * 512:(sc + 1) * 512],
                           wqb[:, dc * 128:(dc + 1) * 128],
                           x1own[:, dc * OWN + sc * 512: dc * OWN + (sc + 1) * 512],
                           start=(dc == 0), stop=(dc == DC - 1),
                           skip_group_check=True),
                       sem="pe" if (dc == DC - 1 and sc == 1) else None)
        e_qp = p.cnt["pe"]
        p.wait("vector", "pe", e_qp)
        if h >= 2:
            p.wait("vector", "pe", e_qk_end_byhead[h - 2])
        e_qe = p.emit("vector",
                      lambda e, h=h, qps=qps: e.tensor_scalar_add(
                          q_sb[:, (h % 2) * OWN:(h % 2 + 1) * OWN],
                          qps[:], bqkv[:, h:h + 1]),
                      sem="dve")
        e_cons[t_qp] = ("dve", e_qe)
        if h + 2 < H:
            p.wait("gpsimd", "pe", e_qp)
            e_wq.append(emit_wq_dma(h + 2))

        for g in range(NG):
            gi = h * NG + g
            t = slot
            slot += 1
            Sb = SBUFS[t % 2]
            k_src, v_src, co, moff = slot_params(g)
            if t - 2 in e_cons:
                p.wait("tensor", *e_cons[t - 2])
            if g == 0:
                p.wait("tensor", "dve", e_qe)

            if g == 1:
                p.wait("tensor", "dve", e_evac_dve)
                p.emit("tensor", lambda e: e.matmul(ctx_ps[:, 0:512], cb_zrow[0:1, 0:128],
                                                    cb_zrow[:], start=True, stop=True,
                                                    skip_group_check=True))
                p.emit("tensor", lambda e: e.matmul(ctx_ps[:, 512:1024], cb_zrow[0:1, 0:128],
                                                    cb_zrow[:], start=True, stop=True,
                                                    skip_group_check=True))
                for _co in (0, 512):
                    p.emit("tensor", lambda e, _co=_co: e.matmul(
                        den_ps[0:1, _co:_co + 512], cb_zrow[0:1, 0:1],
                        cb_zrow[:], start=True, stop=True, skip_group_check=True))
                    p.emit("tensor", lambda e, _co=_co: e.matmul(
                        den_ps[32:33, _co:_co + 512], cb_zrow[0:1, 0:1],
                        cb_zrow[:], start=True, stop=True, skip_group_check=True,
                        tile_position=(0, 32)))

            for i in range(2):
                p.emit("tensor",
                       lambda e, Sb=Sb, k_src=k_src, co=co, h=h, i=i: e.matmul(
                           Sb[:, i * 512:(i + 1) * 512],
                           k_src(i),
                           q_sb[:, (h % 2) * OWN + co:(h % 2) * OWN + co + 512],
                           start=True, stop=True, skip_group_check=True),
                       sem="pe" if i == 1 else None)
            e_qk = p.cnt["pe"]

            if pend_grp[0] is not None:
                emit_pv(pend_grp[0])
            pend_grp[0] = gi

            p.wait("scalar", "pe", e_qk)
            p.wait("scalar", "pe", last_p0_pe[gi % 2])
            e_exp = p.emit("scalar",
                           lambda e, Sb=Sb, gi=gi, g=g: e.activation(
                               out=p0_sb[:, (gi % 2) * 1024:(gi % 2 + 1) * 1024],
                               in_=Sb[:], func=AF.Exp, scale=EXP_SCALE,
                               bias=bkb[:, g:g + 1]),
                           sem="act")
            e_cons[t] = ("act", e_exp)

            if moff is None:
                e_p0rdy[gi] = ("act", e_exp)
            else:
                p.wait("vector", "act", e_exp)
                e_m = p.emit("vector",
                             lambda e, gi=gi, moff=moff: e.tensor_mul(
                                 p0_sb[:, (gi % 2) * 1024:(gi % 2 + 1) * 1024],
                                 p0_sb[:, (gi % 2) * 1024:(gi % 2 + 1) * 1024],
                                 maskb[:, moff:moff + 1024]),
                             sem="dve")
                e_p0rdy[gi] = ("dve", e_m)

        e_qk_end_byhead[h] = p.cnt["pe"]

        emit_pv(pend_grp[0])
        pend_grp[0] = None
        e_last_pv = p.cnt["pe"]

        p.wait("vector", "pe", e_last_pv)
        p.emit("vector", lambda e: e.tensor_copy(out=d32row[:], in_=den_ps[32:33, 0:512]))
        p.emit("vector", lambda e: e.tensor_add(densum[:], den_ps[0:1, 0:512], d32row[:]))
        e_rA = p.emit("vector", lambda e: e.reciprocal(recA[:], densum[:]), sem="dve")
        p.emit("vector", lambda e: e.tensor_copy(out=d32row[:], in_=den_ps[32:33, 512:1024]))
        p.emit("vector", lambda e: e.tensor_add(densum[:], den_ps[0:1, 512:1024], d32row[:]))
        e_rB = p.emit("vector", lambda e: e.reciprocal(recB[:], densum[:]), sem="dve")

        t_bA = slot
        slot += 1
        t_bB = slot
        slot += 1
        SbA = SBUFS[t_bA % 2]
        SbB = SBUFS[t_bB % 2]
        if t_bA - 2 in e_cons:
            p.wait("tensor", *e_cons[t_bA - 2])
        p.wait("tensor", "dve", e_rA)
        e_bA = p.emit("tensor",
                      lambda e, SbA=SbA: e.matmul(SbA[:, 0:512], cf_oner[:], recA[:],
                                                  start=True, stop=True,
                                                  skip_group_check=True),
                      sem="pe")
        if t_bB - 2 in e_cons:
            p.wait("tensor", *e_cons[t_bB - 2])
        p.wait("tensor", "dve", e_rB)
        e_bB = p.emit("tensor",
                      lambda e, SbB=SbB: e.matmul(SbB[:, 0:512], cf_oner[:], recB[:],
                                                  start=True, stop=True,
                                                  skip_group_check=True),
                      sem="pe")
        p.wait("vector", "pe", e_bA)
        e_cA = p.emit("vector",
                      lambda e, SbA=SbA: e.tensor_copy(out=recbcA[:], in_=SbA[:, 0:512]),
                      sem="dve")
        e_cons[t_bA] = ("dve", e_cA)
        p.wait("vector", "dve", e_cA)
        p.emit("vector",
               lambda e, h=h: e.tensor_mul(ctxb[:, h * OWN: h * OWN + 512],
                                           ctx_ps[:, 0:512], recbcA[:]),
               sem="dve")
        p.wait("vector", "pe", e_bB)
        e_cB = p.emit("vector",
                      lambda e, SbB=SbB: e.tensor_copy(out=recbcB[:], in_=SbB[:, 0:512]),
                      sem="dve")
        e_cons[t_bB] = ("dve", e_cB)
        p.wait("vector", "dve", e_cB)
        e_evac_dve = p.emit("vector",
                            lambda e, h=h: e.tensor_mul(ctxb[:, h * OWN + 512:(h + 1) * OWN],
                                                        ctx_ps[:, 512:1024], recbcB[:]),
                            sem="dve")

    e_ctx_done = e_evac_dve
    cps.close()

    def emit_ow_dma(ec):
        owb = (ow0, ow1)[ec % 2]
        return p.dma("gpsimd",
                     owb[:].rearrange("p (dc j) -> p dc j", dc=DC),
                     owT.ap().rearrange("(dc p) e -> p dc e", p=128)
                     [:, :, ec * 128:(ec + 1) * 128])

    e_ow = [emit_ow_dma(0), emit_ow_dma(1)]

    with nc.psum_tensor("pd", [128, 1024], F32) as pd:
        p.wait("tensor", "dve", e_ctx_done)
        p.wait("vector", "sy", e_resid_init)
        e_ev_bank = {0: 0, 1: 0}
        e_mm_byec = {}
        for ec in range(DC):
            owb = (ow0, ow1)[ec % 2]
            p.wait("tensor", "gp", e_ow[ec])
            for sc in range(2):
                bank = (2 * ec + sc) % 2
                p.wait("tensor", "dve", e_ev_bank[bank])
                for dc in range(DC):
                    p.emit("tensor",
                           lambda e, dc=dc, sc=sc, bank=bank, owb=owb: e.matmul(
                               pd[:, bank * 512:(bank + 1) * 512],
                               owb[:, dc * 128:(dc + 1) * 128],
                               ctxb[:, dc * OWN + sc * 512: dc * OWN + (sc + 1) * 512],
                               start=(dc == 0), stop=(dc == DC - 1),
                               skip_group_check=True),
                           sem="pe" if dc == DC - 1 else None)
                e_mm = p.cnt["pe"]
                p.wait("vector", "pe", e_mm)
                e_ev_bank[bank] = p.emit(
                    "vector",
                    lambda e, ec=ec, sc=sc, bank=bank: e.scalar_tensor_tensor(
                        resid[:, ec * OWN + sc * 512: ec * OWN + (sc + 1) * 512],
                        pd[:, bank * 512:(bank + 1) * 512],
                        bo[:, ec:ec + 1],
                        resid[:, ec * OWN + sc * 512: ec * OWN + (sc + 1) * 512],
                        ALU.add, ALU.add),
                    sem="dve")
            e_mm_byec[ec] = p.cnt["pe"]
            if ec + 2 < DC:
                p.wait("gpsimd", "pe", e_mm_byec[ec])
                e_ow.append(emit_ow_dma(ec + 2))
    e_D_dve = p.cnt["dve"]
    e_D_pe = p.cnt["pe"]

    ac.close()

    fs = ExitStack()
    fsb = lambda n, sh, dt: fs.enter_context(nc.sbuf_tensor(n, sh, dt))
    x2_sb = fsb("x2_sb", [128, DC * OWN], BF16)
    hmid = fsb("hmid", [128, 16 * OWN], BF16)
    csx2_row = fsb("csx2_row", [1, 1024], BF16)
    csh_row = fsb("csh_row", [1, 1024], BF16)
    gsil = fsb("gsil", [128, 1024], BF16)
    usca = fsb("usca", [128, 1024], BF16)

    p.barrier("tensor", "vector", "scalar")

    with nc.psum_tensor("pl_sum", [1, 512], F32) as pl_sum, \
         nc.psum_tensor("pl_sumsq", [1, 512], F32) as pl_sumsq, \
         nc.psum_tensor("pl_bc", [128, 1024], F32) as pl_bc, \
         nc.sbuf_tensor("m2bc", [128, OWN], F32) as m2bc, \
         nc.sbuf_tensor("rstd2bc", [128, OWN], F32) as rstd2bc, \
         nc.sbuf_tensor("m2_row", [1, 512], F32) as m2_row, \
         nc.sbuf_tensor("msq2", [1, 512], F32) as msq2, \
         nc.sbuf_tensor("var2", [1, 512], F32) as var2, \
         nc.sbuf_tensor("sd2", [1, 512], F32) as sd2, \
         nc.sbuf_tensor("rstd2", [1, 512], F32) as rstd2, \
         nc.sbuf_tensor("xsq2", [128, 2 * 512], BF16) as xsq2, \
         nc.sbuf_tensor("tmpf", [128, 512], F32) as tmpf:
        for sc in range(2):
            rsl = lambda dc, sc=sc: resid[:, dc * OWN + sc * 512: dc * OWN + (sc + 1) * 512]
            if sc == 1:
                p.wait("tensor", "dve", p.cnt["dve"])
            for dc in range(DC):
                p.emit("tensor",
                       lambda e, dc=dc, rsl=rsl: e.matmul(pl_sum[:], cf_onec[:], rsl(dc),
                                                          start=(dc == 0), stop=(dc == DC - 1)),
                       sem="pe" if dc == DC - 1 else None)
            e_sum = p.cnt["pe"]
            e_mm_sq = {}
            e_sq = {}
            for dc in range(DC):
                if dc >= 2:
                    p.wait("scalar", "pe", e_mm_sq[dc - 2])
                e_sq[dc] = p.emit("scalar",
                                  lambda e, dc=dc, rsl=rsl: e.activation(
                                      out=xsq2[:, (dc % 2) * 512:(dc % 2 + 1) * 512],
                                      in_=rsl(dc), func=AF.Square),
                                  sem="act")
                p.wait("tensor", "act", e_sq[dc])
                e_mm_sq[dc] = p.emit("tensor",
                                     lambda e, dc=dc: e.matmul(
                                         pl_sumsq[:], cb_onec[:],
                                         xsq2[:, (dc % 2) * 512:(dc % 2 + 1) * 512],
                                         start=(dc == 0), stop=(dc == DC - 1)),
                                     sem="pe")
            e_sumsq = p.cnt["pe"]
            p.wait("vector", "pe", e_sum)
            e_mf = p.emit("vector",
                          lambda e: e.tensor_scalar_mul(m2_row[:], pl_sum[:], 1.0 / D),
                          sem="dve")
            p.wait("vector", "dve", e_mf)
            p.emit("vector", lambda e: e.tensor_mul(msq2[:], m2_row[:], m2_row[:]))
            p.wait("vector", "pe", e_sumsq)
            e_var = p.emit("vector",
                           lambda e: e.scalar_tensor_tensor(var2[:], pl_sumsq[:], 1.0 / D,
                                                            msq2[:], ALU.mult, ALU.subtract),
                           sem="dve")
            p.wait("scalar", "dve", e_var)
            e_sd = p.emit("scalar",
                          lambda e: e.activation(out=sd2[:], in_=var2[:],
                                                 func=AF.Sqrt, bias=cf_eps[0:1, :]),
                          sem="act")
            p.wait("vector", "act", e_sd)
            e_rstd = p.emit("vector", lambda e: e.reciprocal(rstd2[:], sd2[:]), sem="dve")
            p.wait("tensor", "dve", e_mf)
            p.emit("tensor",
                   lambda e: e.matmul(pl_bc[:, 0:512], cf_oner[:], m2_row[:],
                                      start=True, stop=True))
            p.wait("tensor", "dve", e_rstd)
            e_b2 = p.emit("tensor",
                          lambda e: e.matmul(pl_bc[:, 512:1024], cf_oner[:], rstd2[:],
                                             start=True, stop=True),
                          sem="pe")
            p.wait("vector", "pe", e_b2)
            p.emit("vector",
                   lambda e, sc=sc: e.tensor_copy(out=m2bc[:, sc * 512:(sc + 1) * 512],
                                                  in_=pl_bc[:, 0:512]))
            p.emit("vector",
                   lambda e, sc=sc: e.tensor_copy(out=rstd2bc[:, sc * 512:(sc + 1) * 512],
                                                  in_=pl_bc[:, 512:1024]),
                   sem="dve")
        e_bc2 = p.cnt["dve"]
        p.wait("vector", "dve", e_bc2)
        for dc in range(DC):
            for sc in range(2):
                p.emit("vector",
                       lambda e, dc=dc, sc=sc: e.tensor_sub(
                           tmpf[:],
                           resid[:, dc * OWN + sc * 512: dc * OWN + (sc + 1) * 512],
                           m2bc[:, sc * 512:(sc + 1) * 512]))
                p.emit("vector",
                       lambda e, dc=dc, sc=sc: e.tensor_mul(
                           x2_sb[:, dc * OWN + sc * 512: dc * OWN + (sc + 1) * 512],
                           tmpf[:],
                           rstd2bc[:, sc * 512:(sc + 1) * 512]),
                       sem="dve")
        e_x2_done = p.cnt["dve"]
        p.wait("tensor", "dve", e_x2_done)
        for sc in range(2):
            tgt = (pl_sum, pl_sumsq)[sc]
            for dc in range(DC):
                p.emit("tensor",
                       lambda e, dc=dc, sc=sc, tgt=tgt: e.matmul(
                           tgt[:], cb_onec[:],
                           x2_sb[:, dc * OWN + sc * 512: dc * OWN + (sc + 1) * 512],
                           start=(dc == 0), stop=(dc == DC - 1)),
                       sem="pe" if dc == DC - 1 else None)
            e_cs = p.cnt["pe"]
            p.wait("vector", "pe", e_cs)
            p.emit("vector",
                   lambda e, sc=sc, tgt=tgt: e.tensor_copy(
                       out=csx2_row[0:1, sc * 512:(sc + 1) * 512], in_=tgt[:]),
                   sem="dve")
    e_csx2 = p.cnt["dve"]

    wg0 = fsb("wg0", [128, DC * 128], BF16)
    wg1 = fsb("wg1", [128, DC * 128], BF16)
    wu0 = fsb("wu0", [128, DC * 128], BF16)
    wu1 = fsb("wu1", [128, DC * 128], BF16)
    dw0 = fsb("dw0", [128, 16 * 512], BF16)
    dw1 = fsb("dw1", [128, 16 * 512], BF16)

    def emit_gu_dma(ic):
        wgb = (wg0, wg1)[ic % 2]
        wub = (wu0, wu1)[ic % 2]
        a = p.dma("gpsimd",
                  wgb[:].rearrange("p (dc j) -> p dc j", dc=DC),
                  wgT.ap().rearrange("(dc p) i -> p dc i", p=128)
                  [:, :, ic * 128:(ic + 1) * 128])
        b = p.dma("gpsimd",
                  wub[:].rearrange("p (dc j) -> p dc j", dc=DC),
                  wuT.ap().rearrange("(dc p) i -> p dc i", p=128)
                  [:, :, ic * 128:(ic + 1) * 128])
        return b

    def emit_dw_dma(gidx):
        q4, hg = gidx // 4, gidx % 4
        dwb = (dw0, dw1)[gidx % 2]
        return p.dma("gpsimd",
                     dwb[:].rearrange("p (ic j) -> p ic j", ic=16),
                     wdT.ap().rearrange("(ic p) h -> p ic h", p=128)
                     [:, q4 * 16:(q4 + 1) * 16, hg * 512:(hg + 1) * 512])

    p.wait("gpsimd", "pe", p.cnt["pe"])
    p.wait("gpsimd", "dve", p.cnt["dve"])
    p.wait("gpsimd", "act", p.cnt["act"])
    e_gu = {0: emit_gu_dma(0), 1: emit_gu_dma(1)}
    e_dw = {}

    with nc.psum_tensor("pf", [128, 4096], F32) as pf:
        e_gu_evac_act = {}
        e_dwmm = {}
        e_gu_silu = {}
        e_hmid_dve = {}
        e_FB_pe = 0
        e_FB_evac_dve = 0
        e_csh_copy = 0
        for q4 in range(4):
            for il in range(16):
                ic = q4 * 16 + il
                wgb = (wg0, wg1)[ic % 2]
                wub = (wu0, wu1)[ic % 2]
                p.wait("tensor", "gp", e_gu[ic])
                if il == 0:
                    p.wait("tensor", "dve", max(e_csx2, e_FB_evac_dve))
                for w, wb, reg in ((0, wgb, 0), (1, wub, 1)):
                    if ic >= 1:
                        p.wait("tensor", "act",
                               (e_gu_silu if w == 0 else e_gu_evac_act).get(ic - 1, 0))
                    for sc in range(2):
                        off = reg * 1024 + sc * 512
                        for dc in range(DC):
                            p.emit("tensor",
                                   lambda e, dc=dc, sc=sc, off=off, wb=wb: e.matmul(
                                       pf[:, off:off + 512],
                                       wb[:, dc * 128:(dc + 1) * 128],
                                       x2_sb[:, dc * OWN + sc * 512: dc * OWN + (sc + 1) * 512],
                                       start=(dc == 0), stop=False,
                                       skip_group_check=True))
                        p.emit("tensor",
                               lambda e, sc=sc, off=off: e.matmul(
                                   pf[:, off:off + 512], cb_neg128[:],
                                   csx2_row[0:1, sc * 512:(sc + 1) * 512],
                                   start=False, stop=True, skip_group_check=True),
                               sem="pe" if (w == 1 and sc == 1) else None)
                e_gumm = p.cnt["pe"]
                p.wait("scalar", "pe", e_gumm)
                if ic >= 1:
                    p.wait("scalar", "dve", e_hmid_dve.get(ic - 1, 0))
                e_gu_silu[ic] = p.emit("scalar",
                       lambda e, ic=ic: e.activation(out=gsil[:], in_=pf[:, 0:1024],
                                                     func=AF.Silu, scale=gsc[:, ic:ic + 1]),
                       sem="act")
                e_ua = p.emit("scalar",
                              lambda e, ic=ic: e.activation(out=usca[:], in_=pf[:, 1024:2048],
                                                            func=AF.Copy,
                                                            scale=usc[:, ic:ic + 1]),
                              sem="act")
                e_gu_evac_act[ic] = e_ua
                p.wait("vector", "act", e_ua)
                if q4 > 0 and il == 0:
                    p.wait("vector", "pe", e_FB_pe)
                e_hm = p.emit("vector",
                              lambda e, il=il: e.tensor_mul(
                                  hmid[:, il * OWN:(il + 1) * OWN], gsil[:], usca[:]),
                              sem="dve")
                e_hmid_dve[ic] = e_hm
                p.wait("tensor", "dve", e_hm)
                if il == 0:
                    p.wait("tensor", "dve", e_csh_copy)
                for sc in range(2):
                    p.emit("tensor",
                           lambda e, il=il, sc=sc: e.matmul(
                               pf[0:1, 2048 + sc * 512: 2048 + (sc + 1) * 512],
                               cb_onec[:],
                               hmid[:, il * OWN + sc * 512: il * OWN + (sc + 1) * 512],
                               start=(il == 0), stop=(il == 15),
                               skip_group_check=True),
                           sem="pe" if (il == 15 and sc == 1) else None)
                if ic + 2 < 64 and (ic + 2) // 16 == q4:
                    p.wait("gpsimd", "pe", e_gumm)
                    e_gu[ic + 2] = emit_gu_dma(ic + 2)
                elif ic + 2 < 64 and (ic + 2) // 16 != q4:
                    pass
            e_csh_mm = p.cnt["pe"]
            p.wait("vector", "pe", e_csh_mm)
            e_csh_copy = p.emit("vector",
                                lambda e: e.tensor_copy(out=csh_row[:], in_=pf[0:1, 2048:3072]),
                                sem="dve")

            if q4 == 0:
                e_dw[0] = emit_dw_dma(0)
                e_dw[1] = emit_dw_dma(1)
            for hg in range(4):
                gidx = q4 * 4 + hg
                if hg == 0:
                    p.wait("tensor", "dve", max(e_csh_copy, e_FB_evac_dve))
                else:
                    p.wait("tensor", "dve", e_FB_evac_dve)
                dwb = (dw0, dw1)[gidx % 2]
                p.wait("tensor", "gp", e_dw[gidx])
                for il in range(16):
                    for hcl in range(4):
                        for sc in range(2):
                            r = hcl * 2 + sc
                            last = (il == 15 and hcl == 3 and sc == 1)
                            p.emit("tensor",
                                   lambda e, il=il, hcl=hcl, sc=sc, r=r,
                                   dwb=dwb: e.matmul(
                                       pf[:, r * 512:(r + 1) * 512],
                                       dwb[:, il * 512 + hcl * 128:
                                           il * 512 + (hcl + 1) * 128],
                                       hmid[:, il * OWN + sc * 512:
                                            il * OWN + (sc + 1) * 512],
                                       start=(il == 0), stop=False,
                                       skip_group_check=True),
                                   sem="pe" if last else None)
                e_dwmm[gidx] = p.cnt["pe"]
                if gidx + 2 < 16:
                    p.wait("gpsimd", "pe", e_dwmm[gidx])
                    e_dw[gidx + 2] = emit_dw_dma(gidx + 2)
                for hcl in range(4):
                    for sc in range(2):
                        r = hcl * 2 + sc
                        p.emit("tensor",
                               lambda e, r=r, sc=sc: e.matmul(
                                   pf[:, r * 512:(r + 1) * 512], cb_neg128[:],
                                   csh_row[0:1, sc * 512:(sc + 1) * 512],
                                   start=False, stop=True, skip_group_check=True),
                               sem="pe" if r == 7 else None)
                e_dn = p.cnt["pe"]
                p.wait("vector", "pe", e_dn)
                for hcl in range(4):
                    hc = hg * 4 + hcl
                    for sc in range(2):
                        r = hcl * 2 + sc
                        p.emit("vector",
                               lambda e, hc=hc, sc=sc, r=r: e.scalar_tensor_tensor(
                                   resid[:, hc * OWN + sc * 512: hc * OWN + (sc + 1) * 512],
                                   pf[:, r * 512:(r + 1) * 512],
                                   dsc[:, hc:hc + 1],
                                   resid[:, hc * OWN + sc * 512: hc * OWN + (sc + 1) * 512],
                                   ALU.mult, ALU.add),
                               sem="dve" if r == 7 else None)
                e_FB_evac_dve = p.cnt["dve"]
            e_FB_pe = p.cnt["pe"]
            if q4 + 1 < 4:
                p.wait("gpsimd", "pe", e_FB_pe)
                e_gu[(q4 + 1) * 16] = emit_gu_dma((q4 + 1) * 16)
                e_gu[(q4 + 1) * 16 + 1] = emit_gu_dma((q4 + 1) * 16 + 1)

    e_F_dve = p.cnt["dve"]
    e_F_pe = p.cnt["pe"]

    ot0 = fsb("ot0", [128, D], F32)
    ot1 = ot0
    with nc.psum_tensor("pg0", [128, 128], F32) as pg0, \
         nc.psum_tensor("pg1", [128, 128], F32) as pg1:
        p.wait("tensor", "dve", e_F_dve)
        e_store = {}
        e_otcp = {}
        for st8 in range(8):
            otb = (ot0, ot1)[st8 % 2]
            pgs = (pg0, pg1)
            for dc in range(DC):
                pg = pgs[dc % 2]
                prev = (st8, dc - 2) if dc >= 2 else (st8 - 1, DC - 2 + dc)
                p.wait("tensor", "dve", e_otcp.get(prev, 0))
                e_tp = p.emit("tensor",
                              lambda e, dc=dc, st8=st8, pg=pg: e.transpose(
                                  pg[:],
                                  resid[:, dc * OWN + st8 * 128: dc * OWN + (st8 + 1) * 128],
                                  cf_id[:]),
                              sem="pe")
                p.wait("vector", "pe", e_tp)
                if st8 >= 1 and dc == 0:
                    p.wait("vector", "sy", e_store[st8 - 1])
                e_otcp[(st8, dc)] = p.emit("vector",
                                           lambda e, dc=dc, pg=pg, otb=otb: e.tensor_copy(
                                               out=otb[:, dc * 128:(dc + 1) * 128], in_=pg[:]),
                                           sem="dve")
            p.wait("sync", "dve", e_otcp[(st8, DC - 1)])
            e_store[st8] = p.dma("sync", out_own.ap()[st8 * 128:(st8 + 1) * 128, :], otb[:])
        p.wait("sync", "sy", p.cnt["sy"])

    with nc.Block() as block:
        @block.sync
        def _(e):
            for f in p.q["sync"]:
                f(e)

        @block.gpsimd
        def _(e):
            for f in p.q["gpsimd"]:
                f(e)

        @block.tensor
        def _(e):
            for f in p.q["tensor"]:
                f(e)

        @block.vector
        def _(e):
            for f in p.q["vector"]:
                f(e)

        @block.scalar
        def _(e):
            for f in p.q["scalar"]:
                f(e)

    fs.close()
    es.close()
    return nc


def _host_inputs(L0_attn_x, L0_attn_Wqkv_w, L0_attn_Wqkv_b, L0_attn_out_w,
                 L0_attn_out_b, L0_ffn_gate_w, L0_ffn_gate_scale, L0_ffn_up_w,
                 L0_ffn_up_scale, L0_ffn_down_w, L0_ffn_down_scale):
    C = np.ascontiguousarray
    xT = C(np.asarray(L0_attn_x, np.float32)[0].T)
    wqkvT = C(np.asarray(L0_attn_Wqkv_w, np.float32).T)
    qkv_b2d = C(np.asarray(L0_attn_Wqkv_b, np.float32).reshape(18, 128).T)
    owT = C(np.asarray(L0_attn_out_w, np.float32).T)
    ob2d = C(np.asarray(L0_attn_out_b, np.float32).reshape(16, 128).T)
    wgT = C(np.asarray(L0_ffn_gate_w, np.int32).T)
    wuT = C(np.asarray(L0_ffn_up_w, np.int32).T)
    wdT = C(np.asarray(L0_ffn_down_w, np.int32).T)
    gsc2d = C(np.asarray(L0_ffn_gate_scale, np.float32).reshape(64, 128).T)
    usc2d = C(np.asarray(L0_ffn_up_scale, np.float32).reshape(64, 128).T)
    dsc2d = C(np.asarray(L0_ffn_down_scale, np.float32).reshape(16, 128).T)

    kk = np.arange(128)[:, None]
    qq = np.arange(512)[None, :]
    tri = [(qq >= r * 128 + kk).astype(np.float32) for r in range(4)]
    maskd = C(np.concatenate(tri, axis=1))

    shared = dict(xT=xT, wqkvT=wqkvT, qkv_b2d=qkv_b2d, owT=owT, ob2d=ob2d,
                  wgT=wgT, wuT=wuT, wdT=wdT, gsc2d=gsc2d, usc2d=usc2d,
                  dsc2d=dsc2d, maskd=maskd)

    in_maps = []
    for c in range(8):
        ca, cb = c, 15 - c
        x_own = C(np.concatenate([xT[:, ca * 512:(ca + 1) * 512],
                                  xT[:, cb * 512:(cb + 1) * 512]], axis=1))
        ktbl = np.asarray([[512 * ca, 512 * ca + 256,
                            512 * cb, 512 * cb + 256]], np.int32)
        bk = np.full((128, 48), -1e30, np.float32)
        bk[:, 0] = bk[:, 1] = bk[:, 46] = bk[:, 47] = -12.0
        for s in range(2, 16):
            if s - 2 < 2 * ca:
                bk[:, s] = -12.0
        for s in range(16, 46):
            if s - 16 < 2 * cb:
                bk[:, s] = -12.0
        m = dict(shared)
        m["x_own"] = x_own
        m["ktbl"] = C(ktbl)
        m["bkill"] = C(bk)
        in_maps.append(m)
    return in_maps


def kernel(**inputs):
    in_maps = _host_inputs(**inputs)
    nc = build_program()
    res = run_bass_kernel_spmd(nc, in_maps, core_ids=list(range(8)))
    out = np.empty((S, D), np.float32)
    for c in range(8):
        shard = np.asarray(res.results[c]["out_own"])
        out[c * 512:(c + 1) * 512] = shard[:512]
        out[(15 - c) * 512:(16 - c) * 512] = shard[512:]
    return out.reshape(1, S, D)


# version 06
# speedup vs baseline: 1.0605x; 1.0403x over previous
import numpy as np
import ml_dtypes
from contextlib import ExitStack

import concourse.bass as bass
import concourse.mybir as mybir
from concourse.bass_utils import run_bass_kernel_spmd

F32 = mybir.dt.float32
BF16 = mybir.dt.bfloat16
I32 = mybir.dt.int32
AF = mybir.ActivationFunctionType
ALU = mybir.AluOpType
ET = mybir.EngineType
ds = bass.ds
BF = ml_dtypes.bfloat16

D = 2048
S = 8192
DC = 16
H = 16
OWN = 1024
NSC = 16
I_FFN = 8192
EPS = 1e-5
EXP_SCALE = float(1.0 / np.sqrt(128.0))
EXP_BIAS = -12.0

ENGS = ("sync", "gpsimd", "tensor", "vector", "scalar")


class Prog:
    def __init__(self, nc, es):
        self.nc = nc
        self.es = es
        self.q = {e: [] for e in ENGS}
        self.cnt = {}
        self.sems = {}
        for s in ("sy", "gp", "pe", "act", "dve"):
            self.sems[s] = es.enter_context(nc.semaphore(s))
            self.cnt[s] = 0

    def on(self, eng, fn):
        self.q[eng].append(fn)

    def wait(self, eng, name, val):
        if val is None or val <= 0:
            return
        self.q[eng].append(lambda e, s=self.sems[name], v=val: e.wait_ge(s, v))

    def emit(self, eng, fn, sem=None, by=1):
        if sem is None:
            self.q[eng].append(fn)
            return None
        self.cnt[sem] += by
        n = self.cnt[sem]
        self.q[eng].append(lambda e, f=fn, s=self.sems[sem], b=by: f(e).then_inc(s, b))
        return n

    def dma(self, eng, out_ap, in_ap):
        sem = "sy" if eng == "sync" else "gp"
        return self.emit(eng, lambda e, o=out_ap, i=in_ap: e.dma_start(out=o, in_=i),
                         sem=sem, by=16)

    def barrier(self, *engs):
        snap = dict(self.cnt)
        for eng in engs:
            for s, v in snap.items():
                self.wait(eng, s, v)


def build_program():
    nc = bass.Bass("TRN2", target_bir_lowering=False)

    xT = nc.dram_tensor("xT", [D, S], F32, kind="ExternalInput")
    x_own = nc.dram_tensor("x_own", [D, OWN], F32, kind="ExternalInput")
    wqkvT = nc.dram_tensor("wqkvT", [D, 2304], F32, kind="ExternalInput")
    qkv_b2d = nc.dram_tensor("qkv_b2d", [128, 18], F32, kind="ExternalInput")
    owT = nc.dram_tensor("owT", [D, D], F32, kind="ExternalInput")
    ob2d = nc.dram_tensor("ob2d", [128, 16], F32, kind="ExternalInput")
    wgT = nc.dram_tensor("wgT", [D, I_FFN], I32, kind="ExternalInput")
    wuT = nc.dram_tensor("wuT", [D, I_FFN], I32, kind="ExternalInput")
    wdT = nc.dram_tensor("wdT", [I_FFN, D], I32, kind="ExternalInput")
    gsc2d = nc.dram_tensor("gsc2d", [128, 64], F32, kind="ExternalInput")
    usc2d = nc.dram_tensor("usc2d", [128, 64], F32, kind="ExternalInput")
    dsc2d = nc.dram_tensor("dsc2d", [128, 16], F32, kind="ExternalInput")
    ktbl = nc.dram_tensor("ktbl", [1, 4], I32, kind="ExternalInput")
    bkill = nc.dram_tensor("bkill", [128, 48], F32, kind="ExternalInput")
    maskd = nc.dram_tensor("maskd", [128, 2048], F32, kind="ExternalInput")
    out_own = nc.dram_tensor("out_own", [OWN, D], F32, kind="ExternalOutput")

    id_b = nc.inline_tensor(np.eye(128, dtype=BF), "id_b")
    id_f = nc.inline_tensor(np.eye(128, dtype=np.float32), "id_f")
    onesc_b = nc.inline_tensor(np.ones((128, 1), dtype=BF), "onesc_b")
    onesc_f = nc.inline_tensor(np.ones((128, 1), dtype=np.float32), "onesc_f")
    onesr_f = nc.inline_tensor(np.ones((1, 128), dtype=np.float32), "onesr_f")
    neg128_r = nc.inline_tensor(np.full((1, 128), -128.0, dtype=BF), "neg128_r")
    zeros_r = nc.inline_tensor(np.zeros((1, 512), dtype=BF), "zeros_r")
    eps_c = nc.inline_tensor(np.full((128, 1), EPS, dtype=np.float32), "eps_c")
    expb_c = nc.inline_tensor(np.full((128, 1), EXP_BIAS, dtype=np.float32), "expb_c")

    es = ExitStack()
    p = Prog(nc, es)
    sb = lambda n, sh, dt: es.enter_context(nc.sbuf_tensor(n, sh, dt))

    resid = sb("resid", [128, DC * OWN], F32)
    cb_id = sb("cb_id", [128, 128], BF16)
    cb_onec = sb("cb_onec", [128, 1], BF16)
    cb_neg128 = sb("cb_neg128", [1, 128], BF16)
    cb_zrow = sb("cb_zrow", [1, 512], BF16)
    cf_id = sb("cf_id", [128, 128], F32)
    cf_onec = sb("cf_onec", [128, 1], F32)
    cf_oner = sb("cf_oner", [1, 128], F32)
    maskb = sb("maskb", [128, 2048], BF16)
    ktb = sb("ktb", [1, 4], I32)
    bkb = sb("bkb", [128, 48], F32)
    bqkv = sb("bqkv", [128, 18], F32)
    bo = sb("bo", [128, 16], F32)
    gsc = sb("gsc", [128, 64], F32)
    usc = sb("usc", [128, 64], F32)
    dsc = sb("dsc", [128, 16], F32)
    cf_eps = sb("cf_eps", [128, 1], F32)
    cf_expb = sb("cf_expb", [128, 1], F32)
    ow0 = sb("ow0", [128, DC * 128], BF16)
    ow1 = sb("ow1", [128, DC * 128], BF16)

    for dst, src in ((cb_id, id_b), (cb_onec, onesc_b), (cb_neg128, neg128_r),
                     (cb_zrow, zeros_r), (cf_id, id_f), (cf_onec, onesc_f),
                     (cf_oner, onesr_f), (cf_eps, eps_c), (cf_expb, expb_c)):
        p.dma("sync", dst[:], src.ap())
    p.dma("gpsimd", maskb[:], maskd.ap())
    for dst, src in ((ktb, ktbl), (bkb, bkill), (bqkv, qkv_b2d), (bo, ob2d),
                     (gsc, gsc2d), (usc, usc2d), (dsc, dsc2d)):
        p.dma("sync", dst[:], src.ap())
    e_resid_init = p.dma("sync", resid[:].rearrange("p (dc s) -> p dc s", dc=DC),
                         x_own.ap().rearrange("(dc p) s -> p dc s", p=128))
    e_consts_sy = p.cnt["sy"]
    e_consts_gp = p.cnt["gp"]
    for _eng in ("tensor", "vector", "scalar"):
        p.wait(_eng, "sy", e_consts_sy)

    ac = ExitStack()
    asb = lambda n, sh, dt: ac.enter_context(nc.sbuf_tensor(n, sh, dt))
    k_fm = asb("k_fm", [128, S], BF16)
    v_tm = asb("v_tm", [128, S], BF16)

    pa = ExitStack()
    pa_t = lambda n, sh, dt: pa.enter_context(nc.sbuf_tensor(n, sh, dt))
    pa_p = lambda n, sh, dt: pa.enter_context(nc.psum_tensor(n, sh, dt))
    m_row_b = pa_t("m_row_b", [1, 512], BF16)
    m_row_f = pa_t("m_row_f", [1, 512], F32)
    msq_row = pa_t("msq_row", [1, 512], F32)
    var_row = pa_t("var_row", [1, 512], F32)
    sd_row = pa_t("sd_row", [1, 512], F32)
    rstd_row = pa_t("rstd_row", [1, 512], F32)
    xstage = pa_t("xstage", [128, 2 * DC * 512], BF16)
    wkv = pa_t("wkv", [128, DC * 256], BF16)
    wsn_kv = pa_t("wsn_kv", [1, 256], BF16)
    rstdb_sb = pa_t("rstdb_sb", [128, 512], F32)
    kvtmp = pa_t("kvtmp", [128, 512], F32)
    vchunk = pa_t("vchunk", [128, 512], BF16)
    xsq = pa_t("xsq", [128, 2 * 512], BF16)
    pa_stats = pa_p("pa_stats", [128, 512], F32)
    pa_kv = pa_p("pa_kv", [128, 1024], F32)
    pa_bc = pa_p("pa_bc", [128, 512], F32)
    pa_vt0 = pa_p("pa_vt0", [128, 128], BF16)
    pa_vt1 = pa_p("pa_vt1", [128, 128], BF16)
    pa_ws = pa_p("pa_ws", [1, 256], F32)
    if True:
        e_wkv = p.dma("gpsimd",
                      wkv[:].rearrange("p (dc e) -> p dc e", dc=DC),
                      wqkvT.ap().rearrange("(dc p) e -> p dc e", p=128)[:, :, 2048:2304])

        def emit_xs_dma(sc):
            buf = sc % 2
            return p.dma("gpsimd",
                         xstage[:, buf * DC * 512:(buf + 1) * DC * 512]
                         .rearrange("p (dc s) -> p dc s", dc=DC),
                         xT.ap().rearrange("(dc p) s -> p dc s", p=128)
                         [:, :, sc * 512:(sc + 1) * 512])

        e_xs = [emit_xs_dma(0), emit_xs_dma(1)]

        p.wait("tensor", "gp", e_wkv)
        for dc in range(DC):
            p.emit("tensor",
                   lambda e, dc=dc: e.matmul(pa_ws[:], cb_onec[:],
                                             wkv[:, dc * 256:(dc + 1) * 256],
                                             start=(dc == 0), stop=(dc == DC - 1)),
                   sem="pe" if dc == DC - 1 else None)
        e_ws = p.cnt["pe"]
        p.wait("vector", "pe", e_ws)
        e_wsn = p.emit("vector",
                       lambda e: e.tensor_scalar_mul(wsn_kv[:], pa_ws[:], -1.0), sem="dve")

        e_prev_var = 0
        e_prev_vm = 0
        e_prev_bcc = 0
        e_prev_bc = 0
        e_prev_vt_pe = 0
        e_mm_sq_prev = {}
        for sc in range(NSC):
            buf = sc % 2
            xsl = lambda dc, buf=buf: xstage[:, buf * DC * 512 + dc * 512:
                                             buf * DC * 512 + (dc + 1) * 512]
            p.wait("tensor", "gp", e_xs[sc])
            p.wait("tensor", "dve", e_prev_var)
            p.wait("scalar", "gp", e_xs[sc])
            e_mm_sq = {}
            e_sq = {}
            for dc in range(DC):
                if dc >= 2:
                    p.wait("scalar", "pe", e_mm_sq[dc - 2])
                elif sc > 0:
                    p.wait("scalar", "pe", e_mm_sq_prev[DC - 2 + dc])
                e_sq[dc] = p.emit("scalar",
                                  lambda e, dc=dc, xsl=xsl: e.activation(
                                      out=xsq[:, (dc % 2) * 512:(dc % 2 + 1) * 512],
                                      in_=xsl(dc), func=AF.Square),
                                  sem="act")
                p.emit("tensor",
                       lambda e, dc=dc, xsl=xsl: e.matmul(pa_stats[0:1, :], cb_onec[:],
                                                          xsl(dc), start=(dc == 0),
                                                          stop=(dc == DC - 1),
                                                          skip_group_check=True))
                p.wait("tensor", "act", e_sq[dc])
                e_mm_sq[dc] = p.emit("tensor",
                                     lambda e, dc=dc: e.matmul(
                                         pa_stats[32:33, :], cb_onec[:],
                                         xsq[:, (dc % 2) * 512:(dc % 2 + 1) * 512],
                                         start=(dc == 0), stop=(dc == DC - 1),
                                         skip_group_check=True, tile_position=(0, 32)),
                                     sem="pe")
            e_stats = p.cnt["pe"]
            e_mm_sq_prev = e_mm_sq
            p.wait("tensor", "dve", e_prev_vm)
            for dc in range(DC):
                for jc in range(2):
                    p.emit("tensor",
                           lambda e, dc=dc, jc=jc, xsl=xsl: e.matmul(
                               pa_kv[:, jc * 512:(jc + 1) * 512],
                               wkv[:, dc * 256 + jc * 128: dc * 256 + (jc + 1) * 128],
                               xsl(dc), start=(dc == 0), stop=False,
                               skip_group_check=True),
                           sem="pe" if (dc == DC - 1 and jc == 1) else None)
            e_kv = p.cnt["pe"]
            p.wait("vector", "pe", max(e_stats, e_prev_bc))
            p.emit("vector", lambda e: e.tensor_scalar_mul(m_row_b[:], pa_stats[0:1, :], 1.0 / D))
            e_mb = p.emit("vector",
                          lambda e: e.tensor_scalar_mul(m_row_f[:], pa_stats[0:1, :], 1.0 / D),
                          sem="dve")
            p.wait("vector", "dve", e_mb)
            p.emit("vector", lambda e: e.tensor_mul(msq_row[:], m_row_f[:], m_row_f[:]))
            e_var = p.emit("vector",
                           lambda e: e.scalar_tensor_tensor(var_row[:], pa_stats[32:33, :],
                                                            1.0 / D, msq_row[:],
                                                            ALU.mult, ALU.subtract),
                           sem="dve")
            e_prev_var = e_var
            p.wait("scalar", "dve", e_var)
            e_sd = p.emit("scalar",
                          lambda e: e.activation(out=sd_row[:], in_=var_row[:],
                                                 func=AF.Sqrt, bias=cf_eps[0:1, :]),
                          sem="act")
            p.wait("vector", "act", e_sd)
            e_rstd = p.emit("vector", lambda e: e.reciprocal(rstd_row[:], sd_row[:]), sem="dve")
            p.wait("tensor", "dve", max(e_mb, e_wsn))
            for jc in range(2):
                p.emit("tensor",
                       lambda e, jc=jc: e.matmul(pa_kv[:, jc * 512:(jc + 1) * 512],
                                                 wsn_kv[0:1, jc * 128:(jc + 1) * 128],
                                                 m_row_b[:], start=False, stop=True,
                                                 skip_group_check=True))
            p.wait("tensor", "dve", max(e_rstd, e_prev_bcc))
            e_bc = p.emit("tensor",
                          lambda e: e.matmul(pa_bc[:], cf_oner[:], rstd_row[:],
                                             start=True, stop=True),
                          sem="pe")
            e_prev_bc = e_bc
            p.wait("vector", "pe", e_bc)
            e_bcc = p.emit("vector", lambda e: e.tensor_copy(out=rstdb_sb[:], in_=pa_bc[:]),
                           sem="dve")
            e_prev_bcc = e_bcc
            p.wait("vector", "dve", e_bcc)
            e_km = p.emit("vector",
                          lambda e: e.tensor_mul(kvtmp[:], pa_kv[:, 0:512], rstdb_sb[:]),
                          sem="dve")
            p.emit("vector",
                   lambda e, sc=sc: e.tensor_scalar_add(
                       k_fm[:, sc * 512:(sc + 1) * 512], kvtmp[:], bqkv[:, 16:17]))
            e_vm = p.emit("vector",
                          lambda e: e.tensor_mul(kvtmp[:], pa_kv[:, 512:1024], rstdb_sb[:]),
                          sem="dve")
            e_prev_vm = e_vm
            p.wait("vector", "pe", e_prev_vt_pe)
            e_v = p.emit("vector",
                         lambda e: e.tensor_scalar_add(vchunk[:], kvtmp[:], bqkv[:, 17:18]),
                         sem="dve")
            p.wait("tensor", "dve", e_v)
            e_vc = {}
            for j in range(4):
                pvt = (pa_vt0, pa_vt1)[j % 2]
                p.wait("tensor", "dve", e_vc.get(j - 2, 0))
                e_t = p.emit("tensor",
                             lambda e, j=j, pvt=pvt: e.transpose(
                                 pvt[:], vchunk[:, j * 128:(j + 1) * 128], cb_id[:]),
                             sem="pe")
                p.wait("vector", "pe", e_t)
                e_vc[j] = p.emit("vector",
                                 lambda e, sc=sc, j=j, pvt=pvt: e.tensor_copy(
                                     out=v_tm[:, sc * 512 + j * 128: sc * 512 + (j + 1) * 128],
                                     in_=pvt[:]),
                                 sem="dve")
            e_prev_vt_pe = p.cnt["pe"]
            if sc + 2 < NSC:
                p.wait("gpsimd", "pe", e_kv)
                p.wait("gpsimd", "act", e_sq[DC - 1])
                e_xs.append(emit_xs_dma(sc + 2))

        p.barrier("tensor", "vector", "scalar")
    pa.close()

    ctxb = asb("ctxb", [128, H * OWN], BF16)
    x1own = asb("x1own", [128, DC * OWN], BF16)
    bsub = ExitStack()
    m_own_bc = bsub.enter_context(nc.sbuf_tensor("m_own_bc", [128, OWN], F32))
    rstd_own_bc = bsub.enter_context(nc.sbuf_tensor("rstd_own_bc", [128, OWN], F32))
    xsq = bsub.enter_context(nc.sbuf_tensor("xsq_b", [128, 2 * 512], BF16))
    kvtmp = bsub.enter_context(nc.sbuf_tensor("kvtmp_b", [128, 512], F32))
    m_row_f = bsub.enter_context(nc.sbuf_tensor("m_row_fb", [1, 512], F32))
    msq_row = bsub.enter_context(nc.sbuf_tensor("msq_rowb", [1, 512], F32))
    var_row = bsub.enter_context(nc.sbuf_tensor("var_rowb", [1, 512], F32))
    sd_row = bsub.enter_context(nc.sbuf_tensor("sd_rowb", [1, 512], F32))
    rstd_row = bsub.enter_context(nc.sbuf_tensor("rstd_rowb", [1, 512], F32))

    p.wait("gpsimd", "pe", p.cnt["pe"])
    p.wait("gpsimd", "act", p.cnt["act"])
    p.wait("gpsimd", "dve", p.cnt["dve"])
    e_xob = p.dma("gpsimd",
                  x1own[:].rearrange("p (dc s) -> p dc s", dc=DC),
                  x_own.ap().rearrange("(dc p) s -> p dc s", p=128))

    with nc.psum_tensor("pb_sum", [1, 512], F32) as pb_sum, \
         nc.psum_tensor("pb_sumsq", [1, 512], F32) as pb_sumsq, \
         nc.psum_tensor("pb_bc", [128, 1024], F32) as pb_bc:
        p.wait("tensor", "gp", e_xob)
        p.wait("scalar", "gp", e_xob)
        for sc in range(2):
            xol = lambda dc, sc=sc: x1own[:, dc * OWN + sc * 512: dc * OWN + (sc + 1) * 512]
            if sc == 1:
                p.wait("tensor", "dve", p.cnt["dve"])
            for dc in range(DC):
                p.emit("tensor",
                       lambda e, dc=dc, xol=xol: e.matmul(pb_sum[:], cb_onec[:], xol(dc),
                                                          start=(dc == 0), stop=(dc == DC - 1)),
                       sem="pe" if dc == DC - 1 else None)
            e_sum = p.cnt["pe"]
            e_mm_sq = {}
            e_sq = {}
            for dc in range(DC):
                if dc >= 2:
                    p.wait("scalar", "pe", e_mm_sq[dc - 2])
                e_sq[dc] = p.emit("scalar",
                                  lambda e, dc=dc, xol=xol: e.activation(
                                      out=xsq[:, (dc % 2) * 512:(dc % 2 + 1) * 512],
                                      in_=xol(dc), func=AF.Square),
                                  sem="act")
                p.wait("tensor", "act", e_sq[dc])
                e_mm_sq[dc] = p.emit("tensor",
                                     lambda e, dc=dc: e.matmul(
                                         pb_sumsq[:], cb_onec[:],
                                         xsq[:, (dc % 2) * 512:(dc % 2 + 1) * 512],
                                         start=(dc == 0), stop=(dc == DC - 1)),
                                     sem="pe")
            e_sumsq = p.cnt["pe"]
            p.wait("vector", "pe", e_sum)
            e_mf = p.emit("vector",
                          lambda e: e.tensor_scalar_mul(m_row_f[:], pb_sum[:], 1.0 / D),
                          sem="dve")
            p.wait("vector", "dve", e_mf)
            p.emit("vector", lambda e: e.tensor_mul(msq_row[:], m_row_f[:], m_row_f[:]))
            p.wait("vector", "pe", e_sumsq)
            e_var = p.emit("vector",
                           lambda e: e.scalar_tensor_tensor(var_row[:], pb_sumsq[:], 1.0 / D,
                                                            msq_row[:], ALU.mult, ALU.subtract),
                           sem="dve")
            p.wait("scalar", "dve", e_var)
            e_sd = p.emit("scalar",
                          lambda e: e.activation(out=sd_row[:], in_=var_row[:],
                                                 func=AF.Sqrt, bias=cf_eps[0:1, :]),
                          sem="act")
            p.wait("vector", "act", e_sd)
            e_rstd = p.emit("vector", lambda e: e.reciprocal(rstd_row[:], sd_row[:]), sem="dve")
            p.wait("tensor", "dve", e_mf)
            e_b1 = p.emit("tensor",
                          lambda e: e.matmul(pb_bc[:, 0:512], cf_oner[:], m_row_f[:],
                                             start=True, stop=True),
                          sem="pe")
            p.wait("tensor", "dve", e_rstd)
            e_b2 = p.emit("tensor",
                          lambda e: e.matmul(pb_bc[:, 512:1024], cf_oner[:], rstd_row[:],
                                             start=True, stop=True),
                          sem="pe")
            p.wait("vector", "pe", e_b2)
            p.emit("vector",
                   lambda e, sc=sc: e.tensor_copy(out=m_own_bc[:, sc * 512:(sc + 1) * 512],
                                                  in_=pb_bc[:, 0:512]))
            p.emit("vector",
                   lambda e, sc=sc: e.tensor_copy(out=rstd_own_bc[:, sc * 512:(sc + 1) * 512],
                                                  in_=pb_bc[:, 512:1024]),
                   sem="dve")
        e_bcast_done = p.cnt["dve"]
        p.wait("vector", "dve", e_bcast_done)
        for dc in range(DC):
            for sc in range(2):
                p.emit("vector",
                       lambda e, dc=dc, sc=sc: e.tensor_sub(
                           kvtmp[:],
                           x1own[:, dc * OWN + sc * 512: dc * OWN + (sc + 1) * 512],
                           m_own_bc[:, sc * 512:(sc + 1) * 512]))
                p.emit("vector",
                       lambda e, dc=dc, sc=sc: e.tensor_mul(
                           x1own[:, dc * OWN + sc * 512: dc * OWN + (sc + 1) * 512],
                           kvtmp[:],
                           rstd_own_bc[:, sc * 512:(sc + 1) * 512]),
                       sem="dve")
    e_x1_done = p.cnt["dve"]
    bsub.close()

    q_sb = asb("q_sb", [128, 2 * OWN], BF16)
    p0_sb = asb("p0_sb", [128, 2 * 1024], BF16)
    wq0 = asb("wq0", [128, DC * 128], BF16)
    wq1 = asb("wq1", [128, DC * 128], BF16)
    kst = asb("kst", [128, 4 * 256], BF16)
    vst = asb("vst", [128, 4 * 256], BF16)
    recA = asb("recA", [1, 512], F32)
    recB = asb("recB", [1, 512], F32)
    recbcA = asb("recbcA", [128, 512], F32)
    recbcB = asb("recbcB", [128, 512], F32)
    d32row = asb("d32row", [1, 512], F32)
    densum = asb("densum", [1, 512], F32)

    cps = ExitStack()
    S0 = cps.enter_context(nc.psum_tensor("pc_s0", [128, 1024], F32))
    S1 = cps.enter_context(nc.psum_tensor("pc_s1", [128, 1024], F32))
    ctx_ps = cps.enter_context(nc.psum_tensor("pc_ctx", [128, 1024], F32))
    den_ps = cps.enter_context(nc.psum_tensor("pc_den", [128, 1024], F32))
    SBUFS = (S0, S1)
    NG = 48

    def emit_wq_dma(h):
        wqb = (wq0, wq1)[h % 2]
        return p.dma("gpsimd",
                     wqb[:].rearrange("p (dc j) -> p dc j", dc=DC),
                     wqkvT.ap().rearrange("(dc p) e -> p dc e", p=128)
                     [:, :, h * 128:(h + 1) * 128])

    p.wait("gpsimd", "pe", p.cnt["pe"])
    p.wait("gpsimd", "act", p.cnt["act"])
    p.wait("gpsimd", "dve", p.cnt["dve"])
    e_wq = [emit_wq_dma(0), emit_wq_dma(1)]

    p.barrier("tensor", "vector", "scalar")
    p.wait("vector", "gp", e_consts_gp)
    p.wait("vector", "sy", e_consts_sy)
    p.wait("tensor", "sy", e_consts_sy)
    p.wait("scalar", "sy", e_consts_sy)

    _dve_et = nc.vector.engine
    r_kt = [nc.alloc_register(_dve_et, f"r_kt{j}") for j in range(4)]
    for j in range(4):
        p.emit("vector", lambda e, j=j: e.reg_load(r_kt[j], ktb[0:1, j:j + 1]))
        kv_ = bass.RuntimeValue(val=r_kt[j], min_val=0, max_val=7936)
        p.emit("vector",
               lambda e, j=j, kv_=kv_: e.tensor_copy(
                   out=kst[:, j * 256:(j + 1) * 256], in_=k_fm[:, ds(kv_, 256)]))
        p.emit("vector",
               lambda e, j=j, kv_=kv_: e.tensor_copy(
                   out=vst[:, j * 256:(j + 1) * 256], in_=v_tm[:, ds(kv_, 256)]),
               sem="dve")
    e_stage = p.cnt["dve"]
    p.wait("tensor", "dve", e_stage)

    def slot_params(s):
        if s == 0:
            return (lambda i: kst[:, 0 * 256 + i * 128: 0 * 256 + (i + 1) * 128],
                    lambda i: vst[:, 0 * 256 + i * 128: 0 * 256 + (i + 1) * 128], 0, 0)
        if s == 1:
            return (lambda i: kst[:, 1 * 256 + i * 128: 1 * 256 + (i + 1) * 128],
                    lambda i: vst[:, 1 * 256 + i * 128: 1 * 256 + (i + 1) * 128], 0, 1024)
        if s == 46:
            return (lambda i: kst[:, 2 * 256 + i * 128: 2 * 256 + (i + 1) * 128],
                    lambda i: vst[:, 2 * 256 + i * 128: 2 * 256 + (i + 1) * 128], 512, 0)
        if s == 47:
            return (lambda i: kst[:, 3 * 256 + i * 128: 3 * 256 + (i + 1) * 128],
                    lambda i: vst[:, 3 * 256 + i * 128: 3 * 256 + (i + 1) * 128], 512, 1024)
        if s < 16:
            kb = 256 * (s - 2)
            co = 0
        else:
            kb = 256 * (s - 16)
            co = 512
        return (lambda i, kb=kb: k_fm[:, kb + i * 128: kb + (i + 1) * 128],
                lambda i, kb=kb: v_tm[:, kb + i * 128: kb + (i + 1) * 128], co, None)

    slot = 0
    e_cons = {}
    e_p0rdy = {}
    last_p0_pe = [0, 0]
    pend_grp = [None]
    e_evac_dve = 0
    e_qk_end_byhead = {}

    def emit_pv(gi):
        s = gi % NG
        _, v_src, co, _ = slot_params(s)
        p.wait("tensor", *e_p0rdy[gi])
        for i in range(2):
            p.emit("tensor",
                   lambda e, gi=gi, i=i, co=co, v_src=v_src: e.matmul(
                       ctx_ps[:, co:co + 512],
                       v_src(i),
                       p0_sb[:, (gi % 2) * 1024 + i * 512:(gi % 2) * 1024 + (i + 1) * 512],
                       start=False, stop=False, skip_group_check=True))
        for i in range(2):
            p.emit("tensor",
                   lambda e, gi=gi, i=i, co=co: e.matmul(
                       den_ps[i * 32:i * 32 + 1, co:co + 512],
                       cb_onec[:],
                       p0_sb[:, (gi % 2) * 1024 + i * 512:(gi % 2) * 1024 + (i + 1) * 512],
                       start=False, stop=False, skip_group_check=True,
                       tile_position=(0, 32 * i)),
                   sem="pe" if i == 1 else None)
        last_p0_pe[gi % 2] = p.cnt["pe"]

    for h in range(H):
        wqb = (wq0, wq1)[h % 2]
        t_qp = slot
        slot += 1
        qps = SBUFS[t_qp % 2]
        if t_qp - 2 in e_cons:
            p.wait("tensor", *e_cons[t_qp - 2])
        p.wait("tensor", "gp", e_wq[h])
        if h == 0:
            p.wait("tensor", "dve", e_x1_done)
        for sc in range(2):
            for dc in range(DC):
                p.emit("tensor",
                       lambda e, dc=dc, sc=sc, qps=qps, wqb=wqb: e.matmul(
                           qps[:, sc * 512:(sc + 1) * 512],
                           wqb[:, dc * 128:(dc + 1) * 128],
                           x1own[:, dc * OWN + sc * 512: dc * OWN + (sc + 1) * 512],
                           start=(dc == 0), stop=(dc == DC - 1),
                           skip_group_check=True),
                       sem="pe" if (dc == DC - 1 and sc == 1) else None)
        e_qp = p.cnt["pe"]
        p.wait("vector", "pe", e_qp)
        if h >= 2:
            p.wait("vector", "pe", e_qk_end_byhead[h - 2])
        e_qe = p.emit("vector",
                      lambda e, h=h, qps=qps: e.tensor_scalar_add(
                          q_sb[:, (h % 2) * OWN:(h % 2 + 1) * OWN],
                          qps[:], bqkv[:, h:h + 1]),
                      sem="dve")
        e_cons[t_qp] = ("dve", e_qe)
        if h + 2 < H:
            p.wait("gpsimd", "pe", e_qp)
            e_wq.append(emit_wq_dma(h + 2))

        for g in range(NG):
            gi = h * NG + g
            t = slot
            slot += 1
            Sb = SBUFS[t % 2]
            k_src, v_src, co, moff = slot_params(g)
            if t - 2 in e_cons:
                p.wait("tensor", *e_cons[t - 2])
            if g == 0:
                p.wait("tensor", "dve", e_qe)

            if g == 1:
                p.wait("tensor", "dve", e_evac_dve)
                p.emit("tensor", lambda e: e.matmul(ctx_ps[:, 0:512], cb_zrow[0:1, 0:128],
                                                    cb_zrow[:], start=True, stop=True,
                                                    skip_group_check=True))
                p.emit("tensor", lambda e: e.matmul(ctx_ps[:, 512:1024], cb_zrow[0:1, 0:128],
                                                    cb_zrow[:], start=True, stop=True,
                                                    skip_group_check=True))
                for _co in (0, 512):
                    p.emit("tensor", lambda e, _co=_co: e.matmul(
                        den_ps[0:1, _co:_co + 512], cb_zrow[0:1, 0:1],
                        cb_zrow[:], start=True, stop=True, skip_group_check=True))
                    p.emit("tensor", lambda e, _co=_co: e.matmul(
                        den_ps[32:33, _co:_co + 512], cb_zrow[0:1, 0:1],
                        cb_zrow[:], start=True, stop=True, skip_group_check=True,
                        tile_position=(0, 32)))

            for i in range(2):
                p.emit("tensor",
                       lambda e, Sb=Sb, k_src=k_src, co=co, h=h, i=i: e.matmul(
                           Sb[:, i * 512:(i + 1) * 512],
                           k_src(i),
                           q_sb[:, (h % 2) * OWN + co:(h % 2) * OWN + co + 512],
                           start=True, stop=True, skip_group_check=True),
                       sem="pe" if i == 1 else None)
            e_qk = p.cnt["pe"]

            if pend_grp[0] is not None:
                emit_pv(pend_grp[0])
            pend_grp[0] = gi

            p.wait("scalar", "pe", e_qk)
            p.wait("scalar", "pe", last_p0_pe[gi % 2])
            e_exp = p.emit("scalar",
                           lambda e, Sb=Sb, gi=gi, g=g: e.activation(
                               out=p0_sb[:, (gi % 2) * 1024:(gi % 2 + 1) * 1024],
                               in_=Sb[:], func=AF.Exp, scale=EXP_SCALE,
                               bias=bkb[:, g:g + 1]),
                           sem="act")
            e_cons[t] = ("act", e_exp)

            if moff is None:
                e_p0rdy[gi] = ("act", e_exp)
            else:
                p.wait("vector", "act", e_exp)
                e_m = p.emit("vector",
                             lambda e, gi=gi, moff=moff: e.tensor_mul(
                                 p0_sb[:, (gi % 2) * 1024:(gi % 2 + 1) * 1024],
                                 p0_sb[:, (gi % 2) * 1024:(gi % 2 + 1) * 1024],
                                 maskb[:, moff:moff + 1024]),
                             sem="dve")
                e_p0rdy[gi] = ("dve", e_m)

        e_qk_end_byhead[h] = p.cnt["pe"]

        emit_pv(pend_grp[0])
        pend_grp[0] = None
        e_last_pv = p.cnt["pe"]

        p.wait("vector", "pe", e_last_pv)
        p.emit("vector", lambda e: e.tensor_copy(out=d32row[:], in_=den_ps[32:33, 0:512]))
        p.emit("vector", lambda e: e.tensor_add(densum[:], den_ps[0:1, 0:512], d32row[:]))
        e_rA = p.emit("vector", lambda e: e.reciprocal(recA[:], densum[:]), sem="dve")
        p.emit("vector", lambda e: e.tensor_copy(out=d32row[:], in_=den_ps[32:33, 512:1024]))
        p.emit("vector", lambda e: e.tensor_add(densum[:], den_ps[0:1, 512:1024], d32row[:]))
        e_rB = p.emit("vector", lambda e: e.reciprocal(recB[:], densum[:]), sem="dve")

        t_bA = slot
        slot += 1
        t_bB = slot
        slot += 1
        SbA = SBUFS[t_bA % 2]
        SbB = SBUFS[t_bB % 2]
        if t_bA - 2 in e_cons:
            p.wait("tensor", *e_cons[t_bA - 2])
        p.wait("tensor", "dve", e_rA)
        e_bA = p.emit("tensor",
                      lambda e, SbA=SbA: e.matmul(SbA[:, 0:512], cf_oner[:], recA[:],
                                                  start=True, stop=True,
                                                  skip_group_check=True),
                      sem="pe")
        if t_bB - 2 in e_cons:
            p.wait("tensor", *e_cons[t_bB - 2])
        p.wait("tensor", "dve", e_rB)
        e_bB = p.emit("tensor",
                      lambda e, SbB=SbB: e.matmul(SbB[:, 0:512], cf_oner[:], recB[:],
                                                  start=True, stop=True,
                                                  skip_group_check=True),
                      sem="pe")
        p.wait("vector", "pe", e_bA)
        e_cA = p.emit("vector",
                      lambda e, SbA=SbA: e.tensor_copy(out=recbcA[:], in_=SbA[:, 0:512]),
                      sem="dve")
        e_cons[t_bA] = ("dve", e_cA)
        p.wait("vector", "dve", e_cA)
        p.emit("vector",
               lambda e, h=h: e.tensor_mul(ctxb[:, h * OWN: h * OWN + 512],
                                           ctx_ps[:, 0:512], recbcA[:]),
               sem="dve")
        p.wait("vector", "pe", e_bB)
        e_cB = p.emit("vector",
                      lambda e, SbB=SbB: e.tensor_copy(out=recbcB[:], in_=SbB[:, 0:512]),
                      sem="dve")
        e_cons[t_bB] = ("dve", e_cB)
        p.wait("vector", "dve", e_cB)
        e_evac_dve = p.emit("vector",
                            lambda e, h=h: e.tensor_mul(ctxb[:, h * OWN + 512:(h + 1) * OWN],
                                                        ctx_ps[:, 512:1024], recbcB[:]),
                            sem="dve")

    e_ctx_done = e_evac_dve
    cps.close()

    def emit_ow_dma(ec):
        owb = (ow0, ow1)[ec % 2]
        return p.dma("gpsimd",
                     owb[:].rearrange("p (dc j) -> p dc j", dc=DC),
                     owT.ap().rearrange("(dc p) e -> p dc e", p=128)
                     [:, :, ec * 128:(ec + 1) * 128])

    e_ow = [emit_ow_dma(0), emit_ow_dma(1)]

    with nc.psum_tensor("pd", [128, 1024], F32) as pd:
        p.wait("tensor", "dve", e_ctx_done)
        p.wait("vector", "sy", e_resid_init)
        e_ev_bank = {0: 0, 1: 0}
        e_mm_byec = {}
        for ec in range(DC):
            owb = (ow0, ow1)[ec % 2]
            p.wait("tensor", "gp", e_ow[ec])
            for sc in range(2):
                bank = (2 * ec + sc) % 2
                p.wait("tensor", "dve", e_ev_bank[bank])
                for dc in range(DC):
                    p.emit("tensor",
                           lambda e, dc=dc, sc=sc, bank=bank, owb=owb: e.matmul(
                               pd[:, bank * 512:(bank + 1) * 512],
                               owb[:, dc * 128:(dc + 1) * 128],
                               ctxb[:, dc * OWN + sc * 512: dc * OWN + (sc + 1) * 512],
                               start=(dc == 0), stop=(dc == DC - 1),
                               skip_group_check=True),
                           sem="pe" if dc == DC - 1 else None)
                e_mm = p.cnt["pe"]
                p.wait("vector", "pe", e_mm)
                e_ev_bank[bank] = p.emit(
                    "vector",
                    lambda e, ec=ec, sc=sc, bank=bank: e.scalar_tensor_tensor(
                        resid[:, ec * OWN + sc * 512: ec * OWN + (sc + 1) * 512],
                        pd[:, bank * 512:(bank + 1) * 512],
                        bo[:, ec:ec + 1],
                        resid[:, ec * OWN + sc * 512: ec * OWN + (sc + 1) * 512],
                        ALU.add, ALU.add),
                    sem="dve")
            e_mm_byec[ec] = p.cnt["pe"]
            if ec + 2 < DC:
                p.wait("gpsimd", "pe", e_mm_byec[ec])
                e_ow.append(emit_ow_dma(ec + 2))
    e_D_dve = p.cnt["dve"]
    e_D_pe = p.cnt["pe"]

    ac.close()

    fs = ExitStack()
    fsb = lambda n, sh, dt: fs.enter_context(nc.sbuf_tensor(n, sh, dt))
    x2_sb = fsb("x2_sb", [128, DC * OWN], BF16)
    hmid = fsb("hmid", [128, 16 * OWN], BF16)
    csx2_row = fsb("csx2_row", [1, 1024], BF16)
    csh_row = fsb("csh_row", [1, 1024], BF16)
    gsil = fsb("gsil", [128, 1024], BF16)
    usca = fsb("usca", [128, 1024], BF16)

    p.barrier("tensor", "vector", "scalar")

    with nc.psum_tensor("pl_sum", [1, 512], F32) as pl_sum, \
         nc.psum_tensor("pl_sumsq", [1, 512], F32) as pl_sumsq, \
         nc.psum_tensor("pl_bc", [128, 1024], F32) as pl_bc, \
         nc.sbuf_tensor("m2bc", [128, OWN], F32) as m2bc, \
         nc.sbuf_tensor("rstd2bc", [128, OWN], F32) as rstd2bc, \
         nc.sbuf_tensor("m2_row", [1, 512], F32) as m2_row, \
         nc.sbuf_tensor("msq2", [1, 512], F32) as msq2, \
         nc.sbuf_tensor("var2", [1, 512], F32) as var2, \
         nc.sbuf_tensor("sd2", [1, 512], F32) as sd2, \
         nc.sbuf_tensor("rstd2", [1, 512], F32) as rstd2, \
         nc.sbuf_tensor("xsq2", [128, 2 * 512], BF16) as xsq2, \
         nc.sbuf_tensor("tmpf", [128, 512], F32) as tmpf:
        for sc in range(2):
            rsl = lambda dc, sc=sc: resid[:, dc * OWN + sc * 512: dc * OWN + (sc + 1) * 512]
            if sc == 1:
                p.wait("tensor", "dve", p.cnt["dve"])
            for dc in range(DC):
                p.emit("tensor",
                       lambda e, dc=dc, rsl=rsl: e.matmul(pl_sum[:], cf_onec[:], rsl(dc),
                                                          start=(dc == 0), stop=(dc == DC - 1)),
                       sem="pe" if dc == DC - 1 else None)
            e_sum = p.cnt["pe"]
            e_mm_sq = {}
            e_sq = {}
            for dc in range(DC):
                if dc >= 2:
                    p.wait("scalar", "pe", e_mm_sq[dc - 2])
                e_sq[dc] = p.emit("scalar",
                                  lambda e, dc=dc, rsl=rsl: e.activation(
                                      out=xsq2[:, (dc % 2) * 512:(dc % 2 + 1) * 512],
                                      in_=rsl(dc), func=AF.Square),
                                  sem="act")
                p.wait("tensor", "act", e_sq[dc])
                e_mm_sq[dc] = p.emit("tensor",
                                     lambda e, dc=dc: e.matmul(
                                         pl_sumsq[:], cb_onec[:],
                                         xsq2[:, (dc % 2) * 512:(dc % 2 + 1) * 512],
                                         start=(dc == 0), stop=(dc == DC - 1)),
                                     sem="pe")
            e_sumsq = p.cnt["pe"]
            p.wait("vector", "pe", e_sum)
            e_mf = p.emit("vector",
                          lambda e: e.tensor_scalar_mul(m2_row[:], pl_sum[:], 1.0 / D),
                          sem="dve")
            p.wait("vector", "dve", e_mf)
            p.emit("vector", lambda e: e.tensor_mul(msq2[:], m2_row[:], m2_row[:]))
            p.wait("vector", "pe", e_sumsq)
            e_var = p.emit("vector",
                           lambda e: e.scalar_tensor_tensor(var2[:], pl_sumsq[:], 1.0 / D,
                                                            msq2[:], ALU.mult, ALU.subtract),
                           sem="dve")
            p.wait("scalar", "dve", e_var)
            e_sd = p.emit("scalar",
                          lambda e: e.activation(out=sd2[:], in_=var2[:],
                                                 func=AF.Sqrt, bias=cf_eps[0:1, :]),
                          sem="act")
            p.wait("vector", "act", e_sd)
            e_rstd = p.emit("vector", lambda e: e.reciprocal(rstd2[:], sd2[:]), sem="dve")
            p.wait("tensor", "dve", e_mf)
            p.emit("tensor",
                   lambda e: e.matmul(pl_bc[:, 0:512], cf_oner[:], m2_row[:],
                                      start=True, stop=True))
            p.wait("tensor", "dve", e_rstd)
            e_b2 = p.emit("tensor",
                          lambda e: e.matmul(pl_bc[:, 512:1024], cf_oner[:], rstd2[:],
                                             start=True, stop=True),
                          sem="pe")
            p.wait("vector", "pe", e_b2)
            p.emit("vector",
                   lambda e, sc=sc: e.tensor_copy(out=m2bc[:, sc * 512:(sc + 1) * 512],
                                                  in_=pl_bc[:, 0:512]))
            p.emit("vector",
                   lambda e, sc=sc: e.tensor_copy(out=rstd2bc[:, sc * 512:(sc + 1) * 512],
                                                  in_=pl_bc[:, 512:1024]),
                   sem="dve")
        e_bc2 = p.cnt["dve"]
        p.wait("vector", "dve", e_bc2)
        for dc in range(DC):
            for sc in range(2):
                p.emit("vector",
                       lambda e, dc=dc, sc=sc: e.tensor_sub(
                           tmpf[:],
                           resid[:, dc * OWN + sc * 512: dc * OWN + (sc + 1) * 512],
                           m2bc[:, sc * 512:(sc + 1) * 512]))
                p.emit("vector",
                       lambda e, dc=dc, sc=sc: e.tensor_mul(
                           x2_sb[:, dc * OWN + sc * 512: dc * OWN + (sc + 1) * 512],
                           tmpf[:],
                           rstd2bc[:, sc * 512:(sc + 1) * 512]),
                       sem="dve")
        e_x2_done = p.cnt["dve"]
        p.wait("tensor", "dve", e_x2_done)
        for sc in range(2):
            tgt = (pl_sum, pl_sumsq)[sc]
            for dc in range(DC):
                p.emit("tensor",
                       lambda e, dc=dc, sc=sc, tgt=tgt: e.matmul(
                           tgt[:], cb_onec[:],
                           x2_sb[:, dc * OWN + sc * 512: dc * OWN + (sc + 1) * 512],
                           start=(dc == 0), stop=(dc == DC - 1)),
                       sem="pe" if dc == DC - 1 else None)
            e_cs = p.cnt["pe"]
            p.wait("vector", "pe", e_cs)
            p.emit("vector",
                   lambda e, sc=sc, tgt=tgt: e.tensor_copy(
                       out=csx2_row[0:1, sc * 512:(sc + 1) * 512], in_=tgt[:]),
                   sem="dve")
    e_csx2 = p.cnt["dve"]

    wg0 = fsb("wg0", [128, DC * 128], BF16)
    wg1 = fsb("wg1", [128, DC * 128], BF16)
    wu0 = fsb("wu0", [128, DC * 128], BF16)
    wu1 = fsb("wu1", [128, DC * 128], BF16)
    dw0 = fsb("dw0", [128, 16 * 512], BF16)
    dw1 = fsb("dw1", [128, 16 * 512], BF16)

    def emit_gu_dma(ic):
        wgb = (wg0, wg1)[ic % 2]
        wub = (wu0, wu1)[ic % 2]
        a = p.dma("gpsimd",
                  wgb[:].rearrange("p (dc j) -> p dc j", dc=DC),
                  wgT.ap().rearrange("(dc p) i -> p dc i", p=128)
                  [:, :, ic * 128:(ic + 1) * 128])
        b = p.dma("gpsimd",
                  wub[:].rearrange("p (dc j) -> p dc j", dc=DC),
                  wuT.ap().rearrange("(dc p) i -> p dc i", p=128)
                  [:, :, ic * 128:(ic + 1) * 128])
        return b

    def emit_dw_dma(gidx):
        q4, hg = gidx // 4, gidx % 4
        dwb = (dw0, dw1)[gidx % 2]
        return p.dma("gpsimd",
                     dwb[:].rearrange("p (ic j) -> p ic j", ic=16),
                     wdT.ap().rearrange("(ic p) h -> p ic h", p=128)
                     [:, q4 * 16:(q4 + 1) * 16, hg * 512:(hg + 1) * 512])

    p.wait("gpsimd", "pe", p.cnt["pe"])
    p.wait("gpsimd", "dve", p.cnt["dve"])
    p.wait("gpsimd", "act", p.cnt["act"])
    e_gu = {0: emit_gu_dma(0), 1: emit_gu_dma(1)}
    e_dw = {}

    with nc.psum_tensor("pf", [128, 4096], F32) as pf:
        e_gu_evac_act = {}
        e_dwmm = {}
        e_gu_silu = {}
        e_hmid_dve = {}
        e_FB_pe = 0
        e_FB_evac_dve = 0
        e_csh_copy = 0
        for q4 in range(4):
            for il in range(16):
                ic = q4 * 16 + il
                wgb = (wg0, wg1)[ic % 2]
                wub = (wu0, wu1)[ic % 2]
                p.wait("tensor", "gp", e_gu[ic])
                if il == 0:
                    p.wait("tensor", "dve", max(e_csx2, e_FB_evac_dve))
                for w, wb, reg in ((0, wgb, 0), (1, wub, 1)):
                    if ic >= 1:
                        p.wait("tensor", "act",
                               (e_gu_silu if w == 0 else e_gu_evac_act).get(ic - 1, 0))
                    for sc in range(2):
                        off = reg * 1024 + sc * 512
                        for dc in range(DC):
                            p.emit("tensor",
                                   lambda e, dc=dc, sc=sc, off=off, wb=wb: e.matmul(
                                       pf[:, off:off + 512],
                                       wb[:, dc * 128:(dc + 1) * 128],
                                       x2_sb[:, dc * OWN + sc * 512: dc * OWN + (sc + 1) * 512],
                                       start=(dc == 0), stop=False,
                                       skip_group_check=True))
                        p.emit("tensor",
                               lambda e, sc=sc, off=off: e.matmul(
                                   pf[:, off:off + 512], cb_neg128[:],
                                   csx2_row[0:1, sc * 512:(sc + 1) * 512],
                                   start=False, stop=True, skip_group_check=True),
                               sem="pe" if (w == 1 and sc == 1) else None)
                e_gumm = p.cnt["pe"]
                p.wait("scalar", "pe", e_gumm)
                if ic >= 1:
                    p.wait("scalar", "dve", e_hmid_dve.get(ic - 1, 0))
                e_gu_silu[ic] = p.emit("scalar",
                       lambda e, ic=ic: e.activation(out=gsil[:], in_=pf[:, 0:1024],
                                                     func=AF.Silu, scale=gsc[:, ic:ic + 1]),
                       sem="act")
                e_ua = p.emit("scalar",
                              lambda e, ic=ic: e.activation(out=usca[:], in_=pf[:, 1024:2048],
                                                            func=AF.Copy,
                                                            scale=usc[:, ic:ic + 1]),
                              sem="act")
                e_gu_evac_act[ic] = e_ua
                p.wait("vector", "act", e_ua)
                if q4 > 0 and il == 0:
                    p.wait("vector", "pe", e_FB_pe)
                e_hm = p.emit("vector",
                              lambda e, il=il: e.tensor_mul(
                                  hmid[:, il * OWN:(il + 1) * OWN], gsil[:], usca[:]),
                              sem="dve")
                e_hmid_dve[ic] = e_hm
                p.wait("tensor", "dve", e_hm)
                if il == 0:
                    p.wait("tensor", "dve", e_csh_copy)
                for sc in range(2):
                    p.emit("tensor",
                           lambda e, il=il, sc=sc: e.matmul(
                               pf[0:1, 2048 + sc * 512: 2048 + (sc + 1) * 512],
                               cb_onec[:],
                               hmid[:, il * OWN + sc * 512: il * OWN + (sc + 1) * 512],
                               start=(il == 0), stop=(il == 15),
                               skip_group_check=True),
                           sem="pe" if (il == 15 and sc == 1) else None)
                if ic + 2 < 64 and (ic + 2) // 16 == q4:
                    p.wait("gpsimd", "pe", e_gumm)
                    e_gu[ic + 2] = emit_gu_dma(ic + 2)
                elif ic + 2 < 64 and (ic + 2) // 16 != q4:
                    pass
            e_csh_mm = p.cnt["pe"]
            p.wait("vector", "pe", e_csh_mm)
            e_csh_copy = p.emit("vector",
                                lambda e: e.tensor_copy(out=csh_row[:], in_=pf[0:1, 2048:3072]),
                                sem="dve")

            if q4 == 0:
                e_dw[0] = emit_dw_dma(0)
                e_dw[1] = emit_dw_dma(1)
            for hg in range(4):
                gidx = q4 * 4 + hg
                if hg == 0:
                    p.wait("tensor", "dve", max(e_csh_copy, e_FB_evac_dve))
                else:
                    p.wait("tensor", "dve", e_FB_evac_dve)
                dwb = (dw0, dw1)[gidx % 2]
                p.wait("tensor", "gp", e_dw[gidx])
                for il in range(16):
                    for hcl in range(4):
                        for sc in range(2):
                            r = hcl * 2 + sc
                            last = (il == 15 and hcl == 3 and sc == 1)
                            p.emit("tensor",
                                   lambda e, il=il, hcl=hcl, sc=sc, r=r,
                                   dwb=dwb: e.matmul(
                                       pf[:, r * 512:(r + 1) * 512],
                                       dwb[:, il * 512 + hcl * 128:
                                           il * 512 + (hcl + 1) * 128],
                                       hmid[:, il * OWN + sc * 512:
                                            il * OWN + (sc + 1) * 512],
                                       start=(il == 0), stop=False,
                                       skip_group_check=True),
                                   sem="pe" if last else None)
                e_dwmm[gidx] = p.cnt["pe"]
                if gidx + 2 < 16:
                    p.wait("gpsimd", "pe", e_dwmm[gidx])
                    e_dw[gidx + 2] = emit_dw_dma(gidx + 2)
                for hcl in range(4):
                    for sc in range(2):
                        r = hcl * 2 + sc
                        p.emit("tensor",
                               lambda e, r=r, sc=sc: e.matmul(
                                   pf[:, r * 512:(r + 1) * 512], cb_neg128[:],
                                   csh_row[0:1, sc * 512:(sc + 1) * 512],
                                   start=False, stop=True, skip_group_check=True),
                               sem="pe" if r == 7 else None)
                e_dn = p.cnt["pe"]
                p.wait("vector", "pe", e_dn)
                for hcl in range(4):
                    hc = hg * 4 + hcl
                    for sc in range(2):
                        r = hcl * 2 + sc
                        p.emit("vector",
                               lambda e, hc=hc, sc=sc, r=r: e.scalar_tensor_tensor(
                                   resid[:, hc * OWN + sc * 512: hc * OWN + (sc + 1) * 512],
                                   pf[:, r * 512:(r + 1) * 512],
                                   dsc[:, hc:hc + 1],
                                   resid[:, hc * OWN + sc * 512: hc * OWN + (sc + 1) * 512],
                                   ALU.mult, ALU.add),
                               sem="dve" if r == 7 else None)
                e_FB_evac_dve = p.cnt["dve"]
            e_FB_pe = p.cnt["pe"]
            if q4 + 1 < 4:
                p.wait("gpsimd", "pe", e_FB_pe)
                e_gu[(q4 + 1) * 16] = emit_gu_dma((q4 + 1) * 16)
                e_gu[(q4 + 1) * 16 + 1] = emit_gu_dma((q4 + 1) * 16 + 1)

    e_F_dve = p.cnt["dve"]
    e_F_pe = p.cnt["pe"]

    ot0 = fsb("ot0", [128, D], F32)
    ot1 = ot0
    with nc.psum_tensor("pg0", [128, 128], F32) as pg0, \
         nc.psum_tensor("pg1", [128, 128], F32) as pg1:
        p.wait("tensor", "dve", e_F_dve)
        e_store = {}
        e_otcp = {}
        for st8 in range(8):
            otb = (ot0, ot1)[st8 % 2]
            pgs = (pg0, pg1)
            for dc in range(DC):
                pg = pgs[dc % 2]
                prev = (st8, dc - 2) if dc >= 2 else (st8 - 1, DC - 2 + dc)
                p.wait("tensor", "dve", e_otcp.get(prev, 0))
                e_tp = p.emit("tensor",
                              lambda e, dc=dc, st8=st8, pg=pg: e.transpose(
                                  pg[:],
                                  resid[:, dc * OWN + st8 * 128: dc * OWN + (st8 + 1) * 128],
                                  cf_id[:]),
                              sem="pe")
                p.wait("vector", "pe", e_tp)
                if st8 >= 1 and dc == 0:
                    p.wait("vector", "sy", e_store[st8 - 1])
                e_otcp[(st8, dc)] = p.emit("vector",
                                           lambda e, dc=dc, pg=pg, otb=otb: e.tensor_copy(
                                               out=otb[:, dc * 128:(dc + 1) * 128], in_=pg[:]),
                                           sem="dve")
            p.wait("sync", "dve", e_otcp[(st8, DC - 1)])
            e_store[st8] = p.dma("sync", out_own.ap()[st8 * 128:(st8 + 1) * 128, :], otb[:])
        p.wait("sync", "sy", p.cnt["sy"])

    with nc.Block() as block:
        @block.sync
        def _(e):
            for f in p.q["sync"]:
                f(e)

        @block.gpsimd
        def _(e):
            for f in p.q["gpsimd"]:
                f(e)

        @block.tensor
        def _(e):
            for f in p.q["tensor"]:
                f(e)

        @block.vector
        def _(e):
            for f in p.q["vector"]:
                f(e)

        @block.scalar
        def _(e):
            for f in p.q["scalar"]:
                f(e)

    fs.close()
    es.close()
    return nc


def _host_inputs(L0_attn_x, L0_attn_Wqkv_w, L0_attn_Wqkv_b, L0_attn_out_w,
                 L0_attn_out_b, L0_ffn_gate_w, L0_ffn_gate_scale, L0_ffn_up_w,
                 L0_ffn_up_scale, L0_ffn_down_w, L0_ffn_down_scale):
    C = np.ascontiguousarray
    xT = C(np.asarray(L0_attn_x, np.float32)[0].T)
    wqkvT = C(np.asarray(L0_attn_Wqkv_w, np.float32).T)
    qkv_b2d = C(np.asarray(L0_attn_Wqkv_b, np.float32).reshape(18, 128).T)
    owT = C(np.asarray(L0_attn_out_w, np.float32).T)
    ob2d = C(np.asarray(L0_attn_out_b, np.float32).reshape(16, 128).T)
    wgT = C(np.asarray(L0_ffn_gate_w, np.int32).T)
    wuT = C(np.asarray(L0_ffn_up_w, np.int32).T)
    wdT = C(np.asarray(L0_ffn_down_w, np.int32).T)
    gsc2d = C(np.asarray(L0_ffn_gate_scale, np.float32).reshape(64, 128).T)
    usc2d = C(np.asarray(L0_ffn_up_scale, np.float32).reshape(64, 128).T)
    dsc2d = C(np.asarray(L0_ffn_down_scale, np.float32).reshape(16, 128).T)

    kk = np.arange(128)[:, None]
    qq = np.arange(512)[None, :]
    tri = [(qq >= r * 128 + kk).astype(np.float32) for r in range(4)]
    maskd = C(np.concatenate(tri, axis=1))

    shared = dict(xT=xT, wqkvT=wqkvT, qkv_b2d=qkv_b2d, owT=owT, ob2d=ob2d,
                  wgT=wgT, wuT=wuT, wdT=wdT, gsc2d=gsc2d, usc2d=usc2d,
                  dsc2d=dsc2d, maskd=maskd)

    in_maps = []
    for c in range(8):
        ca, cb = c, 15 - c
        x_own = C(np.concatenate([xT[:, ca * 512:(ca + 1) * 512],
                                  xT[:, cb * 512:(cb + 1) * 512]], axis=1))
        ktbl = np.asarray([[512 * ca, 512 * ca + 256,
                            512 * cb, 512 * cb + 256]], np.int32)
        bk = np.full((128, 48), -1e30, np.float32)
        bk[:, 0] = bk[:, 1] = bk[:, 46] = bk[:, 47] = -12.0
        for s in range(2, 16):
            if s - 2 < 2 * ca:
                bk[:, s] = -12.0
        for s in range(16, 46):
            if s - 16 < 2 * cb:
                bk[:, s] = -12.0
        m = dict(shared)
        m["x_own"] = x_own
        m["ktbl"] = C(ktbl)
        m["bkill"] = C(bk)
        in_maps.append(m)
    return in_maps


def kernel(**inputs):
    in_maps = _host_inputs(**inputs)
    nc = build_program()
    res = run_bass_kernel_spmd(nc, in_maps, core_ids=list(range(8)))
    out = np.empty((S, D), np.float32)
    for c in range(8):
        shard = np.asarray(res.results[c]["out_own"])
        out[c * 512:(c + 1) * 512] = shard[:512]
        out[(15 - c) * 512:(16 - c) * 512] = shard[512:]
    return out.reshape(1, S, D)


# version 07
# speedup vs baseline: 1.0638x; 1.0031x over previous
import numpy as np
import ml_dtypes
from contextlib import ExitStack

import concourse.bass as bass
import concourse.mybir as mybir
from concourse.bass_utils import run_bass_kernel_spmd

F32 = mybir.dt.float32
BF16 = mybir.dt.bfloat16
I32 = mybir.dt.int32
AF = mybir.ActivationFunctionType
ALU = mybir.AluOpType
ET = mybir.EngineType
ds = bass.ds
BF = ml_dtypes.bfloat16

D = 2048
S = 8192
DC = 16
H = 16
OWN = 1024
NSC = 16
I_FFN = 8192
EPS = 1e-5
EXP_SCALE = float(1.0 / np.sqrt(128.0))
EXP_BIAS = -12.0

ENGS = ("sync", "gpsimd", "tensor", "vector", "scalar")


class Prog:
    def __init__(self, nc, es):
        self.nc = nc
        self.es = es
        self.q = {e: [] for e in ENGS}
        self.cnt = {}
        self.sems = {}
        for s in ("sy", "gp", "pe", "act", "dve"):
            self.sems[s] = es.enter_context(nc.semaphore(s))
            self.cnt[s] = 0

    def on(self, eng, fn):
        self.q[eng].append(fn)

    def wait(self, eng, name, val):
        if val is None or val <= 0:
            return
        self.q[eng].append(lambda e, s=self.sems[name], v=val: e.wait_ge(s, v))

    def emit(self, eng, fn, sem=None, by=1):
        if sem is None:
            self.q[eng].append(fn)
            return None
        self.cnt[sem] += by
        n = self.cnt[sem]
        self.q[eng].append(lambda e, f=fn, s=self.sems[sem], b=by: f(e).then_inc(s, b))
        return n

    def dma(self, eng, out_ap, in_ap):
        sem = "sy" if eng == "sync" else "gp"
        return self.emit(eng, lambda e, o=out_ap, i=in_ap: e.dma_start(out=o, in_=i),
                         sem=sem, by=16)

    def barrier(self, *engs):
        snap = dict(self.cnt)
        for eng in engs:
            for s, v in snap.items():
                self.wait(eng, s, v)


def build_program():
    nc = bass.Bass("TRN2", target_bir_lowering=False)

    xT = nc.dram_tensor("xT", [D, S], F32, kind="ExternalInput")
    x_own = nc.dram_tensor("x_own", [D, OWN], F32, kind="ExternalInput")
    wqkvT = nc.dram_tensor("wqkvT", [D, 2304], F32, kind="ExternalInput")
    qkv_b2d = nc.dram_tensor("qkv_b2d", [128, 18], F32, kind="ExternalInput")
    owT = nc.dram_tensor("owT", [D, D], F32, kind="ExternalInput")
    ob2d = nc.dram_tensor("ob2d", [128, 16], F32, kind="ExternalInput")
    wgT = nc.dram_tensor("wgT", [D, I_FFN], I32, kind="ExternalInput")
    wuT = nc.dram_tensor("wuT", [D, I_FFN], I32, kind="ExternalInput")
    wdT = nc.dram_tensor("wdT", [I_FFN, D], I32, kind="ExternalInput")
    gsc2d = nc.dram_tensor("gsc2d", [128, 64], F32, kind="ExternalInput")
    usc2d = nc.dram_tensor("usc2d", [128, 64], F32, kind="ExternalInput")
    dsc2d = nc.dram_tensor("dsc2d", [128, 16], F32, kind="ExternalInput")
    ktbl = nc.dram_tensor("ktbl", [1, 4], I32, kind="ExternalInput")
    bkill = nc.dram_tensor("bkill", [128, 48], F32, kind="ExternalInput")
    maskd = nc.dram_tensor("maskd", [128, 2048], F32, kind="ExternalInput")
    out_own = nc.dram_tensor("out_own", [OWN, D], F32, kind="ExternalOutput")

    id_b = nc.inline_tensor(np.eye(128, dtype=BF), "id_b")
    id_f = nc.inline_tensor(np.eye(128, dtype=np.float32), "id_f")
    onesc_b = nc.inline_tensor(np.ones((128, 1), dtype=BF), "onesc_b")
    onesc_f = nc.inline_tensor(np.ones((128, 1), dtype=np.float32), "onesc_f")
    onesr_f = nc.inline_tensor(np.ones((1, 128), dtype=np.float32), "onesr_f")
    neg128_r = nc.inline_tensor(np.full((1, 128), -128.0, dtype=BF), "neg128_r")
    zeros_r = nc.inline_tensor(np.zeros((1, 512), dtype=BF), "zeros_r")
    eps_c = nc.inline_tensor(np.full((128, 1), EPS, dtype=np.float32), "eps_c")
    expb_c = nc.inline_tensor(np.full((128, 1), EXP_BIAS, dtype=np.float32), "expb_c")

    es = ExitStack()
    p = Prog(nc, es)
    sb = lambda n, sh, dt: es.enter_context(nc.sbuf_tensor(n, sh, dt))

    resid = sb("resid", [128, DC * OWN], F32)
    cb_id = sb("cb_id", [128, 128], BF16)
    cb_onec = sb("cb_onec", [128, 1], BF16)
    cb_neg128 = sb("cb_neg128", [1, 128], BF16)
    cb_zrow = sb("cb_zrow", [1, 512], BF16)
    cf_id = sb("cf_id", [128, 128], F32)
    cf_onec = sb("cf_onec", [128, 1], F32)
    cf_oner = sb("cf_oner", [1, 128], F32)
    maskb = sb("maskb", [128, 2048], BF16)
    ktb = sb("ktb", [1, 4], I32)
    bkb = sb("bkb", [128, 48], F32)
    bqkv = sb("bqkv", [128, 18], F32)
    bo = sb("bo", [128, 16], F32)
    gsc = sb("gsc", [128, 64], F32)
    usc = sb("usc", [128, 64], F32)
    dsc = sb("dsc", [128, 16], F32)
    cf_eps = sb("cf_eps", [128, 1], F32)
    cf_expb = sb("cf_expb", [128, 1], F32)
    ow0 = sb("ow0", [128, DC * 128], BF16)
    ow1 = sb("ow1", [128, DC * 128], BF16)

    for dst, src in ((cb_id, id_b), (cb_onec, onesc_b), (cb_neg128, neg128_r),
                     (cb_zrow, zeros_r), (cf_id, id_f), (cf_onec, onesc_f),
                     (cf_oner, onesr_f), (cf_eps, eps_c), (cf_expb, expb_c)):
        p.dma("sync", dst[:], src.ap())
    p.dma("gpsimd", maskb[:], maskd.ap())
    for dst, src in ((ktb, ktbl), (bkb, bkill), (bqkv, qkv_b2d), (bo, ob2d),
                     (gsc, gsc2d), (usc, usc2d), (dsc, dsc2d)):
        p.dma("sync", dst[:], src.ap())
    e_resid_init = p.dma("sync", resid[:].rearrange("p (dc s) -> p dc s", dc=DC),
                         x_own.ap().rearrange("(dc p) s -> p dc s", p=128))
    e_consts_sy = p.cnt["sy"]
    e_consts_gp = p.cnt["gp"]
    for _eng in ("tensor", "vector", "scalar"):
        p.wait(_eng, "sy", e_consts_sy)

    ac = ExitStack()
    asb = lambda n, sh, dt: ac.enter_context(nc.sbuf_tensor(n, sh, dt))
    k_fm = asb("k_fm", [128, S], BF16)
    v_tm = asb("v_tm", [128, S], BF16)

    pa = ExitStack()
    pa_t = lambda n, sh, dt: pa.enter_context(nc.sbuf_tensor(n, sh, dt))
    pa_p = lambda n, sh, dt: pa.enter_context(nc.psum_tensor(n, sh, dt))
    m_row_b = pa_t("m_row_b", [1, 512], BF16)
    m_row_f = pa_t("m_row_f", [1, 512], F32)
    msq_row = pa_t("msq_row", [1, 512], F32)
    var_row = pa_t("var_row", [1, 512], F32)
    sd_row = pa_t("sd_row", [1, 512], F32)
    rstd_row = pa_t("rstd_row", [1, 512], F32)
    xstage = pa_t("xstage", [128, 2 * DC * 512], BF16)
    wkv = pa_t("wkv", [128, DC * 256], BF16)
    wsn_kv = pa_t("wsn_kv", [1, 256], BF16)
    rstdb_sb = pa_t("rstdb_sb", [128, 512], F32)
    kvtmp = pa_t("kvtmp", [128, 512], F32)
    vchunk = pa_t("vchunk", [128, 512], BF16)
    xsq = pa_t("xsq", [128, 2 * 512], BF16)
    pa_stats = pa_p("pa_stats", [128, 512], F32)
    pa_kv = pa_p("pa_kv", [128, 1024], F32)
    pa_bc = pa_p("pa_bc", [128, 512], F32)
    pa_vt0 = pa_p("pa_vt0", [128, 128], BF16)
    pa_vt1 = pa_p("pa_vt1", [128, 128], BF16)
    pa_ws = pa_p("pa_ws", [1, 256], F32)
    if True:
        e_wkv = p.dma("gpsimd",
                      wkv[:].rearrange("p (dc e) -> p dc e", dc=DC),
                      wqkvT.ap().rearrange("(dc p) e -> p dc e", p=128)[:, :, 2048:2304])

        def emit_xs_dma(sc):
            buf = sc % 2
            return p.dma("gpsimd",
                         xstage[:, buf * DC * 512:(buf + 1) * DC * 512]
                         .rearrange("p (dc s) -> p dc s", dc=DC),
                         xT.ap().rearrange("(dc p) s -> p dc s", p=128)
                         [:, :, sc * 512:(sc + 1) * 512])

        e_xs = [emit_xs_dma(0), emit_xs_dma(1)]

        p.wait("tensor", "gp", e_wkv)
        for dc in range(DC):
            p.emit("tensor",
                   lambda e, dc=dc: e.matmul(pa_ws[:], cb_onec[:],
                                             wkv[:, dc * 256:(dc + 1) * 256],
                                             start=(dc == 0), stop=(dc == DC - 1)),
                   sem="pe" if dc == DC - 1 else None)
        e_ws = p.cnt["pe"]
        p.wait("vector", "pe", e_ws)
        e_wsn = p.emit("vector",
                       lambda e: e.tensor_scalar_mul(wsn_kv[:], pa_ws[:], -1.0), sem="dve")

        e_prev_var = 0
        e_prev_vm = 0
        e_prev_bcc = 0
        e_prev_bc = 0
        e_prev_vt_pe = 0
        e_mm_sq_prev = {}
        for sc in range(NSC):
            buf = sc % 2
            xsl = lambda dc, buf=buf: xstage[:, buf * DC * 512 + dc * 512:
                                             buf * DC * 512 + (dc + 1) * 512]
            p.wait("tensor", "gp", e_xs[sc])
            p.wait("tensor", "dve", e_prev_var)
            p.wait("scalar", "gp", e_xs[sc])
            e_mm_sq = {}
            e_sq = {}
            for dc in range(DC):
                if dc >= 2:
                    p.wait("scalar", "pe", e_mm_sq[dc - 2])
                elif sc > 0:
                    p.wait("scalar", "pe", e_mm_sq_prev[DC - 2 + dc])
                e_sq[dc] = p.emit("scalar",
                                  lambda e, dc=dc, xsl=xsl: e.activation(
                                      out=xsq[:, (dc % 2) * 512:(dc % 2 + 1) * 512],
                                      in_=xsl(dc), func=AF.Square),
                                  sem="act")
                p.emit("tensor",
                       lambda e, dc=dc, xsl=xsl: e.matmul(pa_stats[0:1, :], cb_onec[:],
                                                          xsl(dc), start=(dc == 0),
                                                          stop=(dc == DC - 1),
                                                          skip_group_check=True))
                p.wait("tensor", "act", e_sq[dc])
                e_mm_sq[dc] = p.emit("tensor",
                                     lambda e, dc=dc: e.matmul(
                                         pa_stats[32:33, :], cb_onec[:],
                                         xsq[:, (dc % 2) * 512:(dc % 2 + 1) * 512],
                                         start=(dc == 0), stop=(dc == DC - 1),
                                         skip_group_check=True, tile_position=(0, 32)),
                                     sem="pe")
            e_stats = p.cnt["pe"]
            e_mm_sq_prev = e_mm_sq
            p.wait("tensor", "dve", e_prev_vm)
            for dc in range(DC):
                for jc in range(2):
                    p.emit("tensor",
                           lambda e, dc=dc, jc=jc, xsl=xsl: e.matmul(
                               pa_kv[:, jc * 512:(jc + 1) * 512],
                               wkv[:, dc * 256 + jc * 128: dc * 256 + (jc + 1) * 128],
                               xsl(dc), start=(dc == 0), stop=False,
                               skip_group_check=True),
                           sem="pe" if (dc == DC - 1 and jc == 1) else None)
            e_kv = p.cnt["pe"]
            p.wait("vector", "pe", max(e_stats, e_prev_bc))
            p.emit("vector", lambda e: e.tensor_scalar_mul(m_row_b[:], pa_stats[0:1, :], 1.0 / D))
            e_mb = p.emit("vector",
                          lambda e: e.tensor_scalar_mul(m_row_f[:], pa_stats[0:1, :], 1.0 / D),
                          sem="dve")
            p.wait("vector", "dve", e_mb)
            p.emit("vector", lambda e: e.tensor_mul(msq_row[:], m_row_f[:], m_row_f[:]))
            e_var = p.emit("vector",
                           lambda e: e.scalar_tensor_tensor(var_row[:], pa_stats[32:33, :],
                                                            1.0 / D, msq_row[:],
                                                            ALU.mult, ALU.subtract),
                           sem="dve")
            e_prev_var = e_var
            p.wait("scalar", "dve", e_var)
            e_sd = p.emit("scalar",
                          lambda e: e.activation(out=sd_row[:], in_=var_row[:],
                                                 func=AF.Sqrt, bias=cf_eps[0:1, :]),
                          sem="act")
            p.wait("vector", "act", e_sd)
            e_rstd = p.emit("vector", lambda e: e.reciprocal(rstd_row[:], sd_row[:]), sem="dve")
            p.wait("tensor", "dve", max(e_mb, e_wsn))
            for jc in range(2):
                p.emit("tensor",
                       lambda e, jc=jc: e.matmul(pa_kv[:, jc * 512:(jc + 1) * 512],
                                                 wsn_kv[0:1, jc * 128:(jc + 1) * 128],
                                                 m_row_b[:], start=False, stop=True,
                                                 skip_group_check=True))
            p.wait("tensor", "dve", max(e_rstd, e_prev_bcc))
            e_bc = p.emit("tensor",
                          lambda e: e.matmul(pa_bc[:], cf_oner[:], rstd_row[:],
                                             start=True, stop=True),
                          sem="pe")
            e_prev_bc = e_bc
            p.wait("vector", "pe", e_bc)
            e_bcc = p.emit("vector", lambda e: e.tensor_copy(out=rstdb_sb[:], in_=pa_bc[:]),
                           sem="dve")
            e_prev_bcc = e_bcc
            p.wait("vector", "dve", e_bcc)
            e_km = p.emit("vector",
                          lambda e: e.tensor_mul(kvtmp[:], pa_kv[:, 0:512], rstdb_sb[:]),
                          sem="dve")
            p.emit("vector",
                   lambda e, sc=sc: e.tensor_scalar_add(
                       k_fm[:, sc * 512:(sc + 1) * 512], kvtmp[:], bqkv[:, 16:17]))
            e_vm = p.emit("vector",
                          lambda e: e.tensor_mul(kvtmp[:], pa_kv[:, 512:1024], rstdb_sb[:]),
                          sem="dve")
            e_prev_vm = e_vm
            p.wait("vector", "pe", e_prev_vt_pe)
            e_v = p.emit("vector",
                         lambda e: e.tensor_scalar_add(vchunk[:], kvtmp[:], bqkv[:, 17:18]),
                         sem="dve")
            p.wait("tensor", "dve", e_v)
            e_vc = {}
            for j in range(4):
                pvt = (pa_vt0, pa_vt1)[j % 2]
                p.wait("tensor", "dve", e_vc.get(j - 2, 0))
                e_t = p.emit("tensor",
                             lambda e, j=j, pvt=pvt: e.transpose(
                                 pvt[:], vchunk[:, j * 128:(j + 1) * 128], cb_id[:]),
                             sem="pe")
                p.wait("vector", "pe", e_t)
                e_vc[j] = p.emit("vector",
                                 lambda e, sc=sc, j=j, pvt=pvt: e.tensor_copy(
                                     out=v_tm[:, sc * 512 + j * 128: sc * 512 + (j + 1) * 128],
                                     in_=pvt[:]),
                                 sem="dve")
            e_prev_vt_pe = p.cnt["pe"]
            if sc + 2 < NSC:
                p.wait("gpsimd", "pe", e_kv)
                p.wait("gpsimd", "act", e_sq[DC - 1])
                e_xs.append(emit_xs_dma(sc + 2))

        p.barrier("tensor", "vector", "scalar")
    pa.close()

    ctxb = asb("ctxb", [128, H * OWN], BF16)
    x1own = asb("x1own", [128, DC * OWN], BF16)
    bsub = ExitStack()
    m_own_bc = bsub.enter_context(nc.sbuf_tensor("m_own_bc", [128, OWN], F32))
    rstd_own_bc = bsub.enter_context(nc.sbuf_tensor("rstd_own_bc", [128, OWN], F32))
    xsq = bsub.enter_context(nc.sbuf_tensor("xsq_b", [128, 2 * 512], BF16))
    kvtmp = bsub.enter_context(nc.sbuf_tensor("kvtmp_b", [128, 512], F32))
    m_row_f = bsub.enter_context(nc.sbuf_tensor("m_row_fb", [1, 512], F32))
    msq_row = bsub.enter_context(nc.sbuf_tensor("msq_rowb", [1, 512], F32))
    var_row = bsub.enter_context(nc.sbuf_tensor("var_rowb", [1, 512], F32))
    sd_row = bsub.enter_context(nc.sbuf_tensor("sd_rowb", [1, 512], F32))
    rstd_row = bsub.enter_context(nc.sbuf_tensor("rstd_rowb", [1, 512], F32))

    p.wait("gpsimd", "pe", p.cnt["pe"])
    p.wait("gpsimd", "act", p.cnt["act"])
    p.wait("gpsimd", "dve", p.cnt["dve"])
    e_xob = p.dma("gpsimd",
                  x1own[:].rearrange("p (dc s) -> p dc s", dc=DC),
                  x_own.ap().rearrange("(dc p) s -> p dc s", p=128))

    with nc.psum_tensor("pb_sum", [1, 512], F32) as pb_sum, \
         nc.psum_tensor("pb_sumsq", [1, 512], F32) as pb_sumsq, \
         nc.psum_tensor("pb_bc", [128, 1024], F32) as pb_bc:
        p.wait("tensor", "gp", e_xob)
        p.wait("scalar", "gp", e_xob)
        for sc in range(2):
            xol = lambda dc, sc=sc: x1own[:, dc * OWN + sc * 512: dc * OWN + (sc + 1) * 512]
            if sc == 1:
                p.wait("tensor", "dve", p.cnt["dve"])
            for dc in range(DC):
                p.emit("tensor",
                       lambda e, dc=dc, xol=xol: e.matmul(pb_sum[:], cb_onec[:], xol(dc),
                                                          start=(dc == 0), stop=(dc == DC - 1)),
                       sem="pe" if dc == DC - 1 else None)
            e_sum = p.cnt["pe"]
            e_mm_sq = {}
            e_sq = {}
            for dc in range(DC):
                if dc >= 2:
                    p.wait("scalar", "pe", e_mm_sq[dc - 2])
                e_sq[dc] = p.emit("scalar",
                                  lambda e, dc=dc, xol=xol: e.activation(
                                      out=xsq[:, (dc % 2) * 512:(dc % 2 + 1) * 512],
                                      in_=xol(dc), func=AF.Square),
                                  sem="act")
                p.wait("tensor", "act", e_sq[dc])
                e_mm_sq[dc] = p.emit("tensor",
                                     lambda e, dc=dc: e.matmul(
                                         pb_sumsq[:], cb_onec[:],
                                         xsq[:, (dc % 2) * 512:(dc % 2 + 1) * 512],
                                         start=(dc == 0), stop=(dc == DC - 1)),
                                     sem="pe")
            e_sumsq = p.cnt["pe"]
            p.wait("vector", "pe", e_sum)
            e_mf = p.emit("vector",
                          lambda e: e.tensor_scalar_mul(m_row_f[:], pb_sum[:], 1.0 / D),
                          sem="dve")
            p.wait("vector", "dve", e_mf)
            p.emit("vector", lambda e: e.tensor_mul(msq_row[:], m_row_f[:], m_row_f[:]))
            p.wait("vector", "pe", e_sumsq)
            e_var = p.emit("vector",
                           lambda e: e.scalar_tensor_tensor(var_row[:], pb_sumsq[:], 1.0 / D,
                                                            msq_row[:], ALU.mult, ALU.subtract),
                           sem="dve")
            p.wait("scalar", "dve", e_var)
            e_sd = p.emit("scalar",
                          lambda e: e.activation(out=sd_row[:], in_=var_row[:],
                                                 func=AF.Sqrt, bias=cf_eps[0:1, :]),
                          sem="act")
            p.wait("vector", "act", e_sd)
            e_rstd = p.emit("vector", lambda e: e.reciprocal(rstd_row[:], sd_row[:]), sem="dve")
            p.wait("tensor", "dve", e_mf)
            e_b1 = p.emit("tensor",
                          lambda e: e.matmul(pb_bc[:, 0:512], cf_oner[:], m_row_f[:],
                                             start=True, stop=True),
                          sem="pe")
            p.wait("tensor", "dve", e_rstd)
            e_b2 = p.emit("tensor",
                          lambda e: e.matmul(pb_bc[:, 512:1024], cf_oner[:], rstd_row[:],
                                             start=True, stop=True),
                          sem="pe")
            p.wait("vector", "pe", e_b2)
            p.emit("vector",
                   lambda e, sc=sc: e.tensor_copy(out=m_own_bc[:, sc * 512:(sc + 1) * 512],
                                                  in_=pb_bc[:, 0:512]))
            p.emit("vector",
                   lambda e, sc=sc: e.tensor_copy(out=rstd_own_bc[:, sc * 512:(sc + 1) * 512],
                                                  in_=pb_bc[:, 512:1024]),
                   sem="dve")
        e_bcast_done = p.cnt["dve"]
        p.wait("vector", "dve", e_bcast_done)
        for dc in range(DC):
            for sc in range(2):
                p.emit("vector",
                       lambda e, dc=dc, sc=sc: e.tensor_sub(
                           kvtmp[:],
                           x1own[:, dc * OWN + sc * 512: dc * OWN + (sc + 1) * 512],
                           m_own_bc[:, sc * 512:(sc + 1) * 512]))
                p.emit("vector",
                       lambda e, dc=dc, sc=sc: e.tensor_mul(
                           x1own[:, dc * OWN + sc * 512: dc * OWN + (sc + 1) * 512],
                           kvtmp[:],
                           rstd_own_bc[:, sc * 512:(sc + 1) * 512]),
                       sem="dve")
    e_x1_done = p.cnt["dve"]
    bsub.close()

    q_sb = asb("q_sb", [128, 2 * OWN], BF16)
    p0_sb = asb("p0_sb", [128, 2 * 1024], BF16)
    wq0 = asb("wq0", [128, DC * 128], BF16)
    wq1 = asb("wq1", [128, DC * 128], BF16)
    kst = asb("kst", [128, 4 * 256], BF16)
    vst = asb("vst", [128, 4 * 256], BF16)
    recA = asb("recA", [1, 512], F32)
    recB = asb("recB", [1, 512], F32)
    recbcA = asb("recbcA", [128, 512], F32)
    recbcB = asb("recbcB", [128, 512], F32)
    d32row = asb("d32row", [1, 512], F32)
    densum = asb("densum", [1, 512], F32)

    cps = ExitStack()
    S0 = cps.enter_context(nc.psum_tensor("pc_s0", [128, 1024], F32))
    S1 = cps.enter_context(nc.psum_tensor("pc_s1", [128, 1024], F32))
    ctx_ps = cps.enter_context(nc.psum_tensor("pc_ctx", [128, 1024], F32))
    den_ps = cps.enter_context(nc.psum_tensor("pc_den", [128, 1024], F32))
    SBUFS = (S0, S1)
    NG = 48

    def emit_wq_dma(h):
        wqb = (wq0, wq1)[h % 2]
        return p.dma("gpsimd",
                     wqb[:].rearrange("p (dc j) -> p dc j", dc=DC),
                     wqkvT.ap().rearrange("(dc p) e -> p dc e", p=128)
                     [:, :, h * 128:(h + 1) * 128])

    p.wait("gpsimd", "pe", p.cnt["pe"])
    p.wait("gpsimd", "act", p.cnt["act"])
    p.wait("gpsimd", "dve", p.cnt["dve"])
    e_wq = [emit_wq_dma(0), emit_wq_dma(1)]

    p.barrier("tensor", "vector", "scalar")
    p.wait("vector", "gp", e_consts_gp)
    p.wait("vector", "sy", e_consts_sy)
    p.wait("tensor", "sy", e_consts_sy)
    p.wait("scalar", "sy", e_consts_sy)

    _dve_et = nc.vector.engine
    r_kt = [nc.alloc_register(_dve_et, f"r_kt{j}") for j in range(4)]
    for j in range(4):
        p.emit("vector", lambda e, j=j: e.reg_load(r_kt[j], ktb[0:1, j:j + 1]))
        kv_ = bass.RuntimeValue(val=r_kt[j], min_val=0, max_val=7936)
        p.emit("vector",
               lambda e, j=j, kv_=kv_: e.tensor_copy(
                   out=kst[:, j * 256:(j + 1) * 256], in_=k_fm[:, ds(kv_, 256)]))
        p.emit("vector",
               lambda e, j=j, kv_=kv_: e.tensor_copy(
                   out=vst[:, j * 256:(j + 1) * 256], in_=v_tm[:, ds(kv_, 256)]),
               sem="dve")
    e_stage = p.cnt["dve"]
    p.wait("tensor", "dve", e_stage)

    def slot_params(s):
        if s == 0:
            return (lambda i: kst[:, 0 * 256 + i * 128: 0 * 256 + (i + 1) * 128],
                    lambda i: vst[:, 0 * 256 + i * 128: 0 * 256 + (i + 1) * 128], 0, 0)
        if s == 1:
            return (lambda i: kst[:, 1 * 256 + i * 128: 1 * 256 + (i + 1) * 128],
                    lambda i: vst[:, 1 * 256 + i * 128: 1 * 256 + (i + 1) * 128], 0, 1024)
        if s == 46:
            return (lambda i: kst[:, 2 * 256 + i * 128: 2 * 256 + (i + 1) * 128],
                    lambda i: vst[:, 2 * 256 + i * 128: 2 * 256 + (i + 1) * 128], 512, 0)
        if s == 47:
            return (lambda i: kst[:, 3 * 256 + i * 128: 3 * 256 + (i + 1) * 128],
                    lambda i: vst[:, 3 * 256 + i * 128: 3 * 256 + (i + 1) * 128], 512, 1024)
        if s < 16:
            kb = 256 * (s - 2)
            co = 0
        else:
            kb = 256 * (s - 16)
            co = 512
        return (lambda i, kb=kb: k_fm[:, kb + i * 128: kb + (i + 1) * 128],
                lambda i, kb=kb: v_tm[:, kb + i * 128: kb + (i + 1) * 128], co, None)

    e_cons = {}
    e_p0rdy = {}
    last_p0_pe = [0, 0]
    pend_grp = [None]
    e_evac_dve = 0
    e_qk_end_byhead = {}
    e_qe_byhead = {}
    slot_ctr = [0]

    def emit_pv(gi):
        s = gi % NG
        _, v_src, co, _ = slot_params(s)
        p.wait("tensor", *e_p0rdy[gi])
        for i in range(2):
            p.emit("tensor",
                   lambda e, gi=gi, i=i, co=co, v_src=v_src: e.matmul(
                       ctx_ps[:, co:co + 512],
                       v_src(i),
                       p0_sb[:, (gi % 2) * 1024 + i * 512:(gi % 2) * 1024 + (i + 1) * 512],
                       start=False, stop=False, skip_group_check=True))
        for i in range(2):
            p.emit("tensor",
                   lambda e, gi=gi, i=i, co=co: e.matmul(
                       den_ps[i * 32:i * 32 + 1, co:co + 512],
                       cb_onec[:],
                       p0_sb[:, (gi % 2) * 1024 + i * 512:(gi % 2) * 1024 + (i + 1) * 512],
                       start=False, stop=False, skip_group_check=True,
                       tile_position=(0, 32 * i)),
                   sem="pe" if i == 1 else None)
        last_p0_pe[gi % 2] = p.cnt["pe"]

    def emit_qproj(h):
        nonlocal_state = None
        wqb = (wq0, wq1)[h % 2]
        t_qp = slot_ctr[0]
        slot_ctr[0] += 1
        qps = SBUFS[t_qp % 2]
        if t_qp - 2 in e_cons:
            p.wait("tensor", *e_cons[t_qp - 2])
        p.wait("tensor", "gp", e_wq[h])
        if h == 0:
            p.wait("tensor", "dve", e_x1_done)
        for sc in range(2):
            for dc in range(DC):
                p.emit("tensor",
                       lambda e, dc=dc, sc=sc, qps=qps, wqb=wqb: e.matmul(
                           qps[:, sc * 512:(sc + 1) * 512],
                           wqb[:, dc * 128:(dc + 1) * 128],
                           x1own[:, dc * OWN + sc * 512: dc * OWN + (sc + 1) * 512],
                           start=(dc == 0), stop=(dc == DC - 1),
                           skip_group_check=True),
                       sem="pe" if (dc == DC - 1 and sc == 1) else None)
        e_qp = p.cnt["pe"]
        p.wait("vector", "pe", e_qp)
        if h >= 2:
            p.wait("vector", "pe", e_qk_end_byhead[h - 2])
        e_qe = p.emit("vector",
                      lambda e, h=h, qps=qps: e.tensor_scalar_add(
                          q_sb[:, (h % 2) * OWN:(h % 2 + 1) * OWN],
                          qps[:], bqkv[:, h:h + 1]),
                      sem="dve")
        e_cons[t_qp] = ("dve", e_qe)
        e_qe_byhead[h] = e_qe
        if h + 2 < H:
            p.wait("gpsimd", "pe", e_qp)
            e_wq.append(emit_wq_dma(h + 2))

    for h in range(H):
        if h == 0:
            emit_qproj(0)
        e_qe = e_qe_byhead[h]

        for g in range(NG):
            gi = h * NG + g
            t = slot_ctr[0]
            slot_ctr[0] += 1
            Sb = SBUFS[t % 2]
            k_src, v_src, co, moff = slot_params(g)
            if t - 2 in e_cons:
                p.wait("tensor", *e_cons[t - 2])
            if g == 0:
                p.wait("tensor", "dve", e_qe)

            if g == 1:
                p.wait("tensor", "dve", e_evac_dve)
                p.emit("tensor", lambda e: e.matmul(ctx_ps[:, 0:512], cb_zrow[0:1, 0:128],
                                                    cb_zrow[:], start=True, stop=True,
                                                    skip_group_check=True))
                p.emit("tensor", lambda e: e.matmul(ctx_ps[:, 512:1024], cb_zrow[0:1, 0:128],
                                                    cb_zrow[:], start=True, stop=True,
                                                    skip_group_check=True))
                for _co in (0, 512):
                    p.emit("tensor", lambda e, _co=_co: e.matmul(
                        den_ps[0:1, _co:_co + 512], cb_zrow[0:1, 0:1],
                        cb_zrow[:], start=True, stop=True, skip_group_check=True))
                    p.emit("tensor", lambda e, _co=_co: e.matmul(
                        den_ps[32:33, _co:_co + 512], cb_zrow[0:1, 0:1],
                        cb_zrow[:], start=True, stop=True, skip_group_check=True,
                        tile_position=(0, 32)))

            for i in range(2):
                p.emit("tensor",
                       lambda e, Sb=Sb, k_src=k_src, co=co, h=h, i=i: e.matmul(
                           Sb[:, i * 512:(i + 1) * 512],
                           k_src(i),
                           q_sb[:, (h % 2) * OWN + co:(h % 2) * OWN + co + 512],
                           start=True, stop=True, skip_group_check=True),
                       sem="pe" if i == 1 else None)
            e_qk = p.cnt["pe"]

            if pend_grp[0] is not None:
                emit_pv(pend_grp[0])
            pend_grp[0] = gi

            p.wait("scalar", "pe", e_qk)
            p.wait("scalar", "pe", last_p0_pe[gi % 2])
            e_exp = p.emit("scalar",
                           lambda e, Sb=Sb, gi=gi, g=g: e.activation(
                               out=p0_sb[:, (gi % 2) * 1024:(gi % 2 + 1) * 1024],
                               in_=Sb[:], func=AF.Exp, scale=EXP_SCALE,
                               bias=bkb[:, g:g + 1]),
                           sem="act")
            e_cons[t] = ("act", e_exp)

            if moff is None:
                e_p0rdy[gi] = ("act", e_exp)
            else:
                p.wait("vector", "act", e_exp)
                e_m = p.emit("vector",
                             lambda e, gi=gi, moff=moff: e.tensor_mul(
                                 p0_sb[:, (gi % 2) * 1024:(gi % 2 + 1) * 1024],
                                 p0_sb[:, (gi % 2) * 1024:(gi % 2 + 1) * 1024],
                                 maskb[:, moff:moff + 1024]),
                             sem="dve")
                e_p0rdy[gi] = ("dve", e_m)

        e_qk_end_byhead[h] = p.cnt["pe"]

        emit_pv(pend_grp[0])
        pend_grp[0] = None
        e_last_pv = p.cnt["pe"]

        if h + 1 < H:
            emit_qproj(h + 1)

        p.wait("vector", "pe", e_last_pv)
        p.emit("vector", lambda e: e.tensor_copy(out=d32row[:], in_=den_ps[32:33, 0:512]))
        p.emit("vector", lambda e: e.tensor_add(densum[:], den_ps[0:1, 0:512], d32row[:]))
        e_rA = p.emit("vector", lambda e: e.reciprocal(recA[:], densum[:]), sem="dve")
        p.emit("vector", lambda e: e.tensor_copy(out=d32row[:], in_=den_ps[32:33, 512:1024]))
        p.emit("vector", lambda e: e.tensor_add(densum[:], den_ps[0:1, 512:1024], d32row[:]))
        e_rB = p.emit("vector", lambda e: e.reciprocal(recB[:], densum[:]), sem="dve")

        t_bA = slot_ctr[0]
        slot_ctr[0] += 1
        t_bB = slot_ctr[0]
        slot_ctr[0] += 1
        SbA = SBUFS[t_bA % 2]
        SbB = SBUFS[t_bB % 2]
        if t_bA - 2 in e_cons:
            p.wait("tensor", *e_cons[t_bA - 2])
        p.wait("tensor", "dve", e_rA)
        e_bA = p.emit("tensor",
                      lambda e, SbA=SbA: e.matmul(SbA[:, 0:512], cf_oner[:], recA[:],
                                                  start=True, stop=True,
                                                  skip_group_check=True),
                      sem="pe")
        if t_bB - 2 in e_cons:
            p.wait("tensor", *e_cons[t_bB - 2])
        p.wait("tensor", "dve", e_rB)
        e_bB = p.emit("tensor",
                      lambda e, SbB=SbB: e.matmul(SbB[:, 0:512], cf_oner[:], recB[:],
                                                  start=True, stop=True,
                                                  skip_group_check=True),
                      sem="pe")
        p.wait("vector", "pe", e_bA)
        e_cA = p.emit("vector",
                      lambda e, SbA=SbA: e.tensor_copy(out=recbcA[:], in_=SbA[:, 0:512]),
                      sem="dve")
        e_cons[t_bA] = ("dve", e_cA)
        p.wait("vector", "dve", e_cA)
        p.emit("vector",
               lambda e, h=h: e.tensor_mul(ctxb[:, h * OWN: h * OWN + 512],
                                           ctx_ps[:, 0:512], recbcA[:]),
               sem="dve")
        p.wait("vector", "pe", e_bB)
        e_cB = p.emit("vector",
                      lambda e, SbB=SbB: e.tensor_copy(out=recbcB[:], in_=SbB[:, 0:512]),
                      sem="dve")
        e_cons[t_bB] = ("dve", e_cB)
        p.wait("vector", "dve", e_cB)
        e_evac_dve = p.emit("vector",
                            lambda e, h=h: e.tensor_mul(ctxb[:, h * OWN + 512:(h + 1) * OWN],
                                                        ctx_ps[:, 512:1024], recbcB[:]),
                            sem="dve")

    e_ctx_done = e_evac_dve
    cps.close()

    def emit_ow_dma(ec):
        owb = (ow0, ow1)[ec % 2]
        return p.dma("gpsimd",
                     owb[:].rearrange("p (dc j) -> p dc j", dc=DC),
                     owT.ap().rearrange("(dc p) e -> p dc e", p=128)
                     [:, :, ec * 128:(ec + 1) * 128])

    e_ow = [emit_ow_dma(0), emit_ow_dma(1)]

    with nc.psum_tensor("pd", [128, 1024], F32) as pd:
        p.wait("tensor", "dve", e_ctx_done)
        p.wait("vector", "sy", e_resid_init)
        e_ev_bank = {0: 0, 1: 0}
        e_mm_byec = {}
        for ec in range(DC):
            owb = (ow0, ow1)[ec % 2]
            p.wait("tensor", "gp", e_ow[ec])
            for sc in range(2):
                bank = (2 * ec + sc) % 2
                p.wait("tensor", "dve", e_ev_bank[bank])
                for dc in range(DC):
                    p.emit("tensor",
                           lambda e, dc=dc, sc=sc, bank=bank, owb=owb: e.matmul(
                               pd[:, bank * 512:(bank + 1) * 512],
                               owb[:, dc * 128:(dc + 1) * 128],
                               ctxb[:, dc * OWN + sc * 512: dc * OWN + (sc + 1) * 512],
                               start=(dc == 0), stop=(dc == DC - 1),
                               skip_group_check=True),
                           sem="pe" if dc == DC - 1 else None)
                e_mm = p.cnt["pe"]
                p.wait("vector", "pe", e_mm)
                e_ev_bank[bank] = p.emit(
                    "vector",
                    lambda e, ec=ec, sc=sc, bank=bank: e.scalar_tensor_tensor(
                        resid[:, ec * OWN + sc * 512: ec * OWN + (sc + 1) * 512],
                        pd[:, bank * 512:(bank + 1) * 512],
                        bo[:, ec:ec + 1],
                        resid[:, ec * OWN + sc * 512: ec * OWN + (sc + 1) * 512],
                        ALU.add, ALU.add),
                    sem="dve")
            e_mm_byec[ec] = p.cnt["pe"]
            if ec + 2 < DC:
                p.wait("gpsimd", "pe", e_mm_byec[ec])
                e_ow.append(emit_ow_dma(ec + 2))
    e_D_dve = p.cnt["dve"]
    e_D_pe = p.cnt["pe"]

    ac.close()

    fs = ExitStack()
    fsb = lambda n, sh, dt: fs.enter_context(nc.sbuf_tensor(n, sh, dt))
    x2_sb = fsb("x2_sb", [128, DC * OWN], BF16)
    hmid = fsb("hmid", [128, 16 * OWN], BF16)
    csx2_row = fsb("csx2_row", [1, 1024], BF16)
    csh_row = fsb("csh_row", [1, 1024], BF16)
    gsil = fsb("gsil", [128, 1024], BF16)
    usca = fsb("usca", [128, 1024], BF16)

    p.barrier("tensor", "vector", "scalar")

    with nc.psum_tensor("pl_sum", [1, 512], F32) as pl_sum, \
         nc.psum_tensor("pl_sumsq", [1, 512], F32) as pl_sumsq, \
         nc.psum_tensor("pl_bc", [128, 1024], F32) as pl_bc, \
         nc.sbuf_tensor("m2bc", [128, OWN], F32) as m2bc, \
         nc.sbuf_tensor("rstd2bc", [128, OWN], F32) as rstd2bc, \
         nc.sbuf_tensor("m2_row", [1, 512], F32) as m2_row, \
         nc.sbuf_tensor("msq2", [1, 512], F32) as msq2, \
         nc.sbuf_tensor("var2", [1, 512], F32) as var2, \
         nc.sbuf_tensor("sd2", [1, 512], F32) as sd2, \
         nc.sbuf_tensor("rstd2", [1, 512], F32) as rstd2, \
         nc.sbuf_tensor("xsq2", [128, 2 * 512], BF16) as xsq2, \
         nc.sbuf_tensor("tmpf", [128, 512], F32) as tmpf:
        for sc in range(2):
            rsl = lambda dc, sc=sc: resid[:, dc * OWN + sc * 512: dc * OWN + (sc + 1) * 512]
            if sc == 1:
                p.wait("tensor", "dve", p.cnt["dve"])
            for dc in range(DC):
                p.emit("tensor",
                       lambda e, dc=dc, rsl=rsl: e.matmul(pl_sum[:], cf_onec[:], rsl(dc),
                                                          start=(dc == 0), stop=(dc == DC - 1)),
                       sem="pe" if dc == DC - 1 else None)
            e_sum = p.cnt["pe"]
            e_mm_sq = {}
            e_sq = {}
            for dc in range(DC):
                if dc >= 2:
                    p.wait("scalar", "pe", e_mm_sq[dc - 2])
                e_sq[dc] = p.emit("scalar",
                                  lambda e, dc=dc, rsl=rsl: e.activation(
                                      out=xsq2[:, (dc % 2) * 512:(dc % 2 + 1) * 512],
                                      in_=rsl(dc), func=AF.Square),
                                  sem="act")
                p.wait("tensor", "act", e_sq[dc])
                e_mm_sq[dc] = p.emit("tensor",
                                     lambda e, dc=dc: e.matmul(
                                         pl_sumsq[:], cb_onec[:],
                                         xsq2[:, (dc % 2) * 512:(dc % 2 + 1) * 512],
                                         start=(dc == 0), stop=(dc == DC - 1)),
                                     sem="pe")
            e_sumsq = p.cnt["pe"]
            p.wait("vector", "pe", e_sum)
            e_mf = p.emit("vector",
                          lambda e: e.tensor_scalar_mul(m2_row[:], pl_sum[:], 1.0 / D),
                          sem="dve")
            p.wait("vector", "dve", e_mf)
            p.emit("vector", lambda e: e.tensor_mul(msq2[:], m2_row[:], m2_row[:]))
            p.wait("vector", "pe", e_sumsq)
            e_var = p.emit("vector",
                           lambda e: e.scalar_tensor_tensor(var2[:], pl_sumsq[:], 1.0 / D,
                                                            msq2[:], ALU.mult, ALU.subtract),
                           sem="dve")
            p.wait("scalar", "dve", e_var)
            e_sd = p.emit("scalar",
                          lambda e: e.activation(out=sd2[:], in_=var2[:],
                                                 func=AF.Sqrt, bias=cf_eps[0:1, :]),
                          sem="act")
            p.wait("vector", "act", e_sd)
            e_rstd = p.emit("vector", lambda e: e.reciprocal(rstd2[:], sd2[:]), sem="dve")
            p.wait("tensor", "dve", e_mf)
            p.emit("tensor",
                   lambda e: e.matmul(pl_bc[:, 0:512], cf_oner[:], m2_row[:],
                                      start=True, stop=True))
            p.wait("tensor", "dve", e_rstd)
            e_b2 = p.emit("tensor",
                          lambda e: e.matmul(pl_bc[:, 512:1024], cf_oner[:], rstd2[:],
                                             start=True, stop=True),
                          sem="pe")
            p.wait("vector", "pe", e_b2)
            p.emit("vector",
                   lambda e, sc=sc: e.tensor_copy(out=m2bc[:, sc * 512:(sc + 1) * 512],
                                                  in_=pl_bc[:, 0:512]))
            p.emit("vector",
                   lambda e, sc=sc: e.tensor_copy(out=rstd2bc[:, sc * 512:(sc + 1) * 512],
                                                  in_=pl_bc[:, 512:1024]),
                   sem="dve")
        e_bc2 = p.cnt["dve"]
        p.wait("vector", "dve", e_bc2)
        for dc in range(DC):
            for sc in range(2):
                p.emit("vector",
                       lambda e, dc=dc, sc=sc: e.tensor_sub(
                           tmpf[:],
                           resid[:, dc * OWN + sc * 512: dc * OWN + (sc + 1) * 512],
                           m2bc[:, sc * 512:(sc + 1) * 512]))
                p.emit("vector",
                       lambda e, dc=dc, sc=sc: e.tensor_mul(
                           x2_sb[:, dc * OWN + sc * 512: dc * OWN + (sc + 1) * 512],
                           tmpf[:],
                           rstd2bc[:, sc * 512:(sc + 1) * 512]),
                       sem="dve")
        e_x2_done = p.cnt["dve"]
        p.wait("tensor", "dve", e_x2_done)
        for sc in range(2):
            tgt = (pl_sum, pl_sumsq)[sc]
            for dc in range(DC):
                p.emit("tensor",
                       lambda e, dc=dc, sc=sc, tgt=tgt: e.matmul(
                           tgt[:], cb_onec[:],
                           x2_sb[:, dc * OWN + sc * 512: dc * OWN + (sc + 1) * 512],
                           start=(dc == 0), stop=(dc == DC - 1)),
                       sem="pe" if dc == DC - 1 else None)
            e_cs = p.cnt["pe"]
            p.wait("vector", "pe", e_cs)
            p.emit("vector",
                   lambda e, sc=sc, tgt=tgt: e.tensor_copy(
                       out=csx2_row[0:1, sc * 512:(sc + 1) * 512], in_=tgt[:]),
                   sem="dve")
    e_csx2 = p.cnt["dve"]

    wg0 = fsb("wg0", [128, DC * 128], BF16)
    wg1 = fsb("wg1", [128, DC * 128], BF16)
    wu0 = fsb("wu0", [128, DC * 128], BF16)
    wu1 = fsb("wu1", [128, DC * 128], BF16)
    dw0 = fsb("dw0", [128, 16 * 512], BF16)
    dw1 = fsb("dw1", [128, 16 * 512], BF16)

    def emit_gu_dma(ic):
        wgb = (wg0, wg1)[ic % 2]
        wub = (wu0, wu1)[ic % 2]
        a = p.dma("gpsimd",
                  wgb[:].rearrange("p (dc j) -> p dc j", dc=DC),
                  wgT.ap().rearrange("(dc p) i -> p dc i", p=128)
                  [:, :, ic * 128:(ic + 1) * 128])
        b = p.dma("gpsimd",
                  wub[:].rearrange("p (dc j) -> p dc j", dc=DC),
                  wuT.ap().rearrange("(dc p) i -> p dc i", p=128)
                  [:, :, ic * 128:(ic + 1) * 128])
        return b

    def emit_dw_dma(gidx):
        q4, hg = gidx // 4, gidx % 4
        dwb = (dw0, dw1)[gidx % 2]
        return p.dma("gpsimd",
                     dwb[:].rearrange("p (ic j) -> p ic j", ic=16),
                     wdT.ap().rearrange("(ic p) h -> p ic h", p=128)
                     [:, q4 * 16:(q4 + 1) * 16, hg * 512:(hg + 1) * 512])

    p.wait("gpsimd", "pe", p.cnt["pe"])
    p.wait("gpsimd", "dve", p.cnt["dve"])
    p.wait("gpsimd", "act", p.cnt["act"])
    e_gu = {0: emit_gu_dma(0), 1: emit_gu_dma(1)}
    e_dw = {}

    with nc.psum_tensor("pf", [128, 4096], F32) as pf:
        e_gu_evac_act = {}
        e_dwmm = {}
        e_stt_bank = {}
        e_gu_silu = {}
        e_hmid_dve = {}
        e_FB_pe = 0
        e_FB_evac_dve = 0
        e_csh_copy = 0
        for q4 in range(4):
            for il in range(16):
                ic = q4 * 16 + il
                wgb = (wg0, wg1)[ic % 2]
                wub = (wu0, wu1)[ic % 2]
                p.wait("tensor", "gp", e_gu[ic])
                if il == 0:
                    p.wait("tensor", "dve", max(e_csx2, e_FB_evac_dve))
                for w, wb, reg in ((0, wgb, 0), (1, wub, 1)):
                    if ic >= 1:
                        p.wait("tensor", "act",
                               (e_gu_silu if w == 0 else e_gu_evac_act).get(ic - 1, 0))
                    for sc in range(2):
                        off = reg * 1024 + sc * 512
                        for dc in range(DC):
                            p.emit("tensor",
                                   lambda e, dc=dc, sc=sc, off=off, wb=wb: e.matmul(
                                       pf[:, off:off + 512],
                                       wb[:, dc * 128:(dc + 1) * 128],
                                       x2_sb[:, dc * OWN + sc * 512: dc * OWN + (sc + 1) * 512],
                                       start=(dc == 0), stop=False,
                                       skip_group_check=True))
                        p.emit("tensor",
                               lambda e, sc=sc, off=off: e.matmul(
                                   pf[:, off:off + 512], cb_neg128[:],
                                   csx2_row[0:1, sc * 512:(sc + 1) * 512],
                                   start=False, stop=True, skip_group_check=True),
                               sem="pe" if (w == 1 and sc == 1) else None)
                e_gumm = p.cnt["pe"]
                p.wait("scalar", "pe", e_gumm)
                if ic >= 1:
                    p.wait("scalar", "dve", e_hmid_dve.get(ic - 1, 0))
                e_gu_silu[ic] = p.emit("scalar",
                       lambda e, ic=ic: e.activation(out=gsil[:], in_=pf[:, 0:1024],
                                                     func=AF.Silu, scale=gsc[:, ic:ic + 1]),
                       sem="act")
                e_ua = p.emit("scalar",
                              lambda e, ic=ic: e.activation(out=usca[:], in_=pf[:, 1024:2048],
                                                            func=AF.Copy,
                                                            scale=usc[:, ic:ic + 1]),
                              sem="act")
                e_gu_evac_act[ic] = e_ua
                p.wait("vector", "act", e_ua)
                if q4 > 0 and il == 0:
                    p.wait("vector", "pe", e_FB_pe)
                e_hm = p.emit("vector",
                              lambda e, il=il: e.tensor_mul(
                                  hmid[:, il * OWN:(il + 1) * OWN], gsil[:], usca[:]),
                              sem="dve")
                e_hmid_dve[ic] = e_hm
                p.wait("tensor", "dve", e_hm)
                if il == 0:
                    p.wait("tensor", "dve", e_csh_copy)
                for sc in range(2):
                    p.emit("tensor",
                           lambda e, il=il, sc=sc: e.matmul(
                               pf[0:1, 2048 + sc * 512: 2048 + (sc + 1) * 512],
                               cb_onec[:],
                               hmid[:, il * OWN + sc * 512: il * OWN + (sc + 1) * 512],
                               start=(il == 0), stop=(il == 15),
                               skip_group_check=True),
                           sem="pe" if (il == 15 and sc == 1) else None)
                if ic + 2 < 64 and (ic + 2) // 16 == q4:
                    p.wait("gpsimd", "pe", e_gumm)
                    e_gu[ic + 2] = emit_gu_dma(ic + 2)
                elif ic + 2 < 64 and (ic + 2) // 16 != q4:
                    pass
            e_csh_mm = p.cnt["pe"]
            p.wait("vector", "pe", e_csh_mm)
            e_csh_copy = p.emit("vector",
                                lambda e: e.tensor_copy(out=csh_row[:], in_=pf[0:1, 2048:3072]),
                                sem="dve")

            if q4 == 0:
                e_dw[0] = emit_dw_dma(0)
                e_dw[1] = emit_dw_dma(1)
            for hg in range(4):
                gidx = q4 * 4 + hg
                if hg == 0:
                    p.wait("tensor", "dve", max(e_csh_copy, e_FB_evac_dve))
                dwb = (dw0, dw1)[gidx % 2]
                p.wait("tensor", "gp", e_dw[gidx])
                for il in range(16):
                    for hcl in range(4):
                        for sc in range(2):
                            r = hcl * 2 + sc
                            last = (il == 15 and hcl == 3 and sc == 1)
                            if il == 0 and hg > 0:
                                p.wait("tensor", "dve", e_stt_bank.get(r, 0))
                            p.emit("tensor",
                                   lambda e, il=il, hcl=hcl, sc=sc, r=r,
                                   dwb=dwb: e.matmul(
                                       pf[:, r * 512:(r + 1) * 512],
                                       dwb[:, il * 512 + hcl * 128:
                                           il * 512 + (hcl + 1) * 128],
                                       hmid[:, il * OWN + sc * 512:
                                            il * OWN + (sc + 1) * 512],
                                       start=(il == 0), stop=False,
                                       skip_group_check=True),
                                   sem="pe" if last else None)
                e_dwmm[gidx] = p.cnt["pe"]
                if gidx + 2 < 16:
                    p.wait("gpsimd", "pe", e_dwmm[gidx])
                    e_dw[gidx + 2] = emit_dw_dma(gidx + 2)
                for hcl in range(4):
                    for sc in range(2):
                        r = hcl * 2 + sc
                        p.emit("tensor",
                               lambda e, r=r, sc=sc: e.matmul(
                                   pf[:, r * 512:(r + 1) * 512], cb_neg128[:],
                                   csh_row[0:1, sc * 512:(sc + 1) * 512],
                                   start=False, stop=True, skip_group_check=True),
                               sem="pe" if r == 7 else None)
                e_dn = p.cnt["pe"]
                p.wait("vector", "pe", e_dn)
                for hcl in range(4):
                    hc = hg * 4 + hcl
                    for sc in range(2):
                        r = hcl * 2 + sc
                        e_stt_bank[r] = p.emit("vector",
                               lambda e, hc=hc, sc=sc, r=r: e.scalar_tensor_tensor(
                                   resid[:, hc * OWN + sc * 512: hc * OWN + (sc + 1) * 512],
                                   pf[:, r * 512:(r + 1) * 512],
                                   dsc[:, hc:hc + 1],
                                   resid[:, hc * OWN + sc * 512: hc * OWN + (sc + 1) * 512],
                                   ALU.mult, ALU.add),
                               sem="dve")
                e_FB_evac_dve = p.cnt["dve"]
            e_FB_pe = p.cnt["pe"]
            if q4 + 1 < 4:
                p.wait("gpsimd", "pe", e_FB_pe)
                e_gu[(q4 + 1) * 16] = emit_gu_dma((q4 + 1) * 16)
                e_gu[(q4 + 1) * 16 + 1] = emit_gu_dma((q4 + 1) * 16 + 1)

    e_F_dve = p.cnt["dve"]
    e_F_pe = p.cnt["pe"]

    ot0 = fsb("ot0", [128, D], F32)
    ot1 = ot0
    with nc.psum_tensor("pg0", [128, 128], F32) as pg0, \
         nc.psum_tensor("pg1", [128, 128], F32) as pg1:
        p.wait("tensor", "dve", e_F_dve)
        e_store = {}
        e_otcp = {}
        for st8 in range(8):
            otb = (ot0, ot1)[st8 % 2]
            pgs = (pg0, pg1)
            for dc in range(DC):
                pg = pgs[dc % 2]
                prev = (st8, dc - 2) if dc >= 2 else (st8 - 1, DC - 2 + dc)
                p.wait("tensor", "dve", e_otcp.get(prev, 0))
                e_tp = p.emit("tensor",
                              lambda e, dc=dc, st8=st8, pg=pg: e.transpose(
                                  pg[:],
                                  resid[:, dc * OWN + st8 * 128: dc * OWN + (st8 + 1) * 128],
                                  cf_id[:]),
                              sem="pe")
                p.wait("vector", "pe", e_tp)
                if st8 >= 1 and dc == 0:
                    p.wait("vector", "sy", e_store[st8 - 1])
                e_otcp[(st8, dc)] = p.emit("vector",
                                           lambda e, dc=dc, pg=pg, otb=otb: e.tensor_copy(
                                               out=otb[:, dc * 128:(dc + 1) * 128], in_=pg[:]),
                                           sem="dve")
            p.wait("sync", "dve", e_otcp[(st8, DC - 1)])
            e_store[st8] = p.dma("sync", out_own.ap()[st8 * 128:(st8 + 1) * 128, :], otb[:])
        p.wait("sync", "sy", p.cnt["sy"])

    with nc.Block() as block:
        @block.sync
        def _(e):
            for f in p.q["sync"]:
                f(e)

        @block.gpsimd
        def _(e):
            for f in p.q["gpsimd"]:
                f(e)

        @block.tensor
        def _(e):
            for f in p.q["tensor"]:
                f(e)

        @block.vector
        def _(e):
            for f in p.q["vector"]:
                f(e)

        @block.scalar
        def _(e):
            for f in p.q["scalar"]:
                f(e)

    fs.close()
    es.close()
    return nc


def _host_inputs(L0_attn_x, L0_attn_Wqkv_w, L0_attn_Wqkv_b, L0_attn_out_w,
                 L0_attn_out_b, L0_ffn_gate_w, L0_ffn_gate_scale, L0_ffn_up_w,
                 L0_ffn_up_scale, L0_ffn_down_w, L0_ffn_down_scale):
    C = np.ascontiguousarray
    xT = C(np.asarray(L0_attn_x, np.float32)[0].T)
    wqkvT = C(np.asarray(L0_attn_Wqkv_w, np.float32).T)
    qkv_b2d = C(np.asarray(L0_attn_Wqkv_b, np.float32).reshape(18, 128).T)
    owT = C(np.asarray(L0_attn_out_w, np.float32).T)
    ob2d = C(np.asarray(L0_attn_out_b, np.float32).reshape(16, 128).T)
    wgT = C(np.asarray(L0_ffn_gate_w, np.int32).T)
    wuT = C(np.asarray(L0_ffn_up_w, np.int32).T)
    wdT = C(np.asarray(L0_ffn_down_w, np.int32).T)
    gsc2d = C(np.asarray(L0_ffn_gate_scale, np.float32).reshape(64, 128).T)
    usc2d = C(np.asarray(L0_ffn_up_scale, np.float32).reshape(64, 128).T)
    dsc2d = C(np.asarray(L0_ffn_down_scale, np.float32).reshape(16, 128).T)

    kk = np.arange(128)[:, None]
    qq = np.arange(512)[None, :]
    tri = [(qq >= r * 128 + kk).astype(np.float32) for r in range(4)]
    maskd = C(np.concatenate(tri, axis=1))

    shared = dict(xT=xT, wqkvT=wqkvT, qkv_b2d=qkv_b2d, owT=owT, ob2d=ob2d,
                  wgT=wgT, wuT=wuT, wdT=wdT, gsc2d=gsc2d, usc2d=usc2d,
                  dsc2d=dsc2d, maskd=maskd)

    in_maps = []
    for c in range(8):
        ca, cb = c, 15 - c
        x_own = C(np.concatenate([xT[:, ca * 512:(ca + 1) * 512],
                                  xT[:, cb * 512:(cb + 1) * 512]], axis=1))
        ktbl = np.asarray([[512 * ca, 512 * ca + 256,
                            512 * cb, 512 * cb + 256]], np.int32)
        bk = np.full((128, 48), -1e30, np.float32)
        bk[:, 0] = bk[:, 1] = bk[:, 46] = bk[:, 47] = -12.0
        for s in range(2, 16):
            if s - 2 < 2 * ca:
                bk[:, s] = -12.0
        for s in range(16, 46):
            if s - 16 < 2 * cb:
                bk[:, s] = -12.0
        m = dict(shared)
        m["x_own"] = x_own
        m["ktbl"] = C(ktbl)
        m["bkill"] = C(bk)
        in_maps.append(m)
    return in_maps


def kernel(**inputs):
    in_maps = _host_inputs(**inputs)
    nc = build_program()
    res = run_bass_kernel_spmd(nc, in_maps, core_ids=list(range(8)))
    out = np.empty((S, D), np.float32)
    for c in range(8):
        shard = np.asarray(res.results[c]["out_own"])
        out[c * 512:(c + 1) * 512] = shard[:512]
        out[(15 - c) * 512:(16 - c) * 512] = shard[512:]
    return out.reshape(1, S, D)
